# version 10
# baseline (speedup 1.0000x reference)
import numpy as np
import ml_dtypes
from contextlib import ExitStack

import concourse.bass as bass
import concourse.mybir as mybir
from concourse.bass_utils import run_bass_kernel_spmd

F32 = mybir.dt.float32
BF16 = mybir.dt.bfloat16
AF = mybir.ActivationFunctionType
ALU = mybir.AluOpType
AX = mybir.AxisListType.X

D = 1024
NCH = 8
TC = 2048
NL = 2052
TT = 512
NT = TC // TT
E = TT + 4
NB = 17
SEQ = 4096

_DBG = False
_SIMGF = False
_DBG3 = False

_SEMS = (["dD", "dX0", "dX1", "dX2", "dHA", "dHB", "dXR", "dB0", "dB1", "dB2",
          "dC", "dC2", "dWo", "dW1a", "dW1b", "dW2a", "dW2b", "dH0", "dH1",
          "dH2", "dH3",
          "dO", "dO1", "sT", "sV", "sA", "sG", "sS", "bt", "bv", "ba", "bg",
          "bs"])


class Sch:
    ENG = "tvags"

    def __init__(self):
        self.q = {k: [] for k in self.ENG}
        self.cnt = {}
        self.lastw = {k: {} for k in self.ENG}

    def tok(self, sem):
        return (sem, self.cnt.get(sem, 0))

    def op(self, eng, fn, waits=(), inc=None, inck=1):
        lw = self.lastw[eng]
        needed = []
        for w in waits:
            if w is None:
                continue
            sname, val = w
            if val > 0 and lw.get(sname, 0) < val:
                lw[sname] = val
                needed.append((sname, val))
        out_tok = None
        if inc is not None:
            self.cnt[inc] = self.cnt.get(inc, 0) + inck
            out_tok = (inc, self.cnt[inc])
        self.q[eng].append((needed, fn, inc, inck))
        return out_tok

    def barrier(self):
        toks = {}
        for e in self.ENG:
            self.op(e, lambda E: E.drain())
            toks[e] = self.op(e, lambda E: E.nop(), inc="b" + e)
        for e in self.ENG:
            self.op(e, lambda E: E.nop(),
                    waits=[toks[o] for o in self.ENG if o != e])


def build_program():
    nc = bass.Bass("TRN2", target_bir_lowering=False, detect_race_conditions=False)

    d_xs = nc.dram_tensor("xs", [NL, D], F32, kind="ExternalInput").ap()
    d_wq = nc.dram_tensor("Wq", [D, D], F32, kind="ExternalInput").ap()
    d_wk = nc.dram_tensor("Wk", [D, D], F32, kind="ExternalInput").ap()
    d_wv = nc.dram_tensor("Wv", [D, D], F32, kind="ExternalInput").ap()
    d_wfb = nc.dram_tensor("Wfb", [2 * D, D], F32, kind="ExternalInput").ap()
    d_wo = nc.dram_tensor("Wo", [D, D], F32, kind="ExternalInput").ap()
    d_w1 = nc.dram_tensor("W1", [D, 4 * D], F32, kind="ExternalInput").ap()
    d_w2 = nc.dram_tensor("W2", [4 * D, D], F32, kind="ExternalInput").ap()
    d_vecs = nc.dram_tensor("vecs", [128, 80], F32, kind="ExternalInput").ap()
    d_b1f = nc.dram_tensor("b1f", [128, 32], F32, kind="ExternalInput").ap()
    d_vb4 = nc.dram_tensor("vb4", [128, 3, 32], F32, kind="ExternalInput").ap()
    d_id = nc.dram_tensor("idm", [128, 128], F32, kind="ExternalInput").ap()
    d_sel = nc.dram_tensor("sel", [16, NCH * 128], F32, kind="ExternalInput").ap()
    d_ind = nc.dram_tensor("ind", [128, NCH * 16], F32, kind="ExternalInput").ap()
    d_ones1 = nc.dram_tensor("ones1", [1, 128], F32, kind="ExternalInput").ap()
    d_onesc = nc.dram_tensor("onesc", [128, 1], F32, kind="ExternalInput").ap()
    d_mpre = nc.dram_tensor("mpre", [16, NT * 6 * 16], F32, kind="ExternalInput").ap()
    d_mpost = nc.dram_tensor("mpost", [16, NT * 6 * 16], F32, kind="ExternalInput").ap()
    d_y = nc.dram_tensor("y", [TC, D], F32, kind="ExternalOutput").ap()
    d_h = nc.dram_tensor("hscr", [32, 128, TC], BF16, kind="Internal").ap()
    if _DBG3:
        d_dbg_sg = nc.dram_tensor("dbg_sg", [128, NCH * TT], BF16, kind="ExternalOutput").ap()
        d_dbg_gsc = nc.dram_tensor("dbg_gsc", [128, 2 * TT], F32, kind="ExternalOutput").ap()
        d_dbg_s2 = nc.dram_tensor("dbg_s2", [128, NCH * TC], F32, kind="ExternalOutput").ap()
    if _DBG:
        d_dbg_st = nc.dram_tensor("dbg_st", [128, NB * 16], F32, kind="ExternalOutput").ap()
        d_dbg_xhat = nc.dram_tensor("dbg_xhat", [128, NCH * NL], BF16, kind="ExternalOutput").ap()
        d_dbg_crow = nc.dram_tensor("dbg_crow", [1, NL], F32, kind="ExternalOutput").ap()
        d_dbg_h2 = nc.dram_tensor("dbg_h2", [128, NCH * TC], BF16, kind="ExternalOutput").ap()
        d_dbg_s = nc.dram_tensor("dbg_s", [128, NCH * TC], F32, kind="ExternalOutput").ap()

    sch = Sch()

    with ExitStack() as ctx:
        en = ctx.enter_context

        regions = []

        BASE = 16512

        def at(name, shape, dtype, off, stages):
            t = nc.alloc_sbuf_tensor_at(name, shape, dtype, offset=BASE + off)
            nbytes = int(np.prod(shape[1:])) * mybir.dt.size(dtype)
            regions.append((name, off, off + nbytes, frozenset(stages)))
            return t

        ALL = "0AWF2"

        s_id = at("c_id", [128, 128], F32, 0, ALL)
        s_idb = at("c_idb", [128, 128], BF16, 512, ALL)
        s_sel = at("c_sel", [16, NCH, 128], BF16, 768, ALL)
        s_ind = at("c_ind", [128, NCH, 16], BF16, 2816, ALL)
        s_on1 = at("c_on1", [1, 128], BF16, 3072, ALL)
        s_onc = at("c_onc", [128, 1], BF16, 3328, ALL)
        s_vecs = at("c_vecs", [128, 10, 8], F32, 3360, ALL)
        s_b1f = at("c_b1f", [128, 32], F32, 3680, ALL)
        s_mpre = at("c_mpre", [16, NT, 6, 16], F32, 3808, ALL)
        s_mpost = at("c_mpost", [16, NT, 6, 16], F32, 5344, ALL)
        VI = {n: i for i, n in enumerate(
            ["g1", "b1", "g2", "b2", "bq", "bk", "bv", "bfb", "bo", "b2f"])}

        def vec(n):
            return s_vecs[:, VI[n], :]

        s_xhat = at("s_xhat", [128, NCH, NL], BF16, 6912, "0AWF")
        s_w2b = at("s_w2b", [128, 16, D], BF16, 6912, "2")
        s_crow = at("s_crow", [1, NL], F32, 39776, "0A")
        s_sg = at("s_sg", [128, NCH, TT], BF16, 39776, "2")

        s_xtok = [at(f"s_xtok{i}", [128, D], F32, 48000 + 4096 * i, "0")
                  for i in range(3)]
        s_xht = [at(f"s_xht{i}", [128, D], BF16, 60288 + 2048 * i, "0")
                 for i in range(2)]
        s_sq = at("s_sq", [128, D], F32, 64384, "0")
        s_scol = at("s_scol", [128, 16], F32, 68480, "0")
        s_ccols = at("s_ccols", [128, 20], F32, 68544, "0")

        s_SM = at("s_SM", [16, 6, TT], F32, 48000, "A")
        s_smxm = at("s_smxm", [16, TT], F32, 60288, "A")
        s_smxr = at("s_smxr", [16, TT], F32, 62336, "A")
        s_wo = at("s_wo", [128, NCH, D], BF16, 48000, "W")
        s_on1f = at("c_on1f", [1, 128], F32, 64384, "A")
        s_vb4 = at("c_vb4", [128, 3, 32], F32, 67968, "A")
        s_SMb = at("s_SMb", [16, 3, TT], BF16, 64896, "A")
        s_hinA = at("s_hinA", [128, 16, TT], BF16, 48000, "2")

        s_h2 = at("s_h2", [128, NCH, TC], BF16, 68672, "AW")
        s_w1b = at("s_w1b", [128, NCH, 2048], BF16, 68672, "F")
        s_hinB = at("s_hinB", [128, 16, TT], BF16, 68672, "2")
        s_xre = at("s_xre", [128, 4, D], F32, 85056, "2")

        A12 = 101504
        s_B = [at(f"s_B{i}", [128, NCH, D], BF16, A12 + 16384 * i, "A")
               for i in range(3)]
        s_dbgst = nc.alloc_sbuf_tensor_at("s_dbgst", [128, NB, 16], F32,
                                          offset=BASE + 68672)
        s_stat = at("s_stat", [128, 15, 20], F32, 150656, "0")
        s_xpb = at("s_xpb", [128, NCH, E], BF16, 150656, "A")
        s_rbf = at("s_rbf", [128, NCH, TT], BF16, 150656, "A")
        s_q = at("s_q", [128, NCH, E], BF16, 158912, "A")
        s_k = at("s_k", [128, NCH, E], BF16, 167168, "A")
        s_v = at("s_v", [128, NCH, E], BF16, 175424, "A")
        s_u = at("s_u", [128, NCH, E - 2], BF16, 183680, "A")
        s_u2 = nc.alloc_sbuf_tensor_at("s_u2", [128, NCH, E - 2], BF16,
                                       offset=BASE + 150656)
        s_ao = at("s_ao", [128, 16, TT], BF16, 183680, "A")
        s_cb = at("s_cb", [128, E], F32, 200064, "A")
        s_t1 = at("s_t1", [128, E], F32, 202144, "A")
        s_x2 = at("s_x2", [128, TT], F32, 204224, "A")
        s_wb = [at(f"s_wb{i}", [128, TT], BF16, 206272 + 1024 * i, "A")
                for i in range(2)]
        s_wvp = [at(f"s_wvp{i}", [128, TT], BF16, 208320 + 1024 * i, "A")
                 for i in range(2)]
        s_rsq = [at(f"s_rsq{i}", [128, TT], BF16, 210368 + 1024 * i, "A")
                 for i in range(2)]

        s_s = at("s_sgate", [128, NCH, TC], F32, A12, "WF2")
        s_w1a = at("s_w1a", [128, NCH, 2048], BF16, 167040, "WF")
        s_w2a = at("s_w2a", [128, 16, D], BF16, 167040, "2")
        s_hst = [at(f"s_hst{i}", [128, TC], BF16, 199808 + 4096 * i, "F")
                 for i in range(2)]
        s_gsc = [at(f"s_gsc{i}", [128, TT], F32, 199808 + 2048 * i, "2")
                 for i in range(2)]
        s_outst = at("s_outst", [128, 2, D], F32, 203904, "2")

        for i in range(len(regions)):
            for j in range(i + 1, len(regions)):
                n1, a1, b1, st1 = regions[i]
                n2, a2, b2, st2 = regions[j]
                if a1 < b2 and a2 < b1 and (st1 & st2):
                    allowed = {("s_xpb", "s_rbf"), ("s_u", "s_ao")}
                    if (n1, n2) not in allowed and (n2, n1) not in allowed:
                        raise AssertionError(f"overlap {n1} {n2}")
                if b1 > 212700 or b2 > 212700:
                    raise AssertionError("sbuf overflow")

        pf = [en(nc.psum_tensor(f"pf{i}", [128, 512], F32)) for i in range(6)]
        pb = [en(nc.psum_tensor(f"pbb{i}", [128, 1024], BF16)) for i in range(2)]

        sems = {n: en(nc.semaphore(n)) for n in _SEMS}

        def OP(eng, fn, waits=(), inc=None, inck=1):
            return sch.op(eng, fn, waits, inc, inck)

        def gfence(target=None):
            if _SIMGF:
                return OP("t", lambda En, t_=target:
                          En.matmul(t_, s_onc[0:1, 0:1], s_on1[0:1, :],
                                    start=True, stop=True), inc="sT")
            anchor = sch.tok("sT")
            return OP("t", lambda En: En.nop(cycle_cnt=700), waits=[anchor],
                      inc="sT")

        def DMA(eng, out, in_, sem, waits=(), **kw):
            return OP(eng,
                      (lambda En, o_=out, i_=in_, k_=dict(kw): En.dma_start(out=o_, in_=i_, **k_)),
                      waits=waits, inc=sem, inck=16)

        DMA("g", s_id[:], d_id, "dC")
        DMA("g", s_idb[:], d_id, "dC")
        DMA("g", s_sel[:], d_sel.rearrange("h (c d) -> h c d", c=NCH), "dC")
        DMA("g", s_ind[:], d_ind.rearrange("p (c h) -> p c h", c=NCH), "dC")
        DMA("g", s_on1[:], d_ones1, "dC")
        DMA("g", s_onc[:], d_onesc, "dC")
        DMA("g", s_vecs[:], d_vecs.rearrange("p (n c) -> p n c", n=10), "dC")
        DMA("g", s_b1f[:], d_b1f, "dC")
        DMA("g", s_mpre[:], d_mpre.rearrange("h (t j c) -> h t j c", t=NT, j=6), "dC")
        tok_consts = DMA("g", s_mpost[:], d_mpost.rearrange("h (t j c) -> h t j c", t=NT, j=6), "dC")

        def wslice(d_w, lo_chunk, n_chunk=8, cols=None):
            ap = d_w[128 * lo_chunk:128 * (lo_chunk + n_chunk), :]
            if cols is not None:
                ap = ap[:, cols[0]:cols[1]]
            return ap.rearrange("(i p) o -> p i o", p=128)

        tok_wq = [None] * (NT + 1)
        tok_wk = [None] * (NT + 1)
        tok_wv = [None] * (NT + 1)
        tok_wq[0] = DMA("g", s_B[0][:], wslice(d_wq, 0), "dB0")
        tok_wk[0] = DMA("g", s_B[1][:], wslice(d_wk, 0), "dB1")
        tok_wv[0] = DMA("g", s_B[2][:], wslice(d_wv, 0), "dB2")

        OP("v", lambda En: En.memset(s_ccols[:], 0.0))

        tok_xtok_free = [None] * NB
        tok_xht_free = [None] * NB
        tok_pb_evac = [None, None]
        tok_pc_evac = None
        tok_load = [None] * NB
        ST = s_stat

        for g0 in range(0, NB, 3):
            grp = list(range(g0, min(g0 + 3, NB)))
            g1 = grp[-1] + 1
            for b in grp:
                nb_ = 4 if b == NB - 1 else 128
                w = [tok_xtok_free[b - 3]] if b >= 3 else []
                tok_load[b] = DMA("s", s_xtok[b % 3][0:nb_, :],
                                  d_xs[128 * b:128 * b + nb_, :],
                                  f"dX{b % 3}", waits=w)
            rtoks = []
            for b in grp:
                xt = s_xtok[b % 3]
                rtoks.append(OP("v", lambda En, a=ST[:, 0, b:b + 1], i=xt[:]:
                                En.tensor_reduce(a, i, AX, ALU.add),
                                waits=[tok_load[b]], inc="sV"))
                rtoks.append(OP("v", lambda En, i=xt[:], a=ST[:, 1, b:b + 1]:
                                En.scalar_tensor_tensor(
                                    s_sq[:], i, 1.0, i, ALU.mult, ALU.mult,
                                    accum_out=a), inc="sV"))
            w1 = OP("v", lambda En, o_=ST[:, 2, g0:g1], a_=ST[:, 0, g0:g1]:
                    En.tensor_scalar_mul(o_, a_, 1.0 / D),
                    waits=rtoks, inc="sV")
            OP("v", lambda En, o_=ST[:, 3, g0:g1], a_=ST[:, 1, g0:g1]:
               En.tensor_scalar_mul(o_, a_, 1.0 / D))
            w2 = OP("v", lambda En, o_=ST[:, 4, g0:g1], a_=ST[:, 2, g0:g1]:
                    En.tensor_mul(o_, a_, a_), waits=[w1], inc="sV")
            w3 = OP("v", lambda En, o_=ST[:, 5, g0:g1], a_=ST[:, 3, g0:g1],
                    b_=ST[:, 4, g0:g1]:
                    En.tensor_sub(o_, a_, b_), waits=[w2], inc="sV")
            tA = OP("v", lambda En, o_=ST[:, 6, g0:g1], a_=ST[:, 5, g0:g1]:
                    En.tensor_scalar_add(o_, a_, 1e-5), waits=[w3], inc="sV")
            wa1 = OP("a", lambda En, o_=ST[:, 7, g0:g1], a_=ST[:, 6, g0:g1]:
                     En.activation(o_, a_, AF.Ln), waits=[tA], inc="sA")
            OP("a", lambda En, o_=ST[:, 8, g0:g1], a_=ST[:, 7, g0:g1]:
               En.activation(o_, a_, AF.Exp, scale=-0.5), waits=[wa1])
            tB = OP("a", lambda En, o_=ST[:, 9, g0:g1], a_=ST[:, 7, g0:g1]:
                    En.activation(o_, a_, AF.Exp, scale=-1.0), inc="sA")
            w4 = OP("v", lambda En, o_=ST[:, 10, g0:g1], a_=ST[:, 5, g0:g1],
                    b_=ST[:, 9, g0:g1]:
                    En.tensor_mul(o_, a_, b_), waits=[tB], inc="sV")
            tC = OP("v", lambda En, o_=ST[:, 11, g0:g1], a_=ST[:, 10, g0:g1]:
                    En.tensor_scalar_add(o_, a_, 1e-6), waits=[w4], inc="sV")
            wa2 = OP("a", lambda En, o_=ST[:, 12, g0:g1], a_=ST[:, 11, g0:g1]:
                     En.activation(o_, a_, AF.Ln), waits=[tC], inc="sA")
            tD = OP("a", lambda En, o_=ST[:, 13, g0:g1], a_=ST[:, 12, g0:g1]:
                    En.activation(o_, a_, AF.Exp, scale=-0.5),
                    waits=[wa2], inc="sA")
            w5b = OP("v", lambda En, o_=ST[:, 14, g0:g1], a_=ST[:, 2, g0:g1],
                     b_=ST[:, 8, g0:g1]:
                     En.scalar_tensor_tensor(o_, a_, -1.0, b_,
                                             ALU.mult, ALU.mult),
                     waits=[tD], inc="sV")

            for b in grp:
                nb_ = 4 if b == NB - 1 else 128
                xs_ = b % 2
                L0 = 128 * b
                xt = s_xtok[b % 3]
                wx = [w5b]
                if b >= 2:
                    wx.append(tok_xht_free[b - 2])
                tE = OP("v", lambda En, o_=s_xht[xs_][:], i=xt[:],
                        g_=ST[:, 8, b:b + 1], m_=ST[:, 14, b:b + 1]:
                        En.tensor_scalar(o_, i, g_, m_, ALU.mult, ALU.add),
                        waits=wx, inc="sV")
                tok_xtok_free[b] = tE

                wt = [tE, tok_consts]
                if tok_pb_evac[xs_] is not None:
                    wt.append(tok_pb_evac[xs_])
                for c in range(NCH):
                    OP("t", lambda En, o_=pb[xs_][:, 128 * c:128 * (c + 1)],
                       i=s_xht[xs_][:, 128 * c:128 * (c + 1)]:
                       En.transpose(o_, i, s_idb[:]), waits=wt)
                    wt = []
                wc = [tok_pc_evac] if tok_pc_evac else []
                OP("t", lambda En, i=ST[:, 13, b:b + 1]:
                   En.transpose(pf[0][0:1, 0:128], i, s_id[:]), waits=wc)
                tok_tr = OP("t", lambda En:
                            En.matmul(pf[1][0:1, 0:128], s_onc[0:1, 0:1],
                                      s_on1[0:1, :],
                                      start=True, stop=True), inc="sT")
                tok_xht_free[b] = tok_tr
                tok_pb_evac[xs_] = OP(
                    "a", lambda En, o_=s_xhat[:, :, L0:L0 + nb_],
                    i=pb[xs_][:].rearrange("p (c t) -> p c t", c=NCH)[:, :, 0:nb_]:
                    En.copy(o_, i), waits=[tok_tr], inc="sA")
                tok_pc_evac = OP(
                    "a", lambda En, o_=s_crow[0:1, L0:L0 + nb_],
                    i=pf[0][0:1, 0:nb_]:
                    En.copy(o_, i), waits=[tok_tr], inc="sA")

        sch.barrier()
        if _DBG:
            DMA("s", d_dbg_st, s_dbgst[:].rearrange("p b c -> p (b c)"), "dD")
            DMA("s", d_dbg_xhat, s_xhat[:].rearrange("p c t -> p (c t)"), "dD")
            DMA("s", d_dbg_crow, s_crow[:], "dD")

        tok_on1f = DMA("g", s_on1f[:], d_ones1, "dC2")
        tok_vb4 = DMA("g", s_vb4[:], d_vb4, "dC2")
        t_pf_free = [None] * 6
        tok_fa = [None] * NT
        tok_fb = [None] * NT
        tok_u_freed = None
        tok_xpb_free = None
        tok_ao_free = None

        for t in range(NT):
            e0 = TT * t
            qb = s_B[(0 - t) % 3]
            kb = s_B[(1 - t) % 3]
            vb = s_B[(2 - t) % 3]

            t1m = OP("t", lambda En, r_=s_crow[0:1, e0:e0 + 512]:
                     En.matmul(pf[4][:], s_on1f[:], r_, start=True, stop=True),
                     waits=[tok_consts, tok_on1f, tok_pc_evac, t_pf_free[4]], inc="sT")
            t2m = OP("t", lambda En, r_=s_crow[0:1, e0 + 512:e0 + E]:
                     En.matmul(pf[5][:, 0:4], s_on1f[:], r_, start=True, stop=True),
                     waits=[t_pf_free[5]], inc="sT")
            fcb = gfence(pf[5][32:33, 256:384])
            ce1 = OP("a", lambda En: En.copy(s_cb[:, 0:512], pf[4][:]),
                     waits=[fcb], inc="sA")
            ce2 = OP("a", lambda En: En.copy(s_cb[:, 512:E], pf[5][:, 0:4]),
                     inc="sA")
            t_pf_free[4] = ce1
            t_pf_free[5] = ce2

            tok_xpb = [None] * NCH
            for c in range(NCH):
                OP("v", lambda En, i=s_xhat[:, c, e0:e0 + E]:
                   En.tensor_tensor(s_t1[:], i, s_cb[:], ALU.mult),
                   waits=[ce1, ce2, tok_xpb_free])
                tok_xpb[c] = OP(
                    "v", lambda En, o_=s_xpb[:, c, :], g_=vec("g1")[:, c:c + 1],
                    b_=vec("b1")[:, c:c + 1]:
                    En.tensor_scalar(o_, s_t1[:], g_, b_, ALU.mult, ALU.add),
                    inc="sV")

            tok_qe = [None] * NCH
            tok_ke = [None] * NCH
            tok_ve = [None] * NCH
            evs = {"q": tok_qe, "k": tok_ke, "v": tok_ve}
            obuf = {"q": s_q, "k": s_k, "v": s_v}
            bias = {"q": "bq", "k": "bk", "v": "bv"}
            wtk = {"q": tok_wq[t], "k": tok_wk[t], "v": tok_wv[t]}
            wbf = {"q": qb, "k": kb, "v": vb}
            lastmm = {}
            qkv_pend = None

            def qkv_flush(pv, cover):
                wn_, oc_, p5_ = pv
                ev1 = OP("a", lambda En, o_=obuf[wn_][:, oc_, 0:512], p_=p5_,
                         b_=vec(bias[wn_])[:, oc_:oc_ + 1]:
                         En.activation(o_, p_[:], AF.Identity, bias=b_, scale=1.0),
                         waits=[cover], inc="sA")
                evs[wn_][oc_] = ev1
                t_pf_free[(0, 1, 4)[oc_ % 3]] = ev1
                if oc_ == NCH - 1:
                    wni = {"q": 0, "k": 1, "v": 2}[wn_]
                    ev2 = OP("a", lambda En, o_=s_t1[:, 32 * wni:32 * wni + 32],
                             p_=pf[2][:, 0:32]: En.copy(o_, p_), inc="sA")
                    t_pf_free[2] = ev2
                    OP("v", lambda En,
                       o_=obuf[wn_][:, :, 512:E],
                       a_=s_t1[:, 32 * wni:32 * wni + 32],
                       b_=s_vb4[:, wni:wni + 1, :]:
                       En.tensor_tensor(o_, a_, b_, ALU.add),
                       waits=[ev2, tok_vb4])

            for wn in ("q", "k", "v"):
                B_ = wbf[wn]
                for oc in range(NCH):
                    bmain = (0, 1, 4)[oc % 3]
                    p5 = pf[bmain]
                    w0 = [wtk[wn], t_pf_free[bmain]]
                    for ic in range(NCH):
                        mmm = OP("t", lambda En, l_=B_[:, ic, 128 * oc:128 * (oc + 1)],
                                 r_=s_xpb[:, ic, 0:512], st=(ic == 0), sp=(ic == NCH - 1),
                                 p_=p5:
                                 En.matmul(p_[:], l_, r_, start=st, stop=sp),
                                 waits=w0 + [tok_xpb[ic]],
                                 inc=("sT" if ic == 0 else None))
                        w0 = []
                        if ic == 0 and qkv_pend is not None:
                            qkv_flush(qkv_pend, mmm)
                            qkv_pend = None
                        mmt = OP("t", lambda En, l_=B_[:, ic, 128 * oc:128 * (oc + 1)],
                                 r_=s_xpb[:, ic, 512:E], st=(ic == 0), sp=(ic == NCH - 1),
                                 p_=pf[2], co=4 * oc:
                                 En.matmul(p_[:, co:co + 4], l_, r_, start=st, stop=sp),
                                 waits=([t_pf_free[2]] if ic == 0 and oc == 0 else []),
                                 inc=("sT" if ic == NCH - 1 else None))
                    qkv_pend = (wn, oc, p5)
                lastmm[wn] = mmt

            tok_fa[t] = DMA("g", qb[:], wslice(d_wfb, 0), f"dB{(0 - t) % 3}", waits=[lastmm["q"]])
            tok_fb[t] = DMA("g", kb[:], wslice(d_wfb, 8), f"dB{(1 - t) % 3}", waits=[lastmm["k"]])
            if t + 1 < NT:
                tok_wq[t + 1] = DMA("g", vb[:], wslice(d_wq, 0), f"dB{(2 - t) % 3}",
                                    waits=[lastmm["v"]])

            tok_sm = []
            prev_scores = [None, None]
            sc_pend = None
            for j in range(3):
                ub = (s_u, s_u2, s_u)[j]
                ut = [None] * NCH
                for c in range(NCH):
                    w = [tok_qe[c], tok_ke[c]]
                    if j == 0 and c == 0 and tok_u_freed is not None:
                        w.append(tok_u_freed)
                    if c == 0 and j >= 2 and prev_scores[j % 2] is not None:
                        w.append(prev_scores[j % 2])
                    ut[c] = OP("v", lambda En, o_=ub[:, c, :],
                               a_=s_q[:, c, 1:E - 1], b_=s_k[:, c, 2 - j:E - j]:
                               En.tensor_tensor(o_, a_, b_, ALU.mult),
                               waits=w, inc="sV")
                wF = [t_pf_free[4]]
                for c in range(NCH):
                    mmF = OP("t", lambda En, l_=s_ind[:, c, :], r_=ub[:, c, 0:512],
                             st=(c == 0), sp=(c == NCH - 1):
                             En.matmul(pf[4][0:16, :], l_, r_, start=st, stop=sp),
                             waits=wF + [ut[c]],
                             inc=("sT" if c in (0, NCH - 1) else None))
                    wF = []
                    if c == 0:
                        if qkv_pend is not None:
                            qkv_flush(qkv_pend, mmF)
                            qkv_pend = None
                        if sc_pend is not None:
                            e2 = OP("a", lambda En, jp=sc_pend,
                                    o_=s_SM[:, 3 + (2 - sc_pend), :]:
                                    En.copy(o_, pf[5][0:16, :]),
                                    waits=[mmF], inc="sA")
                            t_pf_free[5] = e2
                            tok_sm.append(e2)
                            sc_pend = None
                wR = [t_pf_free[5]]
                mmR0 = None
                for c in range(NCH):
                    mmR = OP("t", lambda En, l_=s_ind[:, c, :], r_=ub[:, c, 2:514],
                             st=(c == 0), sp=(c == NCH - 1):
                             En.matmul(pf[5][0:16, :], l_, r_, start=st, stop=sp),
                             waits=wR,
                             inc=("sT" if c in (0, NCH - 1) else None))
                    wR = []
                    if c == 0:
                        mmR0 = mmR
                e1 = OP("a", lambda En, o_=s_SM[:, j, :]:
                        En.copy(o_, pf[4][0:16, :]), waits=[mmR0], inc="sA")
                t_pf_free[4] = e1
                tok_sm.append(e1)
                sc_pend = j
                prev_scores[j % 2] = mmR
            fsc = gfence(pf[4][32:33, 256:384])
            e2 = OP("a", lambda En, o_=s_SM[:, 3 + (2 - 2), :]:
                    En.copy(o_, pf[5][0:16, :]), waits=[fsc], inc="sA")
            t_pf_free[5] = e2
            tok_sm.append(e2)
            sc_pend = None
            tok_u_read = sch.tok("sT")

            OP("v", lambda En, m_=s_mpre[:, t, :, 0:8]:
               En.tensor_tensor(s_SM[:, :, 0:8], s_SM[:, :, 0:8], m_, ALU.mult),
               waits=tok_sm)
            OP("v", lambda En, m_=s_mpre[:, t, :, 8:16]:
               En.tensor_tensor(s_SM[:, :, TT - 8:TT], s_SM[:, :, TT - 8:TT], m_, ALU.mult))

            tok_ao = [None] * 16
            par = 0
            wb_read = [None, None, None, None]
            tok_prev_wv = None
            wvbbuf = (s_wb[0], s_wb[1], s_rsq[0], s_rsq[1])

            def wv_flush(pv, cover_tok):
                i_, pw_, c_, j_, dd_ = pv
                wbb = wvbbuf[i_ % 4]
                ewb = OP("a", lambda En, o_=wbb[:], p_=pw_:
                         En.copy(o_, p_[:]),
                         waits=[cover_tok, wb_read[i_ % 4]], inc="sA")
                t_pf_free[2 + i_ % 4] = ewb
                vs = (2 - j_) if dd_ == 0 else (2 + j_)
                if j_ == 0:
                    tok_ao[8 * dd_ + c_] = OP(
                        "v", lambda En, o_=s_ao[:, 8 * dd_ + c_, :],
                        a_=wbb[:], b_=s_v[:, c_, vs:vs + 512]:
                        En.tensor_mul(o_, a_, b_),
                        waits=[ewb, tok_ve[c_]], inc="sV")
                    wb_read[i_ % 4] = tok_ao[8 * dd_ + c_]
                else:
                    wb_read[i_ % 4] = OP(
                        "v", lambda En, o_=s_wvp[i_ % 2][:],
                        a_=wbb[:], b_=s_v[:, c_, vs:vs + 512]:
                        En.tensor_mul(o_, a_, b_), waits=[ewb], inc="sV")
                    tok_ao[8 * dd_ + c_] = OP(
                        "v", lambda En, o_=s_ao[:, 8 * dd_ + c_, :],
                        a_=s_wvp[i_ % 2][:]:
                        En.tensor_add(o_, o_, a_), inc="sV")
            wv_pend = None
            for dd in range(2):
                d0 = 3 * dd
                OP("v", lambda En, a_=s_SM[:, d0, :], b_=s_SM[:, d0 + 1, :]:
                   En.tensor_max(s_smxm[:], a_, b_))
                OP("v", lambda En, a_=s_SM[:, d0 + 2, :]:
                   En.tensor_max(s_smxm[:], s_smxm[:], a_))
                for j in range(3):
                    tsb = OP("v", lambda En, a_=s_SM[:, d0 + j, :]:
                             En.tensor_sub(a_, a_, s_smxm[:]),
                             inc=("sV" if j == 2 else None))
                tex = OP("a", lambda En, a_=s_SM[:, d0:d0 + 3, :]:
                         En.activation(a_, a_, AF.Exp),
                         waits=[tsb], inc="sA")
                OP("v", lambda En, a_=s_SM[:, d0, :], b_=s_SM[:, d0 + 1, :]:
                   En.tensor_add(s_smxr[:], a_, b_), waits=[tex])
                tsum = OP("v", lambda En, a_=s_SM[:, d0 + 2, :]:
                          En.tensor_add(s_smxr[:], s_smxr[:], a_), inc="sV")
                OP("a", lambda En: En.activation(s_smxm[:], s_smxr[:], AF.Ln),
                   waits=[tsum])
                trc = OP("a", lambda En: En.activation(s_smxr[:], s_smxm[:], AF.Exp, scale=-1.0),
                         inc="sA")
                for j in range(3):
                    OP("v", lambda En, a_=s_SM[:, d0 + j, :], o_=s_SMb[:, j, :]:
                       En.tensor_mul(o_, a_, s_smxr[:]),
                       waits=([trc, tok_prev_wv] if j == 0 else []))
                OP("v", lambda En, m_=s_mpost[:, t, d0:d0 + 3, 0:8]:
                   En.tensor_tensor(s_SMb[:, :, 0:8], s_SMb[:, :, 0:8], m_, ALU.mult))
                tok_w_final = OP("v", lambda En, m_=s_mpost[:, t, d0:d0 + 3, 8:16]:
                                 En.tensor_tensor(s_SMb[:, :, TT - 8:TT], s_SMb[:, :, TT - 8:TT], m_, ALU.mult),
                                 inc="sV")

                wvw = [tok_w_final, tok_u_read]
                if dd == 0 and tok_ao_free is not None:
                    wvw.append(tok_ao_free)
                for c in range(NCH):
                    for j in range(3):
                        pw = pf[2 + par % 4]
                        mmW = OP("t", lambda En, l_=s_sel[:, c, :], r_=s_SMb[:, j, :],
                                 p_=pw:
                                 En.matmul(p_[:], l_, r_, start=True, stop=True),
                                 waits=wvw + [t_pf_free[2 + par % 4]], inc="sT")
                        wvw = []
                        if wv_pend is not None:
                            wv_flush(wv_pend, mmW)
                        wv_pend = (par, pw, c, j, dd)
                        par += 1
                tok_prev_wv = sch.tok("sT")

            tok_rbf = [None] * NCH
            tok_rsq = [None] * NCH
            stat_mm2 = [None] * NCH
            def emit_stats(so):
                wS = [t_pf_free[4], tok_rbf[so]] if so == 0 else [tok_rbf[so]]
                mS = OP("t", lambda En, r_=s_rbf[:, so, :], st=(so == 0),
                        sp=(so == NCH - 1):
                        En.matmul(pf[4][0:1, :], s_onc[:], r_, start=st, stop=sp),
                        waits=wS, inc=("sT" if so == NCH - 1 else None))
                wS2 = [t_pf_free[5], tok_rsq[so]] if so == 0 else [tok_rsq[so]]
                stat_mm2[so] = mS2 = OP(
                    "t", lambda En, r_=s_rsq[so % 2][:], st=(so == 0),
                    sp=(so == NCH - 1):
                    En.matmul(pf[5][0:1, :], s_onc[:], r_, start=st, stop=sp),
                    waits=wS2, inc="sT")
                return mS, mS2

            for oc in range(NCH):
                p5 = pf[oc % 2]
                w0 = [tok_fa[t], tok_fb[t], t_pf_free[oc % 2]]
                for ic in range(16):
                    B_ = qb if ic < 8 else kb
                    mmt = OP("t", lambda En, l_=B_[:, ic % 8, 128 * oc:128 * (oc + 1)],
                             r_=s_ao[:, ic, :], st=(ic == 0), sp=(ic == 15), p_=p5:
                             En.matmul(p_[:], l_, r_, start=st, stop=sp),
                             waits=w0 + [tok_ao[ic]],
                             inc=("sT" if ic == 15 else None))
                    w0 = []
                    if wv_pend is not None:
                        wv_flush(wv_pend, mmt)
                        wv_pend = None
                if oc >= 1:
                    emit_stats(oc - 1)
                    cov = stat_mm2[oc - 1]
                else:
                    cov = gfence(pf[4][32:33, 256:384])
                OP("v", lambda En, i=s_xhat[:, oc, e0 + 2:e0 + 2 + TT]:
                   En.tensor_tensor(s_x2[:], i, s_cb[:, 2:2 + TT], ALU.mult),
                   waits=[cov])
                OP("v", lambda En, g_=vec("g1")[:, oc:oc + 1], b_=vec("b1")[:, oc:oc + 1]:
                   En.tensor_scalar(s_x2[:], s_x2[:], g_, b_, ALU.mult, ALU.add))
                tok_rbf[oc] = OP(
                    "v", lambda En, o_=s_rbf[:, oc, :], p_=p5, b_=vec("bfb")[:, oc:oc + 1]:
                    En.scalar_tensor_tensor(o_, p_[:], b_, s_x2[:], ALU.add, ALU.add),
                    inc="sV")
                t_pf_free[oc % 2] = tok_rbf[oc]
                tok_rsq[oc] = OP(
                    "v", lambda En, o_=s_rsq[oc % 2][:], a_=s_rbf[:, oc, :]:
                    En.tensor_mul(o_, a_, a_),
                    waits=[stat_mm2[oc - 2]] if oc >= 2 else [], inc="sV")
                if oc == NCH - 1:
                    emit_stats(oc)
            rowm = s_smxm[0:1, :]
            rowe = s_smxr[0:1, :]
            rowt = s_SM[0:1, 0, :]
            fst = gfence(pf[5][32:33, 256:384])
            OP("v", lambda En: En.tensor_scalar_mul(rowm, pf[4][0:1, :], 1.0 / D),
               waits=[fst])
            OP("v", lambda En: En.tensor_scalar_mul(rowe, pf[5][0:1, :], 1.0 / D))
            OP("v", lambda En: En.tensor_mul(rowt, rowm, rowm))
            OP("v", lambda En: En.tensor_sub(rowe, rowe, rowt))
            tve = OP("v", lambda En: En.tensor_scalar_add(rowe, rowe, 1e-6), inc="sV")
            t_pf_free[4] = tve
            t_pf_free[5] = tve
            OP("a", lambda En: En.activation(rowt, rowe, AF.Ln), waits=[tve])
            trs = OP("a", lambda En: En.activation(rowe, rowt, AF.Exp, scale=-0.5),
                     inc="sA")
            tnm = OP("v", lambda En: En.tensor_scalar_mul(rowm, rowm, -1.0), inc="sV")
            bc1 = OP("t", lambda En: En.matmul(pf[2][:], s_on1f[:], rowm, start=True, stop=True),
                     waits=[tnm, t_pf_free[2]], inc="sT")
            bc2 = OP("t", lambda En: En.matmul(pf[3][:], s_on1f[:], rowe, start=True, stop=True),
                     waits=[trs, t_pf_free[3]], inc="sT")
            fbc = gfence(pf[5][32:33, 256:384])
            th2 = None
            for oc in range(NCH):
                OP("v", lambda En, a_=s_rbf[:, oc, :]:
                   En.tensor_tensor(s_x2[:], a_, pf[2][:], ALU.add),
                   waits=([fbc] if oc == 0 else []))
                OP("v", lambda En: En.tensor_tensor(s_x2[:], s_x2[:], pf[3][:], ALU.mult))
                th2 = OP("v", lambda En, o_=s_h2[:, oc, e0:e0 + TT],
                         g_=vec("g2")[:, oc:oc + 1], b_=vec("b2")[:, oc:oc + 1]:
                         En.tensor_scalar(o_, s_x2[:], g_, b_, ALU.mult, ALU.add),
                         inc="sV")
            t_pf_free[2] = th2
            t_pf_free[3] = th2

            tok_u_freed = tok_u_read
            tok_u_freed = sch.tok("sT")
            tok_xpb_free = sch.tok("sT")
            tok_ao_free = sch.tok("sT")
            if t + 1 < NT:
                tok_wk[t + 1] = DMA("g", qb[:], wslice(d_wk, 0), f"dB{(0 - t) % 3}",
                                    waits=[sch.tok("sT")])
                tok_wv[t + 1] = DMA("g", kb[:], wslice(d_wv, 0), f"dB{(1 - t) % 3}",
                                    waits=[sch.tok("sT")])

        if _DBG:
            tD = DMA("s", d_dbg_h2, s_h2[:].rearrange("p c t -> p (c t)"), "dD",
                     waits=[sch.tok("sV")])
            OP("s", lambda En: En.nop(), waits=[tD])
        sch.barrier()

        tok_wo = DMA("g", s_wo[:], wslice(d_wo, 0), "dWo")
        tok_w1a = DMA("g", s_w1a[:], d_w1[:, 0:2048].rearrange("(i p) o -> p i o", p=128),
                      "dW1a")
        pf_free = [None] * 6
        a3_pend = None

        def a3_flush(po, cover):
            pf_free[NT - 1] = OP(
                "a", lambda En, o_=s_s[:, po, 512 * (NT - 1):512 * NT],
                b_=vec("bo")[:, po:po + 1], p_=pf[NT - 1]:
                En.activation(o_, p_[:], AF.Gelu, bias=b_, scale=1.0),
                waits=[cover], inc="sA")

        for oc in range(NCH):
            lastm = [None] * NT
            w0 = [tok_wo]
            for ic in range(NCH):
                for t4 in range(NT):
                    lastm[t4] = OP(
                        "t", lambda En, l_=s_wo[:, ic, 128 * oc:128 * (oc + 1)],
                        r_=s_h2[:, ic, 512 * t4:512 * (t4 + 1)], st=(ic == 0),
                        sp=(ic == NCH - 1), p_=pf[t4]:
                        En.matmul(p_[:], l_, r_, start=st, stop=sp),
                        waits=w0 + ([pf_free[t4]] if ic == 0 else []),
                        inc=("sT" if ic == NCH - 1 or (ic == 0 and t4 == 0)
                             else None))
                    w0 = []
                    if a3_pend is not None:
                        a3_flush(a3_pend, lastm[0])
                        a3_pend = None
            for t4 in range(NT - 1):
                pf_free[t4] = OP(
                    "a", lambda En, o_=s_s[:, oc, 512 * t4:512 * (t4 + 1)],
                    b_=vec("bo")[:, oc:oc + 1], p_=pf[t4]:
                    En.activation(o_, p_[:], AF.Gelu, bias=b_, scale=1.0),
                    waits=[lastm[t4 + 1]], inc="sA")
            a3_pend = oc
        a3_flush(a3_pend, gfence(pf[4][0:1, 0:128]))
        a3_pend = None

        if _DBG:
            tD = DMA("s", d_dbg_s, s_s[:].rearrange("p c t -> p (c t)"), "dD",
                     waits=[sch.tok("sA")])
            OP("s", lambda En: En.nop(), waits=[tD])
        sch.barrier()

        tok_w1b = DMA("g", s_w1b[:], d_w1[:, 2048:4096].rearrange("(i p) o -> p i o", p=128),
                      "dW1b")
        pf_free = [None] * 6
        tok_hdma = [None] * 32
        tok_w2a = None
        f1_pend = None

        def f1_flush(po, cover):
            pf_free[NT - 1] = OP(
                "a", lambda En, o_=s_hst[po % 2][:, 512 * (NT - 1):512 * NT],
                b_=s_b1f[:, po:po + 1], p_=pf[NT - 1]:
                En.activation(o_, p_[:], AF.Relu, bias=b_, scale=1.0),
                waits=[cover], inc="sA")
            tok_hdma[po] = DMA("a", d_h[po], s_hst[po % 2][:], f"dH{po % 4}",
                               waits=[pf_free[t4] for t4 in range(NT)])

        for oc in range(32):
            wb_ = s_w1a if oc < 16 else s_w1b
            wtok = tok_w1a if oc < 16 else tok_w1b
            col = 128 * (oc % 16)
            lastm = [None] * NT
            w0 = [wtok]
            if oc >= 2:
                w0.append(tok_hdma[oc - 2])
            for ic in range(NCH):
                for t4 in range(NT):
                    lastm[t4] = OP(
                        "t", lambda En, l_=wb_[:, ic, col:col + 128],
                        r_=s_xhat[:, ic, 2 + 512 * t4:2 + 512 * (t4 + 1)],
                        st=(ic == 0), sp=(ic == NCH - 1), p_=pf[t4]:
                        En.matmul(p_[:], l_, r_, start=st, stop=sp),
                        waits=w0 + ([pf_free[t4]] if ic == 0 else []),
                        inc=("sT" if ic == NCH - 1 or (ic == 0 and t4 == 0)
                             else None))
                    w0 = []
                    if f1_pend is not None:
                        f1_flush(f1_pend, lastm[0])
                        f1_pend = None
            for t4 in range(NT - 1):
                pf_free[t4] = OP(
                    "a", lambda En, o_=s_hst[oc % 2][:, 512 * t4:512 * (t4 + 1)],
                    b_=s_b1f[:, oc:oc + 1], p_=pf[t4]:
                    En.activation(o_, p_[:], AF.Relu, bias=b_, scale=1.0),
                    waits=[lastm[t4 + 1]], inc="sA")
            f1_pend = oc
            if oc == 15:
                tok_w2a = DMA("g", s_w2a[:],
                              d_w2[0:2048, :].rearrange("(i p) o -> p i o", p=128),
                              "dW2a", waits=[lastm[NT - 1]])
        f1_flush(f1_pend, gfence(pf[4][0:1, 0:128]))
        f1_pend = None

        sch.barrier()

        tok_w2b = DMA("g", s_w2b[:],
                      d_w2[2048:4096, :].rearrange("(i p) o -> p i o", p=128), "dW2b")
        all_h = [sch.tok(f"dH{i}") for i in range(4)]
        pf_free = [None] * 6
        pb_free = [None, None]
        tok_hinA_free = None
        tok_hinB_free = None
        tok_xre_free = None
        tok_out_dma = None
        out_dma_bi = [None, None]
        for t in range(NT):
            tsl = (512 * t, 512 * (t + 1))
            thA = DMA("s", s_hinA[:],
                      d_h[0:16, :, tsl[0]:tsl[1]].rearrange("i p t -> p i t"),
                      "dHA", waits=all_h + [tok_hinA_free])
            thB = DMA("s", s_hinB[:],
                      d_h[16:32, :, tsl[0]:tsl[1]].rearrange("i p t -> p i t"),
                      "dHB", waits=[tok_hinB_free])
            txr = DMA("s", s_xre[:],
                      d_xs[2 + tsl[0]:2 + tsl[1], :].rearrange("(b p) d -> p b d", p=128),
                      "dXR", waits=[tok_xre_free])
            tok_sg = [None] * NCH
            f2_pend = None

            def f2_flush(pv, cover):
                po, pp = pv
                tgs = OP("a", lambda En, o_=s_gsc[po % 2][:], p_=pp,
                         b_=vec("b2f")[:, po:po + 1]:
                         En.activation(o_, p_[:], AF.Silu, bias=b_, scale=1.0),
                         waits=[cover], inc="sA")
                tok_sg[po] = OP(
                    "v", lambda En, o_=s_sg[:, po, :], a_=s_gsc[po % 2][:],
                    b_=s_s[:, po, tsl[0]:tsl[1]]:
                    En.tensor_mul(o_, a_, b_), waits=[tgs], inc="sV")
                pf_free[po % 2] = tok_sg[po]

            for oc in range(NCH):
                p5 = pf[oc % 2]
                w0 = [thA, thB, tok_w2a, tok_w2b, pf_free[oc % 2]]
                for ic in range(32):
                    wb_ = s_w2a if ic < 16 else s_w2b
                    hb_ = s_hinA if ic < 16 else s_hinB
                    mmt = OP("t", lambda En, l_=wb_[:, ic % 16, 128 * oc:128 * (oc + 1)],
                             r_=hb_[:, ic % 16, :], st=(ic == 0), sp=(ic == 31), p_=p5:
                             En.matmul(p_[:], l_, r_, start=st, stop=sp),
                             waits=w0, inc=("sT" if ic in (0, 31) else None))
                    w0 = []
                    if f2_pend is not None:
                        f2_flush(f2_pend, mmt)
                        f2_pend = None
                f2_pend = (oc, p5)
            f2_flush(f2_pend, gfence(pf[3][0:1, 0:128]))
            f2_pend = None
            for half in range(2):
                touts = []
                for bi in range(2):
                    b4 = 2 * half + bi
                    wv0 = [tok_sg[NCH - 1], pb_free[bi]]
                    if half == 0 and out_dma_bi[bi] is not None:
                        wv0.append(out_dma_bi[bi])
                    for oc in range(NCH):
                        OP("t", lambda En, o_=pb[bi][:, 128 * oc:128 * (oc + 1)],
                           i=s_sg[:, oc, 128 * b4:128 * (b4 + 1)]:
                           En.transpose(o_, i, s_idb[:]), waits=wv0)
                        wv0 = []
                    ttr = OP("t", lambda En:
                             En.matmul(pf[2][0:1, 0:128], s_onc[0:1, 0:1], s_on1[0:1, :],
                                       start=True, stop=True), inc="sT")
                    wadd = [ttr, txr]
                    if out_dma_bi[bi] is not None:
                        wadd.append(out_dma_bi[bi])
                    if t == NT - 1 and half == 1:
                        OP("v", lambda En: En.nop() if _SIMGF else
                           En.nop(cycle_cnt=3000), waits=[ttr])
                    touts.append(OP(
                        "v", lambda En, o_=s_outst[:, bi, :], p_=pb[bi],
                        x_=s_xre[:, b4, :]:
                        En.tensor_tensor(o_, p_[:], x_, ALU.add),
                        waits=wadd, inc="sV"))
                    pb_free[bi] = touts[-1]
                    tcm = OP("v", lambda En: En.nop() if _SIMGF else
                             En.nop(cycle_cnt=700), waits=[touts[-1]],
                             inc="sV")
                    r0 = tsl[0] + 256 * half + 128 * bi
                    out_dma_bi[bi] = DMA(
                        "a", d_y[r0:r0 + 128, :],
                        s_outst[:, bi, :], ("dO", "dO1")[bi], waits=[tcm])
                tok_out_dma = out_dma_bi[1]
            tok_hinA_free = sch.tok("sT")
            tok_hinB_free = sch.tok("sT")
            tok_xre_free = sch.tok("sV")

        if _DBG3:
            DMA("s", d_dbg_sg, s_sg[:].rearrange("p c t -> p (c t)"), "dD",
                waits=[sch.tok("sV")])
            DMA("s", d_dbg_gsc[:, 0:TT], s_gsc[0][:], "dD")
            DMA("s", d_dbg_gsc[:, TT:2 * TT], s_gsc[1][:], "dD")
            tD3 = DMA("s", d_dbg_s2, s_s[:].rearrange("p c t -> p (c t)"), "dD")
            OP("s", lambda En: En.nop(), waits=[tD3])
        OP("a", lambda En: En.nop(), waits=[sch.tok("dO"), sch.tok("dO1")])
        OP("s", lambda En: En.nop(),
           waits=[sch.tok("dO"), sch.tok("dO1"), sch.tok("dD")])

        with nc.Block() as blk:
            def runner(key):
                def body(Eng):
                    for needed, fn, inc, inck in sch.q[key]:
                        for sname, val in needed:
                            Eng.wait_ge(sems[sname], val)
                        ins = fn(Eng)
                        if inc is not None:
                            ins.then_inc(sems[inc], inck)
                return body
            blk.tensor(runner("t"))
            blk.vector(runner("v"))
            blk.scalar(runner("a"))
            blk.gpsimd(runner("g"))
            blk.sync(runner("s"))

    return nc


_NC_CACHE = None


def _get_nc():
    global _NC_CACHE
    if _NC_CACHE is None:
        _NC_CACHE = build_program()
    return _NC_CACHE


def _host_consts():
    f = np.float32
    idm = np.eye(128, dtype=f)
    sel = np.zeros((16, NCH, 128), f)
    ind = np.zeros((128, NCH, 16), f)
    for c in range(NCH):
        for dd in range(128):
            h = 2 * c + (1 if dd >= 64 else 0)
            sel[h, c, dd] = 1.0
            ind[dd, c, h] = 0.125
    ones1 = np.ones((1, 128), f)
    onesc = np.ones((128, 1), f)
    return idm, sel.reshape(16, NCH * 128), ind.reshape(128, NCH * 16), ones1, onesc


def _masks_for_core(t0):
    f = np.float32
    mpre = np.ones((16, NT, 6, 16), f)
    mpost = np.ones((16, NT, 6, 16), f)
    for t in range(NT):
        for col in range(16):
            if col < 8:
                g = t0 + 512 * t + col
            else:
                g = t0 + 512 * t + 504 + (col - 8)
            for j in range(3):
                if not (g >= max(1, j)):
                    mpre[:, t, j, col] = 0.0
                if not (g >= j):
                    mpost[:, t, j, col] = 0.0
                if not (g <= min(SEQ - 2, SEQ - 1 - j)):
                    mpre[:, t, 3 + j, col] = 0.0
                if not (g <= SEQ - 1 - j):
                    mpost[:, t, 3 + j, col] = 0.0
    return mpre.reshape(16, NT * 6 * 16), mpost.reshape(16, NT * 6 * 16)


def kernel(L0_attn_input_feature, L0_attn_Wq, L0_attn_bq, L0_attn_Wk, L0_attn_bk,
           L0_attn_Wv, L0_attn_bv, L0_attn_Wfb, L0_attn_bfb, L0_attn_Wo,
           L0_attn_bo, L0_attn_g1, L0_attn_b1, L0_attn_g2, L0_attn_b2,
           L0_ffn_W1, L0_ffn_b1, L0_ffn_W2, L0_ffn_b2):
    f = np.float32
    x = np.ascontiguousarray(np.asarray(L0_attn_input_feature, f))
    W = {k: np.ascontiguousarray(np.asarray(v, f)) for k, v in dict(
        Wq=L0_attn_Wq, Wk=L0_attn_Wk, Wv=L0_attn_Wv, Wfb=L0_attn_Wfb,
        Wo=L0_attn_Wo, W1=L0_ffn_W1, W2=L0_ffn_W2).items()}
    vl = [L0_attn_g1, L0_attn_b1, L0_attn_g2, L0_attn_b2, L0_attn_bq,
          L0_attn_bk, L0_attn_bv, L0_attn_bfb, L0_attn_bo, L0_ffn_b2]
    vecs = np.stack([np.asarray(v, f).reshape(NCH, 128).T for v in vl], axis=1)
    vecs = np.ascontiguousarray(vecs.reshape(128, 80))
    b1f = np.ascontiguousarray(np.asarray(L0_ffn_b1, f).reshape(32, 128).T)
    vb4 = np.stack([np.repeat(np.asarray(v, f).reshape(NCH, 128).T[:, :, None],
                              4, axis=2)
                    for v in (L0_attn_bq, L0_attn_bk, L0_attn_bv)], axis=1)
    vb4 = np.ascontiguousarray(vb4.reshape(128, 3, 32))
    idm, sel, ind, ones1, onesc = _host_consts()

    zeros2 = np.zeros((2, D), f)
    in_maps = []
    for core in range(8):
        b, half = divmod(core, 2)
        if half == 0:
            xs = np.concatenate([zeros2, x[b, 0:TC + 2]], axis=0)
        else:
            xs = np.concatenate([x[b, TC - 2:2 * TC], zeros2], axis=0)
        mpre, mpost = _masks_for_core(half * TC)
        in_maps.append(dict(
            xs=np.ascontiguousarray(xs), vecs=vecs, b1f=b1f, vb4=vb4, idm=idm,
            sel=sel, ind=ind, ones1=ones1, onesc=onesc, mpre=mpre, mpost=mpost,
            **W))

    nc = _get_nc()
    res = run_bass_kernel_spmd(nc, in_maps, core_ids=list(range(8)))
    if _DBG or _DBG3:
        global _DBG_RES
        _DBG_RES = res.results
    ys = [res.results[i]["y"] for i in range(8)]
    out = np.stack([np.concatenate([ys[2 * b], ys[2 * b + 1]], axis=0)
                    for b in range(4)])
    return np.ascontiguousarray(out.astype(np.float32))


# version 11
# speedup vs baseline: 1.0180x; 1.0180x over previous
import numpy as np
import ml_dtypes
from contextlib import ExitStack

import concourse.bass as bass
import concourse.mybir as mybir
from concourse.bass_utils import run_bass_kernel_spmd

F32 = mybir.dt.float32
BF16 = mybir.dt.bfloat16
AF = mybir.ActivationFunctionType
ALU = mybir.AluOpType
AX = mybir.AxisListType.X

D = 1024
NCH = 8
TC = 2048
NL = 2052
TT = 512
NT = TC // TT
E = TT + 4
NB = 17
SEQ = 4096

_DBG = False
_SIMGF = False
_DBG3 = False

_SEMS = (["dD", "dX0", "dX1", "dX2", "dHA", "dHB", "dXR", "dB0", "dB1", "dB2",
          "dC", "dC2", "dWo", "dW1a", "dW1b", "dW2a", "dW2b", "dH0", "dH1",
          "dH2", "dH3",
          "dO", "dO1", "sT", "sV", "sA", "sG", "sS", "bt", "bv", "ba", "bg",
          "bs"])


class Sch:
    ENG = "tvags"

    def __init__(self):
        self.q = {k: [] for k in self.ENG}
        self.cnt = {}
        self.lastw = {k: {} for k in self.ENG}

    def tok(self, sem):
        return (sem, self.cnt.get(sem, 0))

    def op(self, eng, fn, waits=(), inc=None, inck=1):
        lw = self.lastw[eng]
        needed = []
        for w in waits:
            if w is None:
                continue
            sname, val = w
            if val > 0 and lw.get(sname, 0) < val:
                lw[sname] = val
                needed.append((sname, val))
        out_tok = None
        if inc is not None:
            self.cnt[inc] = self.cnt.get(inc, 0) + inck
            out_tok = (inc, self.cnt[inc])
        self.q[eng].append((needed, fn, inc, inck))
        return out_tok

    def barrier(self):
        toks = {}
        for e in self.ENG:
            self.op(e, lambda E: E.drain())
            toks[e] = self.op(e, lambda E: E.nop(), inc="b" + e)
        for e in self.ENG:
            self.op(e, lambda E: E.nop(),
                    waits=[toks[o] for o in self.ENG if o != e])


def build_program():
    nc = bass.Bass("TRN2", target_bir_lowering=False, detect_race_conditions=False)

    d_xs = nc.dram_tensor("xs", [NL, D], F32, kind="ExternalInput").ap()
    d_wq = nc.dram_tensor("Wq", [D, D], F32, kind="ExternalInput").ap()
    d_wk = nc.dram_tensor("Wk", [D, D], F32, kind="ExternalInput").ap()
    d_wv = nc.dram_tensor("Wv", [D, D], F32, kind="ExternalInput").ap()
    d_wfb = nc.dram_tensor("Wfb", [2 * D, D], F32, kind="ExternalInput").ap()
    d_wo = nc.dram_tensor("Wo", [D, D], F32, kind="ExternalInput").ap()
    d_w1 = nc.dram_tensor("W1", [D, 4 * D], F32, kind="ExternalInput").ap()
    d_w2 = nc.dram_tensor("W2", [4 * D, D], F32, kind="ExternalInput").ap()
    d_vecs = nc.dram_tensor("vecs", [128, 80], F32, kind="ExternalInput").ap()
    d_b1f = nc.dram_tensor("b1f", [128, 32], F32, kind="ExternalInput").ap()
    d_vb4 = nc.dram_tensor("vb4", [128, 3, 32], F32, kind="ExternalInput").ap()
    d_id = nc.dram_tensor("idm", [128, 128], F32, kind="ExternalInput").ap()
    d_sel = nc.dram_tensor("sel", [16, NCH * 128], F32, kind="ExternalInput").ap()
    d_ind = nc.dram_tensor("ind", [128, NCH * 16], F32, kind="ExternalInput").ap()
    d_ones1 = nc.dram_tensor("ones1", [1, 128], F32, kind="ExternalInput").ap()
    d_onesc = nc.dram_tensor("onesc", [128, 1], F32, kind="ExternalInput").ap()
    d_mpre = nc.dram_tensor("mpre", [16, NT * 6 * 16], F32, kind="ExternalInput").ap()
    d_mpost = nc.dram_tensor("mpost", [16, NT * 6 * 16], F32, kind="ExternalInput").ap()
    d_y = nc.dram_tensor("y", [TC, D], F32, kind="ExternalOutput").ap()
    d_h = nc.dram_tensor("hscr", [32, 128, TC], BF16, kind="Internal").ap()
    if _DBG3:
        d_dbg_sg = nc.dram_tensor("dbg_sg", [128, NCH * TT], BF16, kind="ExternalOutput").ap()
        d_dbg_gsc = nc.dram_tensor("dbg_gsc", [128, 2 * TT], F32, kind="ExternalOutput").ap()
        d_dbg_s2 = nc.dram_tensor("dbg_s2", [128, NCH * TC], F32, kind="ExternalOutput").ap()
    if _DBG:
        d_dbg_st = nc.dram_tensor("dbg_st", [128, NB * 16], F32, kind="ExternalOutput").ap()
        d_dbg_xhat = nc.dram_tensor("dbg_xhat", [128, NCH * NL], BF16, kind="ExternalOutput").ap()
        d_dbg_crow = nc.dram_tensor("dbg_crow", [1, NL], F32, kind="ExternalOutput").ap()
        d_dbg_h2 = nc.dram_tensor("dbg_h2", [128, NCH * TC], BF16, kind="ExternalOutput").ap()
        d_dbg_s = nc.dram_tensor("dbg_s", [128, NCH * TC], F32, kind="ExternalOutput").ap()

    sch = Sch()

    with ExitStack() as ctx:
        en = ctx.enter_context

        regions = []

        BASE = 16512

        def at(name, shape, dtype, off, stages):
            t = nc.alloc_sbuf_tensor_at(name, shape, dtype, offset=BASE + off)
            nbytes = int(np.prod(shape[1:])) * mybir.dt.size(dtype)
            regions.append((name, off, off + nbytes, frozenset(stages)))
            return t

        ALL = "0AWF2"

        s_id = at("c_id", [128, 128], F32, 0, ALL)
        s_idb = at("c_idb", [128, 128], BF16, 512, ALL)
        s_sel = at("c_sel", [16, NCH, 128], BF16, 768, ALL)
        s_ind = at("c_ind", [128, NCH, 16], BF16, 2816, ALL)
        s_on1 = at("c_on1", [1, 128], BF16, 3072, ALL)
        s_onc = at("c_onc", [128, 1], BF16, 3328, ALL)
        s_vecs = at("c_vecs", [128, 10, 8], F32, 3360, ALL)
        s_b1f = at("c_b1f", [128, 32], F32, 3680, ALL)
        s_mpre = at("c_mpre", [16, NT, 6, 16], F32, 3808, ALL)
        s_mpost = at("c_mpost", [16, NT, 6, 16], F32, 5344, ALL)
        VI = {n: i for i, n in enumerate(
            ["g1", "b1", "g2", "b2", "bq", "bk", "bv", "bfb", "bo", "b2f"])}

        def vec(n):
            return s_vecs[:, VI[n], :]

        s_xhat = at("s_xhat", [128, NCH, NL], BF16, 6912, "0AWF")
        s_w2b = at("s_w2b", [128, 16, D], BF16, 6912, "2")
        s_crow = at("s_crow", [1, NL], F32, 39776, "0A")
        s_sg = at("s_sg", [128, NCH, TT], BF16, 39776, "2")

        s_xtok = [at(f"s_xtok{i}", [128, D], F32, 48000 + 4096 * i, "0")
                  for i in range(3)]
        s_xht = [at(f"s_xht{i}", [128, D], BF16, 60288 + 2048 * i, "0")
                 for i in range(2)]
        s_sq = at("s_sq", [128, D], F32, 64384, "0")
        s_scol = at("s_scol", [128, 16], F32, 68480, "0")
        s_ccols = at("s_ccols", [128, 20], F32, 68544, "0")

        s_SM = at("s_SM", [16, 6, TT], F32, 48000, "A")
        s_smxm = at("s_smxm", [16, TT], F32, 60288, "A")
        s_smxr = at("s_smxr", [16, TT], F32, 62336, "A")
        s_wo = at("s_wo", [128, NCH, D], BF16, 48000, "W")
        s_on1f = at("c_on1f", [1, 128], F32, 64384, "A")
        s_vb4 = at("c_vb4", [128, 3, 32], F32, 67968, "A")
        s_SMb = at("s_SMb", [16, 3, TT], BF16, 64896, "A")
        s_hinA = at("s_hinA", [128, 16, TT], BF16, 48000, "2")

        s_h2 = at("s_h2", [128, NCH, TC], BF16, 68672, "AW")
        s_w1b = at("s_w1b", [128, NCH, 2048], BF16, 68672, "F")
        s_hinB = at("s_hinB", [128, 16, TT], BF16, 68672, "2")
        s_xre = at("s_xre", [128, 4, D], F32, 85056, "2")

        A12 = 101504
        s_B = [at(f"s_B{i}", [128, NCH, D], BF16, A12 + 16384 * i, "A")
               for i in range(3)]
        s_dbgst = nc.alloc_sbuf_tensor_at("s_dbgst", [128, NB, 16], F32,
                                          offset=BASE + 68672)
        s_stat = at("s_stat", [128, 15, 20], F32, 150656, "0")
        s_xpb = at("s_xpb", [128, NCH, E], BF16, 150656, "A")
        s_rbf = at("s_rbf", [128, NCH, TT], BF16, 150656, "A")
        s_q = at("s_q", [128, NCH, E], BF16, 158912, "A")
        s_k = at("s_k", [128, NCH, E], BF16, 167168, "A")
        s_v = at("s_v", [128, NCH, E], BF16, 175424, "A")
        s_u = at("s_u", [128, NCH, E - 2], BF16, 183680, "A")
        s_u2 = nc.alloc_sbuf_tensor_at("s_u2", [128, NCH, E - 2], BF16,
                                       offset=BASE + 150656)
        s_SMb2 = nc.alloc_sbuf_tensor_at("s_SMb2", [16, 3, TT], BF16,
                                         offset=BASE + 202144)
        s_ao = at("s_ao", [128, 16, TT], BF16, 183680, "A")
        s_cb = at("s_cb", [128, E], F32, 200064, "A")
        s_t1 = at("s_t1", [128, E], F32, 202144, "A")
        s_x2 = at("s_x2", [128, TT], F32, 204224, "A")
        s_wb = [at(f"s_wb{i}", [128, TT], BF16, 206272 + 1024 * i, "A")
                for i in range(2)]
        s_wvp = [at(f"s_wvp{i}", [128, TT], BF16, 208320 + 1024 * i, "A")
                 for i in range(2)]
        s_rsq = [at(f"s_rsq{i}", [128, TT], BF16, 210368 + 1024 * i, "A")
                 for i in range(2)]

        s_s = at("s_sgate", [128, NCH, TC], F32, A12, "WF2")
        s_w1a = at("s_w1a", [128, NCH, 2048], BF16, 167040, "WF")
        s_w2a = at("s_w2a", [128, 16, D], BF16, 167040, "2")
        s_hst = [at(f"s_hst{i}", [128, TC], BF16, 199808 + 4096 * i, "F")
                 for i in range(2)]
        s_gsc = [at(f"s_gsc{i}", [128, TT], F32, 199808 + 2048 * i, "2")
                 for i in range(2)]
        s_outst = at("s_outst", [128, 2, D], F32, 203904, "2")

        for i in range(len(regions)):
            for j in range(i + 1, len(regions)):
                n1, a1, b1, st1 = regions[i]
                n2, a2, b2, st2 = regions[j]
                if a1 < b2 and a2 < b1 and (st1 & st2):
                    allowed = {("s_xpb", "s_rbf"), ("s_u", "s_ao")}
                    if (n1, n2) not in allowed and (n2, n1) not in allowed:
                        raise AssertionError(f"overlap {n1} {n2}")
                if b1 > 212700 or b2 > 212700:
                    raise AssertionError("sbuf overflow")

        pf = [en(nc.psum_tensor(f"pf{i}", [128, 512], F32)) for i in range(6)]
        pb = [en(nc.psum_tensor(f"pbb{i}", [128, 1024], BF16)) for i in range(2)]

        sems = {n: en(nc.semaphore(n)) for n in _SEMS}

        def OP(eng, fn, waits=(), inc=None, inck=1):
            return sch.op(eng, fn, waits, inc, inck)

        def gfence(target=None):
            if _SIMGF:
                return OP("t", lambda En, t_=target:
                          En.matmul(t_, s_onc[0:1, 0:1], s_on1[0:1, :],
                                    start=True, stop=True), inc="sT")
            anchor = sch.tok("sT")
            return OP("t", lambda En: En.nop(cycle_cnt=700), waits=[anchor],
                      inc="sT")

        def DMA(eng, out, in_, sem, waits=(), **kw):
            return OP(eng,
                      (lambda En, o_=out, i_=in_, k_=dict(kw): En.dma_start(out=o_, in_=i_, **k_)),
                      waits=waits, inc=sem, inck=16)

        DMA("g", s_id[:], d_id, "dC")
        DMA("g", s_idb[:], d_id, "dC")
        DMA("g", s_sel[:], d_sel.rearrange("h (c d) -> h c d", c=NCH), "dC")
        DMA("g", s_ind[:], d_ind.rearrange("p (c h) -> p c h", c=NCH), "dC")
        DMA("g", s_on1[:], d_ones1, "dC")
        DMA("g", s_onc[:], d_onesc, "dC")
        DMA("g", s_vecs[:], d_vecs.rearrange("p (n c) -> p n c", n=10), "dC")
        DMA("g", s_b1f[:], d_b1f, "dC")
        DMA("g", s_mpre[:], d_mpre.rearrange("h (t j c) -> h t j c", t=NT, j=6), "dC")
        tok_consts = DMA("g", s_mpost[:], d_mpost.rearrange("h (t j c) -> h t j c", t=NT, j=6), "dC")

        def wslice(d_w, lo_chunk, n_chunk=8, cols=None):
            ap = d_w[128 * lo_chunk:128 * (lo_chunk + n_chunk), :]
            if cols is not None:
                ap = ap[:, cols[0]:cols[1]]
            return ap.rearrange("(i p) o -> p i o", p=128)

        tok_wq = [None] * (NT + 1)
        tok_wk = [None] * (NT + 1)
        tok_wv = [None] * (NT + 1)
        tok_wq[0] = DMA("g", s_B[0][:], wslice(d_wq, 0), "dB0")
        tok_wk[0] = DMA("g", s_B[1][:], wslice(d_wk, 0), "dB1")
        tok_wv[0] = DMA("g", s_B[2][:], wslice(d_wv, 0), "dB2")

        OP("v", lambda En: En.memset(s_ccols[:], 0.0))

        tok_xtok_free = [None] * NB
        tok_xht_free = [None] * NB
        tok_pb_evac = [None, None]
        tok_pc_evac = None
        tok_load = [None] * NB
        ST = s_stat

        for g0 in range(0, NB, 3):
            grp = list(range(g0, min(g0 + 3, NB)))
            g1 = grp[-1] + 1
            for b in grp:
                nb_ = 4 if b == NB - 1 else 128
                w = [tok_xtok_free[b - 3]] if b >= 3 else []
                tok_load[b] = DMA("s", s_xtok[b % 3][0:nb_, :],
                                  d_xs[128 * b:128 * b + nb_, :],
                                  f"dX{b % 3}", waits=w)
            rtoks = []
            for b in grp:
                xt = s_xtok[b % 3]
                rtoks.append(OP("v", lambda En, a=ST[:, 0, b:b + 1], i=xt[:]:
                                En.tensor_reduce(a, i, AX, ALU.add),
                                waits=[tok_load[b]], inc="sV"))
                rtoks.append(OP("v", lambda En, i=xt[:], a=ST[:, 1, b:b + 1]:
                                En.scalar_tensor_tensor(
                                    s_sq[:], i, 1.0, i, ALU.mult, ALU.mult,
                                    accum_out=a), inc="sV"))
            w1 = OP("v", lambda En, o_=ST[:, 2, g0:g1], a_=ST[:, 0, g0:g1]:
                    En.tensor_scalar_mul(o_, a_, 1.0 / D),
                    waits=rtoks, inc="sV")
            OP("v", lambda En, o_=ST[:, 3, g0:g1], a_=ST[:, 1, g0:g1]:
               En.tensor_scalar_mul(o_, a_, 1.0 / D))
            w2 = OP("v", lambda En, o_=ST[:, 4, g0:g1], a_=ST[:, 2, g0:g1]:
                    En.tensor_mul(o_, a_, a_), waits=[w1], inc="sV")
            w3 = OP("v", lambda En, o_=ST[:, 5, g0:g1], a_=ST[:, 3, g0:g1],
                    b_=ST[:, 4, g0:g1]:
                    En.tensor_sub(o_, a_, b_), waits=[w2], inc="sV")
            tA = OP("v", lambda En, o_=ST[:, 6, g0:g1], a_=ST[:, 5, g0:g1]:
                    En.tensor_scalar_add(o_, a_, 1e-5), waits=[w3], inc="sV")
            wa1 = OP("a", lambda En, o_=ST[:, 7, g0:g1], a_=ST[:, 6, g0:g1]:
                     En.activation(o_, a_, AF.Ln), waits=[tA], inc="sA")
            OP("a", lambda En, o_=ST[:, 8, g0:g1], a_=ST[:, 7, g0:g1]:
               En.activation(o_, a_, AF.Exp, scale=-0.5), waits=[wa1])
            tB = OP("a", lambda En, o_=ST[:, 9, g0:g1], a_=ST[:, 7, g0:g1]:
                    En.activation(o_, a_, AF.Exp, scale=-1.0), inc="sA")
            w4 = OP("v", lambda En, o_=ST[:, 10, g0:g1], a_=ST[:, 5, g0:g1],
                    b_=ST[:, 9, g0:g1]:
                    En.tensor_mul(o_, a_, b_), waits=[tB], inc="sV")
            tC = OP("v", lambda En, o_=ST[:, 11, g0:g1], a_=ST[:, 10, g0:g1]:
                    En.tensor_scalar_add(o_, a_, 1e-6), waits=[w4], inc="sV")
            wa2 = OP("a", lambda En, o_=ST[:, 12, g0:g1], a_=ST[:, 11, g0:g1]:
                     En.activation(o_, a_, AF.Ln), waits=[tC], inc="sA")
            tD = OP("a", lambda En, o_=ST[:, 13, g0:g1], a_=ST[:, 12, g0:g1]:
                    En.activation(o_, a_, AF.Exp, scale=-0.5),
                    waits=[wa2], inc="sA")
            w5b = OP("v", lambda En, o_=ST[:, 14, g0:g1], a_=ST[:, 2, g0:g1],
                     b_=ST[:, 8, g0:g1]:
                     En.scalar_tensor_tensor(o_, a_, -1.0, b_,
                                             ALU.mult, ALU.mult),
                     waits=[tD], inc="sV")

            for b in grp:
                nb_ = 4 if b == NB - 1 else 128
                xs_ = b % 2
                L0 = 128 * b
                xt = s_xtok[b % 3]
                wx = [w5b]
                if b >= 2:
                    wx.append(tok_xht_free[b - 2])
                tE = OP("v", lambda En, o_=s_xht[xs_][:], i=xt[:],
                        g_=ST[:, 8, b:b + 1], m_=ST[:, 14, b:b + 1]:
                        En.tensor_scalar(o_, i, g_, m_, ALU.mult, ALU.add),
                        waits=wx, inc="sV")
                tok_xtok_free[b] = tE

                wt = [tE, tok_consts]
                if tok_pb_evac[xs_] is not None:
                    wt.append(tok_pb_evac[xs_])
                for c in range(NCH):
                    OP("t", lambda En, o_=pb[xs_][:, 128 * c:128 * (c + 1)],
                       i=s_xht[xs_][:, 128 * c:128 * (c + 1)]:
                       En.transpose(o_, i, s_idb[:]), waits=wt)
                    wt = []
                wc = [tok_pc_evac] if tok_pc_evac else []
                OP("t", lambda En, i=ST[:, 13, b:b + 1]:
                   En.transpose(pf[0][0:1, 0:128], i, s_id[:]), waits=wc)
                tok_tr = OP("t", lambda En:
                            En.matmul(pf[1][0:1, 0:128], s_onc[0:1, 0:1],
                                      s_on1[0:1, :],
                                      start=True, stop=True), inc="sT")
                tok_xht_free[b] = tok_tr
                tok_pb_evac[xs_] = OP(
                    "a", lambda En, o_=s_xhat[:, :, L0:L0 + nb_],
                    i=pb[xs_][:].rearrange("p (c t) -> p c t", c=NCH)[:, :, 0:nb_]:
                    En.copy(o_, i), waits=[tok_tr], inc="sA")
                tok_pc_evac = OP(
                    "a", lambda En, o_=s_crow[0:1, L0:L0 + nb_],
                    i=pf[0][0:1, 0:nb_]:
                    En.copy(o_, i), waits=[tok_tr], inc="sA")

        sch.barrier()
        if _DBG:
            DMA("s", d_dbg_st, s_dbgst[:].rearrange("p b c -> p (b c)"), "dD")
            DMA("s", d_dbg_xhat, s_xhat[:].rearrange("p c t -> p (c t)"), "dD")
            DMA("s", d_dbg_crow, s_crow[:], "dD")

        tok_on1f = DMA("g", s_on1f[:], d_ones1, "dC2")
        tok_vb4 = DMA("g", s_vb4[:], d_vb4, "dC2")
        t_pf_free = [None] * 6
        tok_fa = [None] * NT
        tok_fb = [None] * NT
        tok_u_freed = None
        tok_xpb_free = None
        tok_ao_free = None

        for t in range(NT):
            e0 = TT * t
            qb = s_B[(0 - t) % 3]
            kb = s_B[(1 - t) % 3]
            vb = s_B[(2 - t) % 3]

            t1m = OP("t", lambda En, r_=s_crow[0:1, e0:e0 + 512]:
                     En.matmul(pf[4][:], s_on1f[:], r_, start=True, stop=True),
                     waits=[tok_consts, tok_on1f, tok_pc_evac, t_pf_free[4]], inc="sT")
            t2m = OP("t", lambda En, r_=s_crow[0:1, e0 + 512:e0 + E]:
                     En.matmul(pf[5][:, 0:4], s_on1f[:], r_, start=True, stop=True),
                     waits=[t_pf_free[5]], inc="sT")
            fcb = gfence(pf[5][32:33, 256:384])
            ce1 = OP("a", lambda En: En.copy(s_cb[:, 0:512], pf[4][:]),
                     waits=[fcb], inc="sA")
            ce2 = OP("a", lambda En: En.copy(s_cb[:, 512:E], pf[5][:, 0:4]),
                     inc="sA")
            t_pf_free[4] = ce1
            t_pf_free[5] = ce2

            tok_xpb = [None] * NCH
            for c in range(NCH):
                OP("v", lambda En, i=s_xhat[:, c, e0:e0 + E]:
                   En.tensor_tensor(s_t1[:], i, s_cb[:], ALU.mult),
                   waits=[ce1, ce2, tok_xpb_free])
                tok_xpb[c] = OP(
                    "v", lambda En, o_=s_xpb[:, c, :], g_=vec("g1")[:, c:c + 1],
                    b_=vec("b1")[:, c:c + 1]:
                    En.tensor_scalar(o_, s_t1[:], g_, b_, ALU.mult, ALU.add),
                    inc="sV")

            tok_qe = [None] * NCH
            tok_ke = [None] * NCH
            tok_ve = [None] * NCH
            evs = {"q": tok_qe, "k": tok_ke, "v": tok_ve}
            obuf = {"q": s_q, "k": s_k, "v": s_v}
            bias = {"q": "bq", "k": "bk", "v": "bv"}
            wtk = {"q": tok_wq[t], "k": tok_wk[t], "v": tok_wv[t]}
            wbf = {"q": qb, "k": kb, "v": vb}
            lastmm = {}
            qkv_pend = None

            def qkv_flush(pv, cover):
                wn_, oc_, p5_ = pv
                ev1 = OP("a", lambda En, o_=obuf[wn_][:, oc_, 0:512], p_=p5_,
                         b_=vec(bias[wn_])[:, oc_:oc_ + 1]:
                         En.activation(o_, p_[:], AF.Identity, bias=b_, scale=1.0),
                         waits=[cover], inc="sA")
                evs[wn_][oc_] = ev1
                t_pf_free[(0, 1, 4)[oc_ % 3]] = ev1
                if oc_ == NCH - 1:
                    wni = {"q": 0, "k": 1, "v": 2}[wn_]
                    ev2 = OP("a", lambda En, o_=s_t1[:, 32 * wni:32 * wni + 32],
                             p_=pf[2][:, 0:32]: En.copy(o_, p_), inc="sA")
                    t_pf_free[2] = ev2
                    OP("v", lambda En,
                       o_=obuf[wn_][:, :, 512:E],
                       a_=s_t1[:, 32 * wni:32 * wni + 32],
                       b_=s_vb4[:, wni:wni + 1, :]:
                       En.tensor_tensor(o_, a_, b_, ALU.add),
                       waits=[ev2, tok_vb4])

            for wn in ("q", "k", "v"):
                B_ = wbf[wn]
                for oc in range(NCH):
                    bmain = (0, 1, 4)[oc % 3]
                    p5 = pf[bmain]
                    w0 = [wtk[wn], t_pf_free[bmain]]
                    for ic in range(NCH):
                        mmm = OP("t", lambda En, l_=B_[:, ic, 128 * oc:128 * (oc + 1)],
                                 r_=s_xpb[:, ic, 0:512], st=(ic == 0), sp=(ic == NCH - 1),
                                 p_=p5:
                                 En.matmul(p_[:], l_, r_, start=st, stop=sp),
                                 waits=w0 + [tok_xpb[ic]],
                                 inc=("sT" if ic == 0 else None))
                        w0 = []
                        if ic == 0 and qkv_pend is not None:
                            qkv_flush(qkv_pend, mmm)
                            qkv_pend = None
                        mmt = OP("t", lambda En, l_=B_[:, ic, 128 * oc:128 * (oc + 1)],
                                 r_=s_xpb[:, ic, 512:E], st=(ic == 0), sp=(ic == NCH - 1),
                                 p_=pf[2], co=4 * oc:
                                 En.matmul(p_[:, co:co + 4], l_, r_, start=st, stop=sp),
                                 waits=([t_pf_free[2]] if ic == 0 and oc == 0 else []),
                                 inc=("sT" if ic == NCH - 1 else None))
                    qkv_pend = (wn, oc, p5)
                lastmm[wn] = mmt

            tok_fa[t] = DMA("g", qb[:], wslice(d_wfb, 0), f"dB{(0 - t) % 3}", waits=[lastmm["q"]])
            tok_fb[t] = DMA("g", kb[:], wslice(d_wfb, 8), f"dB{(1 - t) % 3}", waits=[lastmm["k"]])
            if t + 1 < NT:
                tok_wq[t + 1] = DMA("g", vb[:], wslice(d_wq, 0), f"dB{(2 - t) % 3}",
                                    waits=[lastmm["v"]])

            tok_sm = []
            prev_scores = [None, None]
            sc_pend = None
            for j in range(3):
                ub = (s_u, s_u2, s_u)[j]
                ut = [None] * NCH
                for c in range(NCH):
                    w = [tok_qe[c], tok_ke[c]]
                    if j == 0 and c == 0 and tok_u_freed is not None:
                        w.append(tok_u_freed)
                    if c == 0 and j >= 2 and prev_scores[j % 2] is not None:
                        w.append(prev_scores[j % 2])
                    ut[c] = OP("v", lambda En, o_=ub[:, c, :],
                               a_=s_q[:, c, 1:E - 1], b_=s_k[:, c, 2 - j:E - j]:
                               En.tensor_tensor(o_, a_, b_, ALU.mult),
                               waits=w, inc="sV")
                wF = [t_pf_free[4]]
                for c in range(NCH):
                    mmF = OP("t", lambda En, l_=s_ind[:, c, :], r_=ub[:, c, 0:512],
                             st=(c == 0), sp=(c == NCH - 1):
                             En.matmul(pf[4][0:16, :], l_, r_, start=st, stop=sp),
                             waits=wF + [ut[c]],
                             inc=("sT" if c in (0, NCH - 1) else None))
                    wF = []
                    if c == 0:
                        if qkv_pend is not None:
                            qkv_flush(qkv_pend, mmF)
                            qkv_pend = None
                        if sc_pend is not None:
                            e2 = OP("a", lambda En, jp=sc_pend,
                                    o_=s_SM[:, 3 + (2 - sc_pend), :]:
                                    En.copy(o_, pf[5][0:16, :]),
                                    waits=[mmF], inc="sA")
                            t_pf_free[5] = e2
                            tok_sm.append(e2)
                            sc_pend = None
                wR = [t_pf_free[5]]
                mmR0 = None
                for c in range(NCH):
                    mmR = OP("t", lambda En, l_=s_ind[:, c, :], r_=ub[:, c, 2:514],
                             st=(c == 0), sp=(c == NCH - 1):
                             En.matmul(pf[5][0:16, :], l_, r_, start=st, stop=sp),
                             waits=wR,
                             inc=("sT" if c in (0, NCH - 1) else None))
                    wR = []
                    if c == 0:
                        mmR0 = mmR
                e1 = OP("a", lambda En, o_=s_SM[:, j, :]:
                        En.copy(o_, pf[4][0:16, :]), waits=[mmR0], inc="sA")
                t_pf_free[4] = e1
                tok_sm.append(e1)
                sc_pend = j
                prev_scores[j % 2] = mmR
            fsc = gfence(pf[4][32:33, 256:384])
            e2 = OP("a", lambda En, o_=s_SM[:, 3 + (2 - 2), :]:
                    En.copy(o_, pf[5][0:16, :]), waits=[fsc], inc="sA")
            t_pf_free[5] = e2
            tok_sm.append(e2)
            sc_pend = None
            tok_u_read = sch.tok("sT")

            OP("v", lambda En, m_=s_mpre[:, t, :, 0:8]:
               En.tensor_tensor(s_SM[:, :, 0:8], s_SM[:, :, 0:8], m_, ALU.mult),
               waits=tok_sm)
            OP("v", lambda En, m_=s_mpre[:, t, :, 8:16]:
               En.tensor_tensor(s_SM[:, :, TT - 8:TT], s_SM[:, :, TT - 8:TT], m_, ALU.mult))

            tok_ao = [None] * 16
            par = 0
            wb_read = [None, None, None, None]
            tok_prev_wv = None
            wvbbuf = (s_wb[0], s_wb[1], s_rsq[0], s_rsq[1])

            def wv_flush(pv, cover_tok):
                i_, pw_, c_, j_, dd_ = pv
                wbb = wvbbuf[i_ % 4]
                ewb = OP("a", lambda En, o_=wbb[:], p_=pw_:
                         En.copy(o_, p_[:]),
                         waits=[cover_tok, wb_read[i_ % 4]], inc="sA")
                t_pf_free[2 + i_ % 4] = ewb
                vs = (2 - j_) if dd_ == 0 else (2 + j_)
                if j_ == 0:
                    tok_ao[8 * dd_ + c_] = OP(
                        "v", lambda En, o_=s_ao[:, 8 * dd_ + c_, :],
                        a_=wbb[:], b_=s_v[:, c_, vs:vs + 512]:
                        En.tensor_mul(o_, a_, b_),
                        waits=[ewb, tok_ve[c_]], inc="sV")
                    wb_read[i_ % 4] = tok_ao[8 * dd_ + c_]
                else:
                    wb_read[i_ % 4] = OP(
                        "v", lambda En, o_=s_wvp[i_ % 2][:],
                        a_=wbb[:], b_=s_v[:, c_, vs:vs + 512]:
                        En.tensor_mul(o_, a_, b_), waits=[ewb], inc="sV")
                    tok_ao[8 * dd_ + c_] = OP(
                        "v", lambda En, o_=s_ao[:, 8 * dd_ + c_, :],
                        a_=s_wvp[i_ % 2][:]:
                        En.tensor_add(o_, o_, a_), inc="sV")
            wv_pend = None
            tok_wf = [None, None]
            for dd in range(2):
                SMB = (s_SMb, s_SMb2)[dd]
                d0 = 3 * dd
                OP("v", lambda En, a_=s_SM[:, d0, :], b_=s_SM[:, d0 + 1, :]:
                   En.tensor_max(s_smxm[:], a_, b_))
                OP("v", lambda En, a_=s_SM[:, d0 + 2, :]:
                   En.tensor_max(s_smxm[:], s_smxm[:], a_))
                for j in range(3):
                    tsb = OP("v", lambda En, a_=s_SM[:, d0 + j, :]:
                             En.tensor_sub(a_, a_, s_smxm[:]),
                             inc=("sV" if j == 2 else None))
                tex = OP("a", lambda En, a_=s_SM[:, d0:d0 + 3, :]:
                         En.activation(a_, a_, AF.Exp),
                         waits=[tsb], inc="sA")
                OP("v", lambda En, a_=s_SM[:, d0, :], b_=s_SM[:, d0 + 1, :]:
                   En.tensor_add(s_smxr[:], a_, b_), waits=[tex])
                tsum = OP("v", lambda En, a_=s_SM[:, d0 + 2, :]:
                          En.tensor_add(s_smxr[:], s_smxr[:], a_), inc="sV")
                OP("a", lambda En: En.activation(s_smxm[:], s_smxr[:], AF.Ln),
                   waits=[tsum])
                trc = OP("a", lambda En: En.activation(s_smxr[:], s_smxm[:], AF.Exp, scale=-1.0),
                         inc="sA")
                for j in range(3):
                    OP("v", lambda En, a_=s_SM[:, d0 + j, :], o_=SMB[:, j, :]:
                       En.tensor_mul(o_, a_, s_smxr[:]),
                       waits=([trc, tok_prev_wv] if j == 0 else []))
                OP("v", lambda En, m_=s_mpre[0:16, t, d0:d0 + 3, 0:8],
                   o_=SMB[:, :, 0:8], mm_=s_mpost[:, t, d0:d0 + 3, 0:8]:
                   En.tensor_tensor(o_, o_, mm_, ALU.mult))
                tok_wf[dd] = OP("v", lambda En, o_=SMB[:, :, TT - 8:TT],
                                m_=s_mpost[:, t, d0:d0 + 3, 8:16]:
                                En.tensor_tensor(o_, o_, m_, ALU.mult),
                                inc="sV")

            for dd in range(2):
                SMB = (s_SMb, s_SMb2)[dd]
                wvw = [tok_wf[dd], tok_u_read]
                if dd == 0 and tok_ao_free is not None:
                    wvw.append(tok_ao_free)
                for c in range(NCH):
                    for j in range(3):
                        pw = pf[2 + par % 4]
                        mmW = OP("t", lambda En, l_=s_sel[:, c, :], r_=SMB[:, j, :],
                                 p_=pw:
                                 En.matmul(p_[:], l_, r_, start=True, stop=True),
                                 waits=wvw + [t_pf_free[2 + par % 4]], inc="sT")
                        wvw = []
                        if wv_pend is not None:
                            wv_flush(wv_pend, mmW)
                        wv_pend = (par, pw, c, j, dd)
                        par += 1
            tok_prev_wv = sch.tok("sT")

            tok_rbf = [None] * NCH
            tok_rsq = [None] * NCH
            stat_mm2 = [None] * NCH
            def emit_stats(so):
                wS = [t_pf_free[4], tok_rbf[so]] if so == 0 else [tok_rbf[so]]
                mS = OP("t", lambda En, r_=s_rbf[:, so, :], st=(so == 0),
                        sp=(so == NCH - 1):
                        En.matmul(pf[4][0:1, :], s_onc[:], r_, start=st, stop=sp),
                        waits=wS, inc=("sT" if so == NCH - 1 else None))
                wS2 = [t_pf_free[5], tok_rsq[so]] if so == 0 else [tok_rsq[so]]
                stat_mm2[so] = mS2 = OP(
                    "t", lambda En, r_=s_rsq[so % 2][:], st=(so == 0),
                    sp=(so == NCH - 1):
                    En.matmul(pf[5][0:1, :], s_onc[:], r_, start=st, stop=sp),
                    waits=wS2, inc="sT")
                return mS, mS2

            for oc in range(NCH):
                p5 = pf[oc % 2]
                w0 = [tok_fa[t], tok_fb[t], t_pf_free[oc % 2]]
                for ic in range(16):
                    B_ = qb if ic < 8 else kb
                    mmt = OP("t", lambda En, l_=B_[:, ic % 8, 128 * oc:128 * (oc + 1)],
                             r_=s_ao[:, ic, :], st=(ic == 0), sp=(ic == 15), p_=p5:
                             En.matmul(p_[:], l_, r_, start=st, stop=sp),
                             waits=w0 + [tok_ao[ic]],
                             inc=("sT" if ic == 15 else None))
                    w0 = []
                    if wv_pend is not None:
                        wv_flush(wv_pend, mmt)
                        wv_pend = None
                if oc >= 1:
                    emit_stats(oc - 1)
                    cov = stat_mm2[oc - 1]
                else:
                    cov = gfence(pf[4][32:33, 256:384])
                OP("v", lambda En, i=s_xhat[:, oc, e0 + 2:e0 + 2 + TT]:
                   En.tensor_tensor(s_x2[:], i, s_cb[:, 2:2 + TT], ALU.mult),
                   waits=[cov])
                OP("v", lambda En, g_=vec("g1")[:, oc:oc + 1], b_=vec("b1")[:, oc:oc + 1]:
                   En.tensor_scalar(s_x2[:], s_x2[:], g_, b_, ALU.mult, ALU.add))
                tok_rbf[oc] = OP(
                    "v", lambda En, o_=s_rbf[:, oc, :], p_=p5, b_=vec("bfb")[:, oc:oc + 1]:
                    En.scalar_tensor_tensor(o_, p_[:], b_, s_x2[:], ALU.add, ALU.add),
                    inc="sV")
                t_pf_free[oc % 2] = tok_rbf[oc]
                tok_rsq[oc] = OP(
                    "v", lambda En, o_=s_rsq[oc % 2][:], a_=s_rbf[:, oc, :]:
                    En.tensor_mul(o_, a_, a_),
                    waits=[stat_mm2[oc - 2]] if oc >= 2 else [], inc="sV")
                if oc == NCH - 1:
                    emit_stats(oc)
            rowm = s_smxm[0:1, :]
            rowe = s_smxr[0:1, :]
            rowt = s_SM[0:1, 0, :]
            fst = gfence(pf[5][32:33, 256:384])
            OP("v", lambda En: En.tensor_scalar_mul(rowm, pf[4][0:1, :], 1.0 / D),
               waits=[fst])
            OP("v", lambda En: En.tensor_scalar_mul(rowe, pf[5][0:1, :], 1.0 / D))
            OP("v", lambda En: En.tensor_mul(rowt, rowm, rowm))
            OP("v", lambda En: En.tensor_sub(rowe, rowe, rowt))
            tve = OP("v", lambda En: En.tensor_scalar_add(rowe, rowe, 1e-6), inc="sV")
            t_pf_free[4] = tve
            t_pf_free[5] = tve
            OP("a", lambda En: En.activation(rowt, rowe, AF.Ln), waits=[tve])
            trs = OP("a", lambda En: En.activation(rowe, rowt, AF.Exp, scale=-0.5),
                     inc="sA")
            tnm = OP("v", lambda En: En.tensor_scalar_mul(rowm, rowm, -1.0), inc="sV")
            bc1 = OP("t", lambda En: En.matmul(pf[2][:], s_on1f[:], rowm, start=True, stop=True),
                     waits=[tnm, t_pf_free[2]], inc="sT")
            bc2 = OP("t", lambda En: En.matmul(pf[3][:], s_on1f[:], rowe, start=True, stop=True),
                     waits=[trs, t_pf_free[3]], inc="sT")
            fbc = gfence(pf[5][32:33, 256:384])
            th2 = None
            for oc in range(NCH):
                OP("v", lambda En, a_=s_rbf[:, oc, :]:
                   En.tensor_tensor(s_x2[:], a_, pf[2][:], ALU.add),
                   waits=([fbc] if oc == 0 else []))
                OP("v", lambda En: En.tensor_tensor(s_x2[:], s_x2[:], pf[3][:], ALU.mult))
                th2 = OP("v", lambda En, o_=s_h2[:, oc, e0:e0 + TT],
                         g_=vec("g2")[:, oc:oc + 1], b_=vec("b2")[:, oc:oc + 1]:
                         En.tensor_scalar(o_, s_x2[:], g_, b_, ALU.mult, ALU.add),
                         inc="sV")
            t_pf_free[2] = th2
            t_pf_free[3] = th2

            tok_u_freed = tok_u_read
            tok_u_freed = sch.tok("sT")
            tok_xpb_free = sch.tok("sT")
            tok_ao_free = sch.tok("sT")
            if t + 1 < NT:
                tok_wk[t + 1] = DMA("g", qb[:], wslice(d_wk, 0), f"dB{(0 - t) % 3}",
                                    waits=[sch.tok("sT")])
                tok_wv[t + 1] = DMA("g", kb[:], wslice(d_wv, 0), f"dB{(1 - t) % 3}",
                                    waits=[sch.tok("sT")])

        if _DBG:
            tD = DMA("s", d_dbg_h2, s_h2[:].rearrange("p c t -> p (c t)"), "dD",
                     waits=[sch.tok("sV")])
            OP("s", lambda En: En.nop(), waits=[tD])
        sch.barrier()

        tok_wo = DMA("g", s_wo[:], wslice(d_wo, 0), "dWo")
        tok_w1a = DMA("g", s_w1a[:], d_w1[:, 0:2048].rearrange("(i p) o -> p i o", p=128),
                      "dW1a")
        pf_free = [None] * 6
        a3_pend = None

        def a3_flush(po, cover):
            pf_free[NT - 1] = OP(
                "a", lambda En, o_=s_s[:, po, 512 * (NT - 1):512 * NT],
                b_=vec("bo")[:, po:po + 1], p_=pf[NT - 1]:
                En.activation(o_, p_[:], AF.Gelu, bias=b_, scale=1.0),
                waits=[cover], inc="sA")

        for oc in range(NCH):
            lastm = [None] * NT
            w0 = [tok_wo]
            for ic in range(NCH):
                for t4 in range(NT):
                    lastm[t4] = OP(
                        "t", lambda En, l_=s_wo[:, ic, 128 * oc:128 * (oc + 1)],
                        r_=s_h2[:, ic, 512 * t4:512 * (t4 + 1)], st=(ic == 0),
                        sp=(ic == NCH - 1), p_=pf[t4]:
                        En.matmul(p_[:], l_, r_, start=st, stop=sp),
                        waits=w0 + ([pf_free[t4]] if ic == 0 else []),
                        inc=("sT" if ic == NCH - 1 or (ic == 0 and t4 == 0)
                             else None))
                    w0 = []
                    if a3_pend is not None:
                        a3_flush(a3_pend, lastm[0])
                        a3_pend = None
            for t4 in range(NT - 1):
                pf_free[t4] = OP(
                    "a", lambda En, o_=s_s[:, oc, 512 * t4:512 * (t4 + 1)],
                    b_=vec("bo")[:, oc:oc + 1], p_=pf[t4]:
                    En.activation(o_, p_[:], AF.Gelu, bias=b_, scale=1.0),
                    waits=[lastm[t4 + 1]], inc="sA")
            a3_pend = oc
        a3_flush(a3_pend, gfence(pf[4][0:1, 0:128]))
        a3_pend = None

        if _DBG:
            tD = DMA("s", d_dbg_s, s_s[:].rearrange("p c t -> p (c t)"), "dD",
                     waits=[sch.tok("sA")])
            OP("s", lambda En: En.nop(), waits=[tD])
        sch.barrier()

        tok_w1b = DMA("g", s_w1b[:], d_w1[:, 2048:4096].rearrange("(i p) o -> p i o", p=128),
                      "dW1b")
        pf_free = [None] * 6
        tok_hdma = [None] * 32
        tok_w2a = None
        f1_pend = None

        def f1_flush(po, cover):
            pf_free[NT - 1] = OP(
                "a", lambda En, o_=s_hst[po % 2][:, 512 * (NT - 1):512 * NT],
                b_=s_b1f[:, po:po + 1], p_=pf[NT - 1]:
                En.activation(o_, p_[:], AF.Relu, bias=b_, scale=1.0),
                waits=[cover], inc="sA")
            tok_hdma[po] = DMA("a", d_h[po], s_hst[po % 2][:], f"dH{po % 4}",
                               waits=[pf_free[t4] for t4 in range(NT)])

        for oc in range(32):
            wb_ = s_w1a if oc < 16 else s_w1b
            wtok = tok_w1a if oc < 16 else tok_w1b
            col = 128 * (oc % 16)
            lastm = [None] * NT
            w0 = [wtok]
            if oc >= 2:
                w0.append(tok_hdma[oc - 2])
            for ic in range(NCH):
                for t4 in range(NT):
                    lastm[t4] = OP(
                        "t", lambda En, l_=wb_[:, ic, col:col + 128],
                        r_=s_xhat[:, ic, 2 + 512 * t4:2 + 512 * (t4 + 1)],
                        st=(ic == 0), sp=(ic == NCH - 1), p_=pf[t4]:
                        En.matmul(p_[:], l_, r_, start=st, stop=sp),
                        waits=w0 + ([pf_free[t4]] if ic == 0 else []),
                        inc=("sT" if ic == NCH - 1 or (ic == 0 and t4 == 0)
                             else None))
                    w0 = []
                    if f1_pend is not None:
                        f1_flush(f1_pend, lastm[0])
                        f1_pend = None
            for t4 in range(NT - 1):
                pf_free[t4] = OP(
                    "a", lambda En, o_=s_hst[oc % 2][:, 512 * t4:512 * (t4 + 1)],
                    b_=s_b1f[:, oc:oc + 1], p_=pf[t4]:
                    En.activation(o_, p_[:], AF.Relu, bias=b_, scale=1.0),
                    waits=[lastm[t4 + 1]], inc="sA")
            f1_pend = oc
            if oc == 15:
                tok_w2a = DMA("g", s_w2a[:],
                              d_w2[0:2048, :].rearrange("(i p) o -> p i o", p=128),
                              "dW2a", waits=[lastm[NT - 1]])
        f1_flush(f1_pend, gfence(pf[4][0:1, 0:128]))
        f1_pend = None

        sch.barrier()

        tok_w2b = DMA("g", s_w2b[:],
                      d_w2[2048:4096, :].rearrange("(i p) o -> p i o", p=128), "dW2b")
        all_h = [sch.tok(f"dH{i}") for i in range(4)]
        pf_free = [None] * 6
        pb_free = [None, None]
        tok_hinA_free = None
        tok_hinB_free = None
        tok_xre_free = None
        tok_out_dma = None
        out_dma_bi = [None, None]
        for t in range(NT):
            tsl = (512 * t, 512 * (t + 1))
            thA = DMA("s", s_hinA[:],
                      d_h[0:16, :, tsl[0]:tsl[1]].rearrange("i p t -> p i t"),
                      "dHA", waits=all_h + [tok_hinA_free])
            thB = DMA("s", s_hinB[:],
                      d_h[16:32, :, tsl[0]:tsl[1]].rearrange("i p t -> p i t"),
                      "dHB", waits=[tok_hinB_free])
            txr = DMA("s", s_xre[:],
                      d_xs[2 + tsl[0]:2 + tsl[1], :].rearrange("(b p) d -> p b d", p=128),
                      "dXR", waits=[tok_xre_free])
            tok_sg = [None] * NCH
            f2_pend = None

            def f2_flush(pv, cover):
                po, pp = pv
                tgs = OP("a", lambda En, o_=s_gsc[po % 2][:], p_=pp,
                         b_=vec("b2f")[:, po:po + 1]:
                         En.activation(o_, p_[:], AF.Silu, bias=b_, scale=1.0),
                         waits=[cover], inc="sA")
                tok_sg[po] = OP(
                    "v", lambda En, o_=s_sg[:, po, :], a_=s_gsc[po % 2][:],
                    b_=s_s[:, po, tsl[0]:tsl[1]]:
                    En.tensor_mul(o_, a_, b_), waits=[tgs], inc="sV")
                pf_free[po % 2] = tok_sg[po]

            for oc in range(NCH):
                p5 = pf[oc % 2]
                w0 = [thA, thB, tok_w2a, tok_w2b, pf_free[oc % 2]]
                for ic in range(32):
                    wb_ = s_w2a if ic < 16 else s_w2b
                    hb_ = s_hinA if ic < 16 else s_hinB
                    mmt = OP("t", lambda En, l_=wb_[:, ic % 16, 128 * oc:128 * (oc + 1)],
                             r_=hb_[:, ic % 16, :], st=(ic == 0), sp=(ic == 31), p_=p5:
                             En.matmul(p_[:], l_, r_, start=st, stop=sp),
                             waits=w0, inc=("sT" if ic in (0, 31) else None))
                    w0 = []
                    if f2_pend is not None:
                        f2_flush(f2_pend, mmt)
                        f2_pend = None
                f2_pend = (oc, p5)
            f2_flush(f2_pend, gfence(pf[3][0:1, 0:128]))
            f2_pend = None
            for half in range(2):
                touts = []
                for bi in range(2):
                    b4 = 2 * half + bi
                    wv0 = [tok_sg[NCH - 1], pb_free[bi]]
                    if half == 0 and out_dma_bi[bi] is not None:
                        wv0.append(out_dma_bi[bi])
                    for oc in range(NCH):
                        OP("t", lambda En, o_=pb[bi][:, 128 * oc:128 * (oc + 1)],
                           i=s_sg[:, oc, 128 * b4:128 * (b4 + 1)]:
                           En.transpose(o_, i, s_idb[:]), waits=wv0)
                        wv0 = []
                    ttr = OP("t", lambda En:
                             En.matmul(pf[2][0:1, 0:128], s_onc[0:1, 0:1], s_on1[0:1, :],
                                       start=True, stop=True), inc="sT")
                    wadd = [ttr, txr]
                    if out_dma_bi[bi] is not None:
                        wadd.append(out_dma_bi[bi])
                    if t == NT - 1 and half == 1:
                        OP("v", lambda En: En.nop() if _SIMGF else
                           En.nop(cycle_cnt=3000), waits=[ttr])
                    touts.append(OP(
                        "v", lambda En, o_=s_outst[:, bi, :], p_=pb[bi],
                        x_=s_xre[:, b4, :]:
                        En.tensor_tensor(o_, p_[:], x_, ALU.add),
                        waits=wadd, inc="sV"))
                    pb_free[bi] = touts[-1]
                    tcm = OP("v", lambda En: En.nop() if _SIMGF else
                             En.nop(cycle_cnt=700), waits=[touts[-1]],
                             inc="sV")
                    r0 = tsl[0] + 256 * half + 128 * bi
                    out_dma_bi[bi] = DMA(
                        "a", d_y[r0:r0 + 128, :],
                        s_outst[:, bi, :], ("dO", "dO1")[bi], waits=[tcm])
                tok_out_dma = out_dma_bi[1]
            tok_hinA_free = sch.tok("sT")
            tok_hinB_free = sch.tok("sT")
            tok_xre_free = sch.tok("sV")

        if _DBG3:
            DMA("s", d_dbg_sg, s_sg[:].rearrange("p c t -> p (c t)"), "dD",
                waits=[sch.tok("sV")])
            DMA("s", d_dbg_gsc[:, 0:TT], s_gsc[0][:], "dD")
            DMA("s", d_dbg_gsc[:, TT:2 * TT], s_gsc[1][:], "dD")
            tD3 = DMA("s", d_dbg_s2, s_s[:].rearrange("p c t -> p (c t)"), "dD")
            OP("s", lambda En: En.nop(), waits=[tD3])
        OP("a", lambda En: En.nop(), waits=[sch.tok("dO"), sch.tok("dO1")])
        OP("s", lambda En: En.nop(),
           waits=[sch.tok("dO"), sch.tok("dO1"), sch.tok("dD")])

        with nc.Block() as blk:
            def runner(key):
                def body(Eng):
                    for needed, fn, inc, inck in sch.q[key]:
                        for sname, val in needed:
                            Eng.wait_ge(sems[sname], val)
                        ins = fn(Eng)
                        if inc is not None:
                            ins.then_inc(sems[inc], inck)
                return body
            blk.tensor(runner("t"))
            blk.vector(runner("v"))
            blk.scalar(runner("a"))
            blk.gpsimd(runner("g"))
            blk.sync(runner("s"))

    return nc


_NC_CACHE = None


def _get_nc():
    global _NC_CACHE
    if _NC_CACHE is None:
        _NC_CACHE = build_program()
    return _NC_CACHE


def _host_consts():
    f = np.float32
    idm = np.eye(128, dtype=f)
    sel = np.zeros((16, NCH, 128), f)
    ind = np.zeros((128, NCH, 16), f)
    for c in range(NCH):
        for dd in range(128):
            h = 2 * c + (1 if dd >= 64 else 0)
            sel[h, c, dd] = 1.0
            ind[dd, c, h] = 0.125
    ones1 = np.ones((1, 128), f)
    onesc = np.ones((128, 1), f)
    return idm, sel.reshape(16, NCH * 128), ind.reshape(128, NCH * 16), ones1, onesc


def _masks_for_core(t0):
    f = np.float32
    mpre = np.ones((16, NT, 6, 16), f)
    mpost = np.ones((16, NT, 6, 16), f)
    for t in range(NT):
        for col in range(16):
            if col < 8:
                g = t0 + 512 * t + col
            else:
                g = t0 + 512 * t + 504 + (col - 8)
            for j in range(3):
                if not (g >= max(1, j)):
                    mpre[:, t, j, col] = 0.0
                if not (g >= j):
                    mpost[:, t, j, col] = 0.0
                if not (g <= min(SEQ - 2, SEQ - 1 - j)):
                    mpre[:, t, 3 + j, col] = 0.0
                if not (g <= SEQ - 1 - j):
                    mpost[:, t, 3 + j, col] = 0.0
    return mpre.reshape(16, NT * 6 * 16), mpost.reshape(16, NT * 6 * 16)


def kernel(L0_attn_input_feature, L0_attn_Wq, L0_attn_bq, L0_attn_Wk, L0_attn_bk,
           L0_attn_Wv, L0_attn_bv, L0_attn_Wfb, L0_attn_bfb, L0_attn_Wo,
           L0_attn_bo, L0_attn_g1, L0_attn_b1, L0_attn_g2, L0_attn_b2,
           L0_ffn_W1, L0_ffn_b1, L0_ffn_W2, L0_ffn_b2):
    f = np.float32
    x = np.ascontiguousarray(np.asarray(L0_attn_input_feature, f))
    W = {k: np.ascontiguousarray(np.asarray(v, f)) for k, v in dict(
        Wq=L0_attn_Wq, Wk=L0_attn_Wk, Wv=L0_attn_Wv, Wfb=L0_attn_Wfb,
        Wo=L0_attn_Wo, W1=L0_ffn_W1, W2=L0_ffn_W2).items()}
    vl = [L0_attn_g1, L0_attn_b1, L0_attn_g2, L0_attn_b2, L0_attn_bq,
          L0_attn_bk, L0_attn_bv, L0_attn_bfb, L0_attn_bo, L0_ffn_b2]
    vecs = np.stack([np.asarray(v, f).reshape(NCH, 128).T for v in vl], axis=1)
    vecs = np.ascontiguousarray(vecs.reshape(128, 80))
    b1f = np.ascontiguousarray(np.asarray(L0_ffn_b1, f).reshape(32, 128).T)
    vb4 = np.stack([np.repeat(np.asarray(v, f).reshape(NCH, 128).T[:, :, None],
                              4, axis=2)
                    for v in (L0_attn_bq, L0_attn_bk, L0_attn_bv)], axis=1)
    vb4 = np.ascontiguousarray(vb4.reshape(128, 3, 32))
    idm, sel, ind, ones1, onesc = _host_consts()

    zeros2 = np.zeros((2, D), f)
    in_maps = []
    for core in range(8):
        b, half = divmod(core, 2)
        if half == 0:
            xs = np.concatenate([zeros2, x[b, 0:TC + 2]], axis=0)
        else:
            xs = np.concatenate([x[b, TC - 2:2 * TC], zeros2], axis=0)
        mpre, mpost = _masks_for_core(half * TC)
        in_maps.append(dict(
            xs=np.ascontiguousarray(xs), vecs=vecs, b1f=b1f, vb4=vb4, idm=idm,
            sel=sel, ind=ind, ones1=ones1, onesc=onesc, mpre=mpre, mpost=mpost,
            **W))

    nc = _get_nc()
    res = run_bass_kernel_spmd(nc, in_maps, core_ids=list(range(8)))
    if _DBG or _DBG3:
        global _DBG_RES
        _DBG_RES = res.results
    ys = [res.results[i]["y"] for i in range(8)]
    out = np.stack([np.concatenate([ys[2 * b], ys[2 * b + 1]], axis=0)
                    for b in range(4)])
    return np.ascontiguousarray(out.astype(np.float32))


# version 12
# speedup vs baseline: 1.0438x; 1.0253x over previous
import numpy as np
import ml_dtypes
from contextlib import ExitStack

import concourse.bass as bass
import concourse.mybir as mybir
from concourse.bass_utils import run_bass_kernel_spmd

F32 = mybir.dt.float32
BF16 = mybir.dt.bfloat16
AF = mybir.ActivationFunctionType
ALU = mybir.AluOpType
AX = mybir.AxisListType.X

D = 1024
NCH = 8
TC = 2048
NL = 2052
TT = 512
NT = TC // TT
E = TT + 4
NB = 17
SEQ = 4096

_DBG = False
_SIMGF = False
_DBG3 = False

_SEMS = (["dD", "dX0", "dX1", "dX2", "dHA", "dHB", "dXR", "dB0", "dB1", "dB2",
          "dC", "dC2", "dWo", "dW1a", "dW1b", "dW2a", "dW2b", "dH0", "dH1",
          "dH2", "dH3",
          "dO", "dO1", "sT", "sV", "sA", "sG", "sS", "bt", "bv", "ba", "bg",
          "bs"])


class Sch:
    ENG = "tvags"

    def __init__(self):
        self.q = {k: [] for k in self.ENG}
        self.cnt = {}
        self.lastw = {k: {} for k in self.ENG}

    def tok(self, sem):
        return (sem, self.cnt.get(sem, 0))

    def op(self, eng, fn, waits=(), inc=None, inck=1):
        lw = self.lastw[eng]
        needed = []
        for w in waits:
            if w is None:
                continue
            sname, val = w
            if val > 0 and lw.get(sname, 0) < val:
                lw[sname] = val
                needed.append((sname, val))
        out_tok = None
        if inc is not None:
            self.cnt[inc] = self.cnt.get(inc, 0) + inck
            out_tok = (inc, self.cnt[inc])
        self.q[eng].append((needed, fn, inc, inck))
        return out_tok

    def barrier(self):
        toks = {}
        for e in self.ENG:
            self.op(e, lambda E: E.drain())
            toks[e] = self.op(e, lambda E: E.nop(), inc="b" + e)
        for e in self.ENG:
            self.op(e, lambda E: E.nop(),
                    waits=[toks[o] for o in self.ENG if o != e])


def build_program():
    nc = bass.Bass("TRN2", target_bir_lowering=False, detect_race_conditions=False)

    d_xs = nc.dram_tensor("xs", [NL, D], F32, kind="ExternalInput").ap()
    d_wq = nc.dram_tensor("Wq", [D, D], F32, kind="ExternalInput").ap()
    d_wk = nc.dram_tensor("Wk", [D, D], F32, kind="ExternalInput").ap()
    d_wv = nc.dram_tensor("Wv", [D, D], F32, kind="ExternalInput").ap()
    d_wfb = nc.dram_tensor("Wfb", [2 * D, D], F32, kind="ExternalInput").ap()
    d_wo = nc.dram_tensor("Wo", [D, D], F32, kind="ExternalInput").ap()
    d_w1 = nc.dram_tensor("W1", [D, 4 * D], F32, kind="ExternalInput").ap()
    d_w2 = nc.dram_tensor("W2", [4 * D, D], F32, kind="ExternalInput").ap()
    d_vecs = nc.dram_tensor("vecs", [128, 80], F32, kind="ExternalInput").ap()
    d_b1f = nc.dram_tensor("b1f", [128, 32], F32, kind="ExternalInput").ap()
    d_vb4 = nc.dram_tensor("vb4", [128, 3, 32], F32, kind="ExternalInput").ap()
    d_id = nc.dram_tensor("idm", [128, 128], F32, kind="ExternalInput").ap()
    d_sel = nc.dram_tensor("sel", [16, NCH * 128], F32, kind="ExternalInput").ap()
    d_ind = nc.dram_tensor("ind", [128, NCH * 16], F32, kind="ExternalInput").ap()
    d_ones1 = nc.dram_tensor("ones1", [1, 128], F32, kind="ExternalInput").ap()
    d_onesc = nc.dram_tensor("onesc", [128, 1], F32, kind="ExternalInput").ap()
    d_mpre = nc.dram_tensor("mpre", [16, NT * 6 * 16], F32, kind="ExternalInput").ap()
    d_mpost = nc.dram_tensor("mpost", [16, NT * 6 * 16], F32, kind="ExternalInput").ap()
    d_y = nc.dram_tensor("y", [TC, D], F32, kind="ExternalOutput").ap()
    d_h = nc.dram_tensor("hscr", [32, 128, TC], BF16, kind="Internal").ap()
    if _DBG3:
        d_dbg_sg = nc.dram_tensor("dbg_sg", [128, NCH * TT], BF16, kind="ExternalOutput").ap()
        d_dbg_gsc = nc.dram_tensor("dbg_gsc", [128, 2 * TT], F32, kind="ExternalOutput").ap()
        d_dbg_s2 = nc.dram_tensor("dbg_s2", [128, NCH * TC], F32, kind="ExternalOutput").ap()
    if _DBG:
        d_dbg_st = nc.dram_tensor("dbg_st", [128, NB * 16], F32, kind="ExternalOutput").ap()
        d_dbg_xhat = nc.dram_tensor("dbg_xhat", [128, NCH * NL], BF16, kind="ExternalOutput").ap()
        d_dbg_crow = nc.dram_tensor("dbg_crow", [1, NL], F32, kind="ExternalOutput").ap()
        d_dbg_h2 = nc.dram_tensor("dbg_h2", [128, NCH * TC], BF16, kind="ExternalOutput").ap()
        d_dbg_s = nc.dram_tensor("dbg_s", [128, NCH * TC], F32, kind="ExternalOutput").ap()

    sch = Sch()

    with ExitStack() as ctx:
        en = ctx.enter_context

        regions = []

        BASE = 16512

        def at(name, shape, dtype, off, stages):
            t = nc.alloc_sbuf_tensor_at(name, shape, dtype, offset=BASE + off)
            nbytes = int(np.prod(shape[1:])) * mybir.dt.size(dtype)
            regions.append((name, off, off + nbytes, frozenset(stages)))
            return t

        ALL = "0AWF2"

        s_id = at("c_id", [128, 128], F32, 0, ALL)
        s_idb = at("c_idb", [128, 128], BF16, 512, ALL)
        s_sel = at("c_sel", [16, NCH, 128], BF16, 768, ALL)
        s_ind = at("c_ind", [128, NCH, 16], BF16, 2816, ALL)
        s_on1 = at("c_on1", [1, 128], BF16, 3072, ALL)
        s_onc = at("c_onc", [128, 1], BF16, 3328, ALL)
        s_vecs = at("c_vecs", [128, 10, 8], F32, 3360, ALL)
        s_b1f = at("c_b1f", [128, 32], F32, 3680, ALL)
        s_mpre = at("c_mpre", [16, NT, 6, 16], F32, 3808, ALL)
        s_mpost = at("c_mpost", [16, NT, 6, 16], F32, 5344, ALL)
        VI = {n: i for i, n in enumerate(
            ["g1", "b1", "g2", "b2", "bq", "bk", "bv", "bfb", "bo", "b2f"])}

        def vec(n):
            return s_vecs[:, VI[n], :]

        s_xhat = at("s_xhat", [128, NCH, NL], BF16, 6912, "0AWF")
        s_w2b = at("s_w2b", [128, 16, D], BF16, 6912, "2")
        s_crow = at("s_crow", [1, NL], F32, 39776, "0A")
        s_sg = at("s_sg", [128, NCH, TT], BF16, 39776, "2")

        s_xtok = [at(f"s_xtok{i}", [128, D], F32, 48000 + 4096 * i, "0")
                  for i in range(3)]
        s_xht = [at(f"s_xht{i}", [128, D], BF16, 60288 + 2048 * i, "0")
                 for i in range(2)]
        s_sq = at("s_sq", [128, D], F32, 64384, "0")
        s_scol = at("s_scol", [128, 16], F32, 68480, "0")
        s_ccols = at("s_ccols", [128, 20], F32, 68544, "0")

        s_SM = at("s_SM", [16, 6, TT], F32, 48000, "A")
        s_smxm = at("s_smxm", [16, TT], F32, 60288, "A")
        s_smxr = at("s_smxr", [16, TT], F32, 62336, "A")
        s_wo = at("s_wo", [128, NCH, D], BF16, 48000, "W")
        s_on1f = at("c_on1f", [1, 128], F32, 64384, "A")
        s_vb4 = at("c_vb4", [128, 3, 32], F32, 67968, "A")
        s_SMb = at("s_SMb", [16, 3, TT], BF16, 64896, "A")
        s_hinA = at("s_hinA", [128, 16, TT], BF16, 48000, "2")

        s_h2 = at("s_h2", [128, NCH, TC], BF16, 68672, "AW")
        s_w1b = at("s_w1b", [128, NCH, 2048], BF16, 68672, "F")
        s_hinB = at("s_hinB", [128, 16, TT], BF16, 68672, "2")
        s_xre = at("s_xre", [128, 4, D], F32, 85056, "2")

        A12 = 101504
        s_B = [at(f"s_B{i}", [128, NCH, D], BF16, A12 + 16384 * i, "A")
               for i in range(3)]
        s_dbgst = nc.alloc_sbuf_tensor_at("s_dbgst", [128, NB, 16], F32,
                                          offset=BASE + 68672)
        s_stat = at("s_stat", [128, 15, 20], F32, 150656, "0")
        s_xpb = at("s_xpb", [128, NCH, E], BF16, 150656, "A")
        s_rbf = at("s_rbf", [128, NCH, TT], BF16, 150656, "A")
        s_q = at("s_q", [128, NCH, E], BF16, 158912, "A")
        s_k = at("s_k", [128, NCH, E], BF16, 167168, "A")
        s_v = at("s_v", [128, NCH, E], BF16, 175424, "A")
        s_u = at("s_u", [128, NCH, E - 2], BF16, 183680, "A")
        s_u2 = nc.alloc_sbuf_tensor_at("s_u2", [128, NCH, E - 2], BF16,
                                       offset=BASE + 150656)
        s_SMb2 = nc.alloc_sbuf_tensor_at("s_SMb2", [16, 3, TT], BF16,
                                         offset=BASE + 202144)
        s_ao = at("s_ao", [128, 16, TT], BF16, 183680, "A")
        s_cb = at("s_cb", [128, E], F32, 200064, "A")
        s_t1 = at("s_t1", [128, E], F32, 202144, "A")
        s_x2 = at("s_x2", [128, TT], F32, 204224, "A")
        s_wb = [at(f"s_wb{i}", [128, TT], BF16, 206272 + 1024 * i, "A")
                for i in range(2)]
        s_wvp = [at(f"s_wvp{i}", [128, TT], BF16, 208320 + 1024 * i, "A")
                 for i in range(2)]
        s_rsq = [at(f"s_rsq{i}", [128, TT], BF16, 210368 + 1024 * i, "A")
                 for i in range(2)]

        s_s = at("s_sgate", [128, NCH, TC], F32, A12, "WF2")
        s_w1a = at("s_w1a", [128, NCH, 2048], BF16, 167040, "WF")
        s_w2a = at("s_w2a", [128, 16, D], BF16, 167040, "2")
        s_hst = [at(f"s_hst{i}", [128, TC], BF16, 199808 + 4096 * i, "F")
                 for i in range(2)]
        s_hst.append(at("s_hst2", [128, TC], BF16, 39776, "F"))
        s_gsc = [at(f"s_gsc{i}", [128, TT], F32, 199808 + 2048 * i, "2")
                 for i in range(2)]
        s_outst = at("s_outst", [128, 2, D], F32, 203904, "2")

        for i in range(len(regions)):
            for j in range(i + 1, len(regions)):
                n1, a1, b1, st1 = regions[i]
                n2, a2, b2, st2 = regions[j]
                if a1 < b2 and a2 < b1 and (st1 & st2):
                    allowed = {("s_xpb", "s_rbf"), ("s_u", "s_ao")}
                    if (n1, n2) not in allowed and (n2, n1) not in allowed:
                        raise AssertionError(f"overlap {n1} {n2}")
                if b1 > 212700 or b2 > 212700:
                    raise AssertionError("sbuf overflow")

        pf = [en(nc.psum_tensor(f"pf{i}", [128, 512], F32)) for i in range(6)]
        pb = [en(nc.psum_tensor(f"pbb{i}", [128, 1024], BF16)) for i in range(2)]

        sems = {n: en(nc.semaphore(n)) for n in _SEMS}

        def OP(eng, fn, waits=(), inc=None, inck=1):
            return sch.op(eng, fn, waits, inc, inck)

        def gfence(target=None):
            if _SIMGF:
                return OP("t", lambda En, t_=target:
                          En.matmul(t_, s_onc[0:1, 0:1], s_on1[0:1, :],
                                    start=True, stop=True), inc="sT")
            anchor = sch.tok("sT")
            return OP("t", lambda En: En.nop(cycle_cnt=700), waits=[anchor],
                      inc="sT")

        def DMA(eng, out, in_, sem, waits=(), **kw):
            return OP(eng,
                      (lambda En, o_=out, i_=in_, k_=dict(kw): En.dma_start(out=o_, in_=i_, **k_)),
                      waits=waits, inc=sem, inck=16)

        DMA("g", s_id[:], d_id, "dC")
        DMA("g", s_idb[:], d_id, "dC")
        DMA("g", s_sel[:], d_sel.rearrange("h (c d) -> h c d", c=NCH), "dC")
        DMA("g", s_ind[:], d_ind.rearrange("p (c h) -> p c h", c=NCH), "dC")
        DMA("g", s_on1[:], d_ones1, "dC")
        DMA("g", s_onc[:], d_onesc, "dC")
        DMA("g", s_vecs[:], d_vecs.rearrange("p (n c) -> p n c", n=10), "dC")
        DMA("g", s_b1f[:], d_b1f, "dC")
        DMA("g", s_mpre[:], d_mpre.rearrange("h (t j c) -> h t j c", t=NT, j=6), "dC")
        tok_consts = DMA("g", s_mpost[:], d_mpost.rearrange("h (t j c) -> h t j c", t=NT, j=6), "dC")

        def wslice(d_w, lo_chunk, n_chunk=8, cols=None):
            ap = d_w[128 * lo_chunk:128 * (lo_chunk + n_chunk), :]
            if cols is not None:
                ap = ap[:, cols[0]:cols[1]]
            return ap.rearrange("(i p) o -> p i o", p=128)

        tok_wq = [None] * (NT + 1)
        tok_wk = [None] * (NT + 1)
        tok_wv = [None] * (NT + 1)
        tok_wq[0] = DMA("g", s_B[0][:], wslice(d_wq, 0), "dB0")
        tok_wk[0] = DMA("g", s_B[1][:], wslice(d_wk, 0), "dB1")
        tok_wv[0] = DMA("g", s_B[2][:], wslice(d_wv, 0), "dB2")

        OP("v", lambda En: En.memset(s_ccols[:], 0.0))

        tok_xtok_free = [None] * NB
        tok_xht_free = [None] * NB
        tok_pb_evac = [None, None]
        tok_pc_evac = None
        tok_load = [None] * NB
        ST = s_stat

        for g0 in range(0, NB, 3):
            grp = list(range(g0, min(g0 + 3, NB)))
            g1 = grp[-1] + 1
            for b in grp:
                nb_ = 4 if b == NB - 1 else 128
                w = [tok_xtok_free[b - 3]] if b >= 3 else []
                tok_load[b] = DMA("s", s_xtok[b % 3][0:nb_, :],
                                  d_xs[128 * b:128 * b + nb_, :],
                                  f"dX{b % 3}", waits=w)
            rtoks = []
            for b in grp:
                xt = s_xtok[b % 3]
                rtoks.append(OP("v", lambda En, a=ST[:, 0, b:b + 1], i=xt[:]:
                                En.tensor_reduce(a, i, AX, ALU.add),
                                waits=[tok_load[b]], inc="sV"))
                rtoks.append(OP("v", lambda En, i=xt[:], a=ST[:, 1, b:b + 1]:
                                En.scalar_tensor_tensor(
                                    s_sq[:], i, 1.0, i, ALU.mult, ALU.mult,
                                    accum_out=a), inc="sV"))
            w1 = OP("v", lambda En, o_=ST[:, 2, g0:g1], a_=ST[:, 0, g0:g1]:
                    En.tensor_scalar_mul(o_, a_, 1.0 / D),
                    waits=rtoks, inc="sV")
            OP("v", lambda En, o_=ST[:, 3, g0:g1], a_=ST[:, 1, g0:g1]:
               En.tensor_scalar_mul(o_, a_, 1.0 / D))
            w2 = OP("v", lambda En, o_=ST[:, 4, g0:g1], a_=ST[:, 2, g0:g1]:
                    En.tensor_mul(o_, a_, a_), waits=[w1], inc="sV")
            w3 = OP("v", lambda En, o_=ST[:, 5, g0:g1], a_=ST[:, 3, g0:g1],
                    b_=ST[:, 4, g0:g1]:
                    En.tensor_sub(o_, a_, b_), waits=[w2], inc="sV")
            tA = OP("v", lambda En, o_=ST[:, 6, g0:g1], a_=ST[:, 5, g0:g1]:
                    En.tensor_scalar_add(o_, a_, 1e-5), waits=[w3], inc="sV")
            wa1 = OP("a", lambda En, o_=ST[:, 7, g0:g1], a_=ST[:, 6, g0:g1]:
                     En.activation(o_, a_, AF.Ln), waits=[tA], inc="sA")
            OP("a", lambda En, o_=ST[:, 8, g0:g1], a_=ST[:, 7, g0:g1]:
               En.activation(o_, a_, AF.Exp, scale=-0.5), waits=[wa1])
            tB = OP("a", lambda En, o_=ST[:, 9, g0:g1], a_=ST[:, 7, g0:g1]:
                    En.activation(o_, a_, AF.Exp, scale=-1.0), inc="sA")
            w4 = OP("v", lambda En, o_=ST[:, 10, g0:g1], a_=ST[:, 5, g0:g1],
                    b_=ST[:, 9, g0:g1]:
                    En.tensor_mul(o_, a_, b_), waits=[tB], inc="sV")
            tC = OP("v", lambda En, o_=ST[:, 11, g0:g1], a_=ST[:, 10, g0:g1]:
                    En.tensor_scalar_add(o_, a_, 1e-6), waits=[w4], inc="sV")
            wa2 = OP("a", lambda En, o_=ST[:, 12, g0:g1], a_=ST[:, 11, g0:g1]:
                     En.activation(o_, a_, AF.Ln), waits=[tC], inc="sA")
            tD = OP("a", lambda En, o_=ST[:, 13, g0:g1], a_=ST[:, 12, g0:g1]:
                    En.activation(o_, a_, AF.Exp, scale=-0.5),
                    waits=[wa2], inc="sA")
            w5b = OP("v", lambda En, o_=ST[:, 14, g0:g1], a_=ST[:, 2, g0:g1],
                     b_=ST[:, 8, g0:g1]:
                     En.scalar_tensor_tensor(o_, a_, -1.0, b_,
                                             ALU.mult, ALU.mult),
                     waits=[tD], inc="sV")

            for b in grp:
                nb_ = 4 if b == NB - 1 else 128
                xs_ = b % 2
                L0 = 128 * b
                xt = s_xtok[b % 3]
                wx = [w5b]
                if b >= 2:
                    wx.append(tok_xht_free[b - 2])
                tE = OP("v", lambda En, o_=s_xht[xs_][:], i=xt[:],
                        g_=ST[:, 8, b:b + 1], m_=ST[:, 14, b:b + 1]:
                        En.tensor_scalar(o_, i, g_, m_, ALU.mult, ALU.add),
                        waits=wx, inc="sV")
                tok_xtok_free[b] = tE

                wt = [tE, tok_consts]
                if tok_pb_evac[xs_] is not None:
                    wt.append(tok_pb_evac[xs_])
                for c in range(NCH):
                    OP("t", lambda En, o_=pb[xs_][:, 128 * c:128 * (c + 1)],
                       i=s_xht[xs_][:, 128 * c:128 * (c + 1)]:
                       En.transpose(o_, i, s_idb[:]), waits=wt)
                    wt = []
                wc = [tok_pc_evac] if tok_pc_evac else []
                OP("t", lambda En, i=ST[:, 13, b:b + 1]:
                   En.transpose(pf[0][0:1, 0:128], i, s_id[:]), waits=wc)
                tok_tr = OP("t", lambda En:
                            En.matmul(pf[1][0:1, 0:128], s_onc[0:1, 0:1],
                                      s_on1[0:1, :],
                                      start=True, stop=True), inc="sT")
                tok_xht_free[b] = tok_tr
                tok_pb_evac[xs_] = OP(
                    "a", lambda En, o_=s_xhat[:, :, L0:L0 + nb_],
                    i=pb[xs_][:].rearrange("p (c t) -> p c t", c=NCH)[:, :, 0:nb_]:
                    En.copy(o_, i), waits=[tok_tr], inc="sA")
                tok_pc_evac = OP(
                    "a", lambda En, o_=s_crow[0:1, L0:L0 + nb_],
                    i=pf[0][0:1, 0:nb_]:
                    En.copy(o_, i), waits=[tok_tr], inc="sA")

        sch.barrier()
        if _DBG:
            DMA("s", d_dbg_st, s_dbgst[:].rearrange("p b c -> p (b c)"), "dD")
            DMA("s", d_dbg_xhat, s_xhat[:].rearrange("p c t -> p (c t)"), "dD")
            DMA("s", d_dbg_crow, s_crow[:], "dD")

        tok_on1f = DMA("g", s_on1f[:], d_ones1, "dC2")
        tok_vb4 = DMA("g", s_vb4[:], d_vb4, "dC2")
        t_pf_free = [None] * 6
        tok_fa = [None] * NT
        tok_fb = [None] * NT
        tok_u_freed = None
        tok_xpb_free = None
        tok_ao_free = None

        for t in range(NT):
            e0 = TT * t
            qb = s_B[(0 - t) % 3]
            kb = s_B[(1 - t) % 3]
            vb = s_B[(2 - t) % 3]

            t1m = OP("t", lambda En, r_=s_crow[0:1, e0:e0 + 512]:
                     En.matmul(pf[4][:], s_on1f[:], r_, start=True, stop=True),
                     waits=[tok_consts, tok_on1f, tok_pc_evac, t_pf_free[4]], inc="sT")
            t2m = OP("t", lambda En, r_=s_crow[0:1, e0 + 512:e0 + E]:
                     En.matmul(pf[5][:, 0:4], s_on1f[:], r_, start=True, stop=True),
                     waits=[t_pf_free[5]], inc="sT")
            fcb = gfence(pf[5][32:33, 256:384])
            ce1 = OP("a", lambda En: En.copy(s_cb[:, 0:512], pf[4][:]),
                     waits=[fcb], inc="sA")
            ce2 = OP("a", lambda En: En.copy(s_cb[:, 512:E], pf[5][:, 0:4]),
                     inc="sA")
            t_pf_free[4] = ce1
            t_pf_free[5] = ce2

            tok_xpb = [None] * NCH
            for c in range(NCH):
                OP("v", lambda En, i=s_xhat[:, c, e0:e0 + E]:
                   En.tensor_tensor(s_t1[:], i, s_cb[:], ALU.mult),
                   waits=[ce1, ce2, tok_xpb_free])
                tok_xpb[c] = OP(
                    "v", lambda En, o_=s_xpb[:, c, :], g_=vec("g1")[:, c:c + 1],
                    b_=vec("b1")[:, c:c + 1]:
                    En.tensor_scalar(o_, s_t1[:], g_, b_, ALU.mult, ALU.add),
                    inc="sV")

            tok_qe = [None] * NCH
            tok_ke = [None] * NCH
            tok_ve = [None] * NCH
            evs = {"q": tok_qe, "k": tok_ke, "v": tok_ve}
            obuf = {"q": s_q, "k": s_k, "v": s_v}
            bias = {"q": "bq", "k": "bk", "v": "bv"}
            wtk = {"q": tok_wq[t], "k": tok_wk[t], "v": tok_wv[t]}
            wbf = {"q": qb, "k": kb, "v": vb}
            lastmm = {}
            qkv_pend = None

            def qkv_flush(pv, cover):
                wn_, oc_, p5_ = pv
                ev1 = OP("a", lambda En, o_=obuf[wn_][:, oc_, 0:512], p_=p5_,
                         b_=vec(bias[wn_])[:, oc_:oc_ + 1]:
                         En.activation(o_, p_[:], AF.Identity, bias=b_, scale=1.0),
                         waits=[cover], inc="sA")
                evs[wn_][oc_] = ev1
                t_pf_free[(0, 1, 4)[oc_ % 3]] = ev1
                if oc_ == NCH - 1:
                    wni = {"q": 0, "k": 1, "v": 2}[wn_]
                    ev2 = OP("a", lambda En, o_=s_t1[:, 32 * wni:32 * wni + 32],
                             p_=pf[2][:, 0:32]: En.copy(o_, p_), inc="sA")
                    t_pf_free[2] = ev2
                    OP("v", lambda En,
                       o_=obuf[wn_][:, :, 512:E],
                       a_=s_t1[:, 32 * wni:32 * wni + 32],
                       b_=s_vb4[:, wni:wni + 1, :]:
                       En.tensor_tensor(o_, a_, b_, ALU.add),
                       waits=[ev2, tok_vb4])

            for wn in ("q", "k", "v"):
                B_ = wbf[wn]
                for oc in range(NCH):
                    bmain = (0, 1, 4)[oc % 3]
                    p5 = pf[bmain]
                    w0 = [wtk[wn], t_pf_free[bmain]]
                    for ic in range(NCH):
                        mmm = OP("t", lambda En, l_=B_[:, ic, 128 * oc:128 * (oc + 1)],
                                 r_=s_xpb[:, ic, 0:512], st=(ic == 0), sp=(ic == NCH - 1),
                                 p_=p5:
                                 En.matmul(p_[:], l_, r_, start=st, stop=sp),
                                 waits=w0 + [tok_xpb[ic]],
                                 inc=("sT" if ic == 0 else None))
                        w0 = []
                        if ic == 0 and qkv_pend is not None:
                            qkv_flush(qkv_pend, mmm)
                            qkv_pend = None
                        mmt = OP("t", lambda En, l_=B_[:, ic, 128 * oc:128 * (oc + 1)],
                                 r_=s_xpb[:, ic, 512:E], st=(ic == 0), sp=(ic == NCH - 1),
                                 p_=pf[2], co=4 * oc:
                                 En.matmul(p_[:, co:co + 4], l_, r_, start=st, stop=sp),
                                 waits=([t_pf_free[2]] if ic == 0 and oc == 0 else []),
                                 inc=("sT" if ic == NCH - 1 else None))
                    qkv_pend = (wn, oc, p5)
                lastmm[wn] = mmt

            tok_fa[t] = DMA("g", qb[:], wslice(d_wfb, 0), f"dB{(0 - t) % 3}", waits=[lastmm["q"]])
            tok_fb[t] = DMA("g", kb[:], wslice(d_wfb, 8), f"dB{(1 - t) % 3}", waits=[lastmm["k"]])
            if t + 1 < NT:
                tok_wq[t + 1] = DMA("g", vb[:], wslice(d_wq, 0), f"dB{(2 - t) % 3}",
                                    waits=[lastmm["v"]])

            tok_sm = []
            prev_scores = [None, None]
            sc_pend = None
            for j in range(3):
                ub = (s_u, s_u2, s_u)[j]
                ut = [None] * NCH
                for c in range(NCH):
                    w = [tok_qe[c], tok_ke[c]]
                    if j == 0 and c == 0 and tok_u_freed is not None:
                        w.append(tok_u_freed)
                    if c == 0 and j >= 2 and prev_scores[j % 2] is not None:
                        w.append(prev_scores[j % 2])
                    ut[c] = OP("v", lambda En, o_=ub[:, c, :],
                               a_=s_q[:, c, 1:E - 1], b_=s_k[:, c, 2 - j:E - j]:
                               En.tensor_tensor(o_, a_, b_, ALU.mult),
                               waits=w, inc="sV")
                wF = [t_pf_free[4]]
                for c in range(NCH):
                    mmF = OP("t", lambda En, l_=s_ind[:, c, :], r_=ub[:, c, 0:512],
                             st=(c == 0), sp=(c == NCH - 1):
                             En.matmul(pf[4][0:16, :], l_, r_, start=st, stop=sp),
                             waits=wF + [ut[c]],
                             inc=("sT" if c in (0, NCH - 1) else None))
                    wF = []
                    if c == 0:
                        if qkv_pend is not None:
                            qkv_flush(qkv_pend, mmF)
                            qkv_pend = None
                        if sc_pend is not None:
                            e2 = OP("a", lambda En, jp=sc_pend,
                                    o_=s_SM[:, 3 + (2 - sc_pend), :]:
                                    En.copy(o_, pf[5][0:16, :]),
                                    waits=[mmF], inc="sA")
                            t_pf_free[5] = e2
                            tok_sm.append(e2)
                            sc_pend = None
                wR = [t_pf_free[5]]
                mmR0 = None
                for c in range(NCH):
                    mmR = OP("t", lambda En, l_=s_ind[:, c, :], r_=ub[:, c, 2:514],
                             st=(c == 0), sp=(c == NCH - 1):
                             En.matmul(pf[5][0:16, :], l_, r_, start=st, stop=sp),
                             waits=wR,
                             inc=("sT" if c in (0, NCH - 1) else None))
                    wR = []
                    if c == 0:
                        mmR0 = mmR
                e1 = OP("a", lambda En, o_=s_SM[:, j, :]:
                        En.copy(o_, pf[4][0:16, :]), waits=[mmR0], inc="sA")
                t_pf_free[4] = e1
                tok_sm.append(e1)
                sc_pend = j
                prev_scores[j % 2] = mmR
            fsc = gfence(pf[4][32:33, 256:384])
            e2 = OP("a", lambda En, o_=s_SM[:, 3 + (2 - 2), :]:
                    En.copy(o_, pf[5][0:16, :]), waits=[fsc], inc="sA")
            t_pf_free[5] = e2
            tok_sm.append(e2)
            sc_pend = None
            tok_u_read = sch.tok("sT")

            OP("v", lambda En, m_=s_mpre[:, t, :, 0:8]:
               En.tensor_tensor(s_SM[:, :, 0:8], s_SM[:, :, 0:8], m_, ALU.mult),
               waits=tok_sm)
            OP("v", lambda En, m_=s_mpre[:, t, :, 8:16]:
               En.tensor_tensor(s_SM[:, :, TT - 8:TT], s_SM[:, :, TT - 8:TT], m_, ALU.mult))

            tok_ao = [None] * 16
            par = 0
            wb_read = [None, None, None, None]
            tok_prev_wv = None
            wvbbuf = (s_wb[0], s_wb[1], s_rsq[0], s_rsq[1])

            def wv_flush(pv, cover_tok):
                i_, pw_, c_, j_, dd_ = pv
                wbb = wvbbuf[i_ % 4]
                ewb = OP("a", lambda En, o_=wbb[:], p_=pw_:
                         En.copy(o_, p_[:]),
                         waits=[cover_tok, wb_read[i_ % 4]], inc="sA")
                t_pf_free[2 + i_ % 4] = ewb
                vs = (2 - j_) if dd_ == 0 else (2 + j_)
                if j_ == 0:
                    tok_ao[8 * dd_ + c_] = OP(
                        "v", lambda En, o_=s_ao[:, 8 * dd_ + c_, :],
                        a_=wbb[:], b_=s_v[:, c_, vs:vs + 512]:
                        En.tensor_mul(o_, a_, b_),
                        waits=[ewb, tok_ve[c_]], inc="sV")
                    wb_read[i_ % 4] = tok_ao[8 * dd_ + c_]
                else:
                    wb_read[i_ % 4] = OP(
                        "v", lambda En, o_=s_wvp[i_ % 2][:],
                        a_=wbb[:], b_=s_v[:, c_, vs:vs + 512]:
                        En.tensor_mul(o_, a_, b_), waits=[ewb], inc="sV")
                    tok_ao[8 * dd_ + c_] = OP(
                        "v", lambda En, o_=s_ao[:, 8 * dd_ + c_, :],
                        a_=s_wvp[i_ % 2][:]:
                        En.tensor_add(o_, o_, a_), inc="sV")
            wv_pend = None
            tok_wf = [None, None]
            for dd in range(2):
                SMB = (s_SMb, s_SMb2)[dd]
                d0 = 3 * dd
                OP("v", lambda En, a_=s_SM[:, d0, :], b_=s_SM[:, d0 + 1, :]:
                   En.tensor_max(s_smxm[:], a_, b_))
                OP("v", lambda En, a_=s_SM[:, d0 + 2, :]:
                   En.tensor_max(s_smxm[:], s_smxm[:], a_))
                for j in range(3):
                    tsb = OP("v", lambda En, a_=s_SM[:, d0 + j, :]:
                             En.tensor_sub(a_, a_, s_smxm[:]),
                             inc=("sV" if j == 2 else None))
                tex = OP("a", lambda En, a_=s_SM[:, d0:d0 + 3, :]:
                         En.activation(a_, a_, AF.Exp),
                         waits=[tsb], inc="sA")
                OP("v", lambda En, a_=s_SM[:, d0, :], b_=s_SM[:, d0 + 1, :]:
                   En.tensor_add(s_smxr[:], a_, b_), waits=[tex])
                tsum = OP("v", lambda En, a_=s_SM[:, d0 + 2, :]:
                          En.tensor_add(s_smxr[:], s_smxr[:], a_), inc="sV")
                OP("a", lambda En: En.activation(s_smxm[:], s_smxr[:], AF.Ln),
                   waits=[tsum])
                trc = OP("a", lambda En: En.activation(s_smxr[:], s_smxm[:], AF.Exp, scale=-1.0),
                         inc="sA")
                for j in range(3):
                    OP("v", lambda En, a_=s_SM[:, d0 + j, :], o_=SMB[:, j, :]:
                       En.tensor_mul(o_, a_, s_smxr[:]),
                       waits=([trc, tok_prev_wv] if j == 0 else []))
                OP("v", lambda En, m_=s_mpre[0:16, t, d0:d0 + 3, 0:8],
                   o_=SMB[:, :, 0:8], mm_=s_mpost[:, t, d0:d0 + 3, 0:8]:
                   En.tensor_tensor(o_, o_, mm_, ALU.mult))
                tok_wf[dd] = OP("v", lambda En, o_=SMB[:, :, TT - 8:TT],
                                m_=s_mpost[:, t, d0:d0 + 3, 8:16]:
                                En.tensor_tensor(o_, o_, m_, ALU.mult),
                                inc="sV")

            for dd in range(2):
                SMB = (s_SMb, s_SMb2)[dd]
                wvw = [tok_wf[dd], tok_u_read]
                if dd == 0 and tok_ao_free is not None:
                    wvw.append(tok_ao_free)
                for c in range(NCH):
                    for j in range(3):
                        pw = pf[2 + par % 4]
                        mmW = OP("t", lambda En, l_=s_sel[:, c, :], r_=SMB[:, j, :],
                                 p_=pw:
                                 En.matmul(p_[:], l_, r_, start=True, stop=True),
                                 waits=wvw + [t_pf_free[2 + par % 4]], inc="sT")
                        wvw = []
                        if wv_pend is not None:
                            wv_flush(wv_pend, mmW)
                        wv_pend = (par, pw, c, j, dd)
                        par += 1
            tok_prev_wv = sch.tok("sT")

            tok_rbf = [None] * NCH
            tok_rsq = [None] * NCH
            stat_mm2 = [None] * NCH
            def emit_stats(so):
                wS = [t_pf_free[4], tok_rbf[so]] if so == 0 else [tok_rbf[so]]
                mS = OP("t", lambda En, r_=s_rbf[:, so, :], st=(so == 0),
                        sp=(so == NCH - 1):
                        En.matmul(pf[4][0:1, :], s_onc[:], r_, start=st, stop=sp),
                        waits=wS, inc=("sT" if so == NCH - 1 else None))
                wS2 = [t_pf_free[5], tok_rsq[so]] if so == 0 else [tok_rsq[so]]
                stat_mm2[so] = mS2 = OP(
                    "t", lambda En, r_=s_rsq[so % 2][:], st=(so == 0),
                    sp=(so == NCH - 1):
                    En.matmul(pf[5][0:1, :], s_onc[:], r_, start=st, stop=sp),
                    waits=wS2, inc="sT")
                return mS, mS2

            for oc in range(NCH):
                p5 = pf[oc % 2]
                w0 = [tok_fa[t], tok_fb[t], t_pf_free[oc % 2]]
                for ic in range(16):
                    B_ = qb if ic < 8 else kb
                    mmt = OP("t", lambda En, l_=B_[:, ic % 8, 128 * oc:128 * (oc + 1)],
                             r_=s_ao[:, ic, :], st=(ic == 0), sp=(ic == 15), p_=p5:
                             En.matmul(p_[:], l_, r_, start=st, stop=sp),
                             waits=w0 + [tok_ao[ic]],
                             inc=("sT" if ic == 15 else None))
                    w0 = []
                    if wv_pend is not None:
                        wv_flush(wv_pend, mmt)
                        wv_pend = None
                if oc >= 1:
                    emit_stats(oc - 1)
                    cov = stat_mm2[oc - 1]
                else:
                    cov = gfence(pf[4][32:33, 256:384])
                OP("v", lambda En, i=s_xhat[:, oc, e0 + 2:e0 + 2 + TT]:
                   En.tensor_tensor(s_x2[:], i, s_cb[:, 2:2 + TT], ALU.mult),
                   waits=[cov])
                OP("v", lambda En, g_=vec("g1")[:, oc:oc + 1], b_=vec("b1")[:, oc:oc + 1]:
                   En.tensor_scalar(s_x2[:], s_x2[:], g_, b_, ALU.mult, ALU.add))
                tok_rbf[oc] = OP(
                    "v", lambda En, o_=s_rbf[:, oc, :], p_=p5, b_=vec("bfb")[:, oc:oc + 1]:
                    En.scalar_tensor_tensor(o_, p_[:], b_, s_x2[:], ALU.add, ALU.add),
                    inc="sV")
                t_pf_free[oc % 2] = tok_rbf[oc]
                tok_rsq[oc] = OP(
                    "v", lambda En, o_=s_rsq[oc % 2][:], a_=s_rbf[:, oc, :]:
                    En.tensor_mul(o_, a_, a_),
                    waits=[stat_mm2[oc - 2]] if oc >= 2 else [], inc="sV")
                if oc == NCH - 1:
                    emit_stats(oc)
            rowm = s_smxm[0:1, :]
            rowe = s_smxr[0:1, :]
            rowt = s_SM[0:1, 0, :]
            fst = gfence(pf[5][32:33, 256:384])
            OP("v", lambda En: En.tensor_scalar_mul(rowm, pf[4][0:1, :], 1.0 / D),
               waits=[fst])
            OP("v", lambda En: En.tensor_scalar_mul(rowe, pf[5][0:1, :], 1.0 / D))
            OP("v", lambda En: En.tensor_mul(rowt, rowm, rowm))
            OP("v", lambda En: En.tensor_sub(rowe, rowe, rowt))
            tve = OP("v", lambda En: En.tensor_scalar_add(rowe, rowe, 1e-6), inc="sV")
            t_pf_free[4] = tve
            t_pf_free[5] = tve
            OP("a", lambda En: En.activation(rowt, rowe, AF.Ln), waits=[tve])
            trs = OP("a", lambda En: En.activation(rowe, rowt, AF.Exp, scale=-0.5),
                     inc="sA")
            tnm = OP("v", lambda En: En.tensor_scalar_mul(rowm, rowm, -1.0), inc="sV")
            bc1 = OP("t", lambda En: En.matmul(pf[2][:], s_on1f[:], rowm, start=True, stop=True),
                     waits=[tnm, t_pf_free[2]], inc="sT")
            bc2 = OP("t", lambda En: En.matmul(pf[3][:], s_on1f[:], rowe, start=True, stop=True),
                     waits=[trs, t_pf_free[3]], inc="sT")
            fbc = gfence(pf[5][32:33, 256:384])
            th2 = None
            for oc in range(NCH):
                OP("v", lambda En, a_=s_rbf[:, oc, :]:
                   En.tensor_tensor(s_x2[:], a_, pf[2][:], ALU.add),
                   waits=([fbc] if oc == 0 else []))
                OP("v", lambda En: En.tensor_tensor(s_x2[:], s_x2[:], pf[3][:], ALU.mult))
                th2 = OP("v", lambda En, o_=s_h2[:, oc, e0:e0 + TT],
                         g_=vec("g2")[:, oc:oc + 1], b_=vec("b2")[:, oc:oc + 1]:
                         En.tensor_scalar(o_, s_x2[:], g_, b_, ALU.mult, ALU.add),
                         inc="sV")
            t_pf_free[2] = th2
            t_pf_free[3] = th2

            tok_u_freed = tok_u_read
            tok_u_freed = sch.tok("sT")
            tok_xpb_free = sch.tok("sT")
            tok_ao_free = sch.tok("sT")
            if t + 1 < NT:
                tok_wk[t + 1] = DMA("g", qb[:], wslice(d_wk, 0), f"dB{(0 - t) % 3}",
                                    waits=[sch.tok("sT")])
                tok_wv[t + 1] = DMA("g", kb[:], wslice(d_wv, 0), f"dB{(1 - t) % 3}",
                                    waits=[sch.tok("sT")])

        if _DBG:
            tD = DMA("s", d_dbg_h2, s_h2[:].rearrange("p c t -> p (c t)"), "dD",
                     waits=[sch.tok("sV")])
            OP("s", lambda En: En.nop(), waits=[tD])
        sch.barrier()

        tok_wo = DMA("g", s_wo[:], wslice(d_wo, 0), "dWo")
        tok_w1a = DMA("g", s_w1a[:], d_w1[:, 0:2048].rearrange("(i p) o -> p i o", p=128),
                      "dW1a")
        pf_free = [None] * 6
        a3_pend = None

        def a3_flush(po, cover):
            pf_free[NT - 1] = OP(
                "a", lambda En, o_=s_s[:, po, 512 * (NT - 1):512 * NT],
                b_=vec("bo")[:, po:po + 1], p_=pf[NT - 1]:
                En.activation(o_, p_[:], AF.Gelu, bias=b_, scale=1.0),
                waits=[cover], inc="sA")

        for oc in range(NCH):
            lastm = [None] * NT
            w0 = [tok_wo]
            for ic in range(NCH):
                for t4 in range(NT):
                    lastm[t4] = OP(
                        "t", lambda En, l_=s_wo[:, ic, 128 * oc:128 * (oc + 1)],
                        r_=s_h2[:, ic, 512 * t4:512 * (t4 + 1)], st=(ic == 0),
                        sp=(ic == NCH - 1), p_=pf[t4]:
                        En.matmul(p_[:], l_, r_, start=st, stop=sp),
                        waits=w0 + ([pf_free[t4]] if ic == 0 else []),
                        inc=("sT" if ic == NCH - 1 or (ic == 0 and t4 == 0)
                             else None))
                    w0 = []
                    if a3_pend is not None:
                        a3_flush(a3_pend, lastm[0])
                        a3_pend = None
            for t4 in range(NT - 1):
                pf_free[t4] = OP(
                    "a", lambda En, o_=s_s[:, oc, 512 * t4:512 * (t4 + 1)],
                    b_=vec("bo")[:, oc:oc + 1], p_=pf[t4]:
                    En.activation(o_, p_[:], AF.Gelu, bias=b_, scale=1.0),
                    waits=[lastm[t4 + 1]], inc="sA")
            a3_pend = oc
        a3_flush(a3_pend, gfence(pf[4][0:1, 0:128]))
        a3_pend = None

        if _DBG:
            tD = DMA("s", d_dbg_s, s_s[:].rearrange("p c t -> p (c t)"), "dD",
                     waits=[sch.tok("sA")])
            OP("s", lambda En: En.nop(), waits=[tD])
        sch.barrier()

        tok_w1b = DMA("g", s_w1b[:], d_w1[:, 2048:4096].rearrange("(i p) o -> p i o", p=128),
                      "dW1b")
        pf_free = [None] * 6
        tok_hdma = [None] * 32
        tok_w2a = None
        f1_pend = None

        def f1_flush(po, cover):
            pf_free[NT - 1] = OP(
                "a", lambda En, o_=s_hst[po % 3][:, 512 * (NT - 1):512 * NT],
                b_=s_b1f[:, po:po + 1], p_=pf[NT - 1]:
                En.activation(o_, p_[:], AF.Relu, bias=b_, scale=1.0),
                waits=[cover], inc="sA")
            tok_hdma[po] = DMA("a", d_h[po], s_hst[po % 3][:], f"dH{po % 4}",
                               waits=[pf_free[t4] for t4 in range(NT)])

        for oc in range(32):
            wb_ = s_w1a if oc < 16 else s_w1b
            wtok = tok_w1a if oc < 16 else tok_w1b
            col = 128 * (oc % 16)
            lastm = [None] * NT
            w0 = [wtok]
            if oc >= 3:
                w0.append(tok_hdma[oc - 3])
            for ic in range(NCH):
                for t4 in range(NT):
                    lastm[t4] = OP(
                        "t", lambda En, l_=wb_[:, ic, col:col + 128],
                        r_=s_xhat[:, ic, 2 + 512 * t4:2 + 512 * (t4 + 1)],
                        st=(ic == 0), sp=(ic == NCH - 1), p_=pf[t4]:
                        En.matmul(p_[:], l_, r_, start=st, stop=sp),
                        waits=w0 + ([pf_free[t4]] if ic == 0 else []),
                        inc=("sT" if ic == NCH - 1 or (ic == 0 and t4 == 0)
                             else None))
                    w0 = []
                    if f1_pend is not None:
                        f1_flush(f1_pend, lastm[0])
                        f1_pend = None
            for t4 in range(NT - 1):
                pf_free[t4] = OP(
                    "a", lambda En, o_=s_hst[oc % 3][:, 512 * t4:512 * (t4 + 1)],
                    b_=s_b1f[:, oc:oc + 1], p_=pf[t4]:
                    En.activation(o_, p_[:], AF.Relu, bias=b_, scale=1.0),
                    waits=[lastm[t4 + 1]], inc="sA")
            f1_pend = oc
            if oc == 15:
                tok_w2a = DMA("g", s_w2a[:],
                              d_w2[0:2048, :].rearrange("(i p) o -> p i o", p=128),
                              "dW2a", waits=[lastm[NT - 1]])
        f1_flush(f1_pend, gfence(pf[4][0:1, 0:128]))
        f1_pend = None

        sch.barrier()

        tok_w2b = DMA("g", s_w2b[:],
                      d_w2[2048:4096, :].rearrange("(i p) o -> p i o", p=128), "dW2b")
        all_h = [sch.tok(f"dH{i}") for i in range(4)]
        pf_free = [None] * 6
        pb_free = [None, None]
        tok_hinA_free = None
        tok_hinB_free = None
        tok_xre_free = None
        tok_out_dma = None
        out_dma_bi = [None, None]
        for t in range(NT):
            tsl = (512 * t, 512 * (t + 1))
            thA = DMA("s", s_hinA[:],
                      d_h[0:16, :, tsl[0]:tsl[1]].rearrange("i p t -> p i t"),
                      "dHA", waits=all_h + [tok_hinA_free])
            thB = DMA("s", s_hinB[:],
                      d_h[16:32, :, tsl[0]:tsl[1]].rearrange("i p t -> p i t"),
                      "dHB", waits=[tok_hinB_free])
            txr = DMA("s", s_xre[:],
                      d_xs[2 + tsl[0]:2 + tsl[1], :].rearrange("(b p) d -> p b d", p=128),
                      "dXR", waits=[tok_xre_free])
            tok_sg = [None] * NCH
            f2_pend = None

            def f2_flush(pv, cover):
                po, pp = pv
                tgs = OP("a", lambda En, o_=s_gsc[po % 2][:], p_=pp,
                         b_=vec("b2f")[:, po:po + 1]:
                         En.activation(o_, p_[:], AF.Silu, bias=b_, scale=1.0),
                         waits=[cover], inc="sA")
                tok_sg[po] = OP(
                    "v", lambda En, o_=s_sg[:, po, :], a_=s_gsc[po % 2][:],
                    b_=s_s[:, po, tsl[0]:tsl[1]]:
                    En.tensor_mul(o_, a_, b_), waits=[tgs], inc="sV")
                pf_free[po % 2] = tok_sg[po]

            for oc in range(NCH):
                p5 = pf[oc % 2]
                w0 = [thA, thB, tok_w2a, tok_w2b, pf_free[oc % 2]]
                for ic in range(32):
                    wb_ = s_w2a if ic < 16 else s_w2b
                    hb_ = s_hinA if ic < 16 else s_hinB
                    mmt = OP("t", lambda En, l_=wb_[:, ic % 16, 128 * oc:128 * (oc + 1)],
                             r_=hb_[:, ic % 16, :], st=(ic == 0), sp=(ic == 31), p_=p5:
                             En.matmul(p_[:], l_, r_, start=st, stop=sp),
                             waits=w0, inc=("sT" if ic in (0, 31) else None))
                    w0 = []
                    if f2_pend is not None:
                        f2_flush(f2_pend, mmt)
                        f2_pend = None
                f2_pend = (oc, p5)
            f2_flush(f2_pend, gfence(pf[3][0:1, 0:128]))
            f2_pend = None
            tok_hinA_free = sch.tok("sT")
            tok_hinB_free = sch.tok("sT")
            for half in range(2):
                touts = []
                for bi in range(2):
                    b4 = 2 * half + bi
                    wv0 = [tok_sg[NCH - 1], pb_free[bi]]
                    if half == 0 and out_dma_bi[bi] is not None:
                        wv0.append(out_dma_bi[bi])
                    for oc in range(NCH):
                        OP("t", lambda En, o_=pb[bi][:, 128 * oc:128 * (oc + 1)],
                           i=s_sg[:, oc, 128 * b4:128 * (b4 + 1)]:
                           En.transpose(o_, i, s_idb[:]), waits=wv0)
                        wv0 = []
                    ttr = OP("t", lambda En:
                             En.matmul(pf[2][0:1, 0:128], s_onc[0:1, 0:1], s_on1[0:1, :],
                                       start=True, stop=True), inc="sT")
                    wadd = [ttr, txr]
                    if out_dma_bi[bi] is not None:
                        wadd.append(out_dma_bi[bi])
                    if t == NT - 1 and half == 1:
                        OP("v", lambda En: En.nop() if _SIMGF else
                           En.nop(cycle_cnt=3000), waits=[ttr])
                    touts.append(OP(
                        "v", lambda En, o_=s_outst[:, bi, :], p_=pb[bi],
                        x_=s_xre[:, b4, :]:
                        En.tensor_tensor(o_, p_[:], x_, ALU.add),
                        waits=wadd, inc="sV"))
                    pb_free[bi] = touts[-1]
                    tcm = OP("v", lambda En: En.nop() if _SIMGF else
                             En.nop(cycle_cnt=700), waits=[touts[-1]],
                             inc="sV")
                    r0 = tsl[0] + 256 * half + 128 * bi
                    out_dma_bi[bi] = DMA(
                        "a", d_y[r0:r0 + 128, :],
                        s_outst[:, bi, :], ("dO", "dO1")[bi], waits=[tcm])
                tok_out_dma = out_dma_bi[1]
            tok_xre_free = sch.tok("sV")

        if _DBG3:
            DMA("s", d_dbg_sg, s_sg[:].rearrange("p c t -> p (c t)"), "dD",
                waits=[sch.tok("sV")])
            DMA("s", d_dbg_gsc[:, 0:TT], s_gsc[0][:], "dD")
            DMA("s", d_dbg_gsc[:, TT:2 * TT], s_gsc[1][:], "dD")
            tD3 = DMA("s", d_dbg_s2, s_s[:].rearrange("p c t -> p (c t)"), "dD")
            OP("s", lambda En: En.nop(), waits=[tD3])
        OP("a", lambda En: En.nop(), waits=[sch.tok("dO"), sch.tok("dO1")])
        OP("s", lambda En: En.nop(),
           waits=[sch.tok("dO"), sch.tok("dO1"), sch.tok("dD")])

        with nc.Block() as blk:
            def runner(key):
                def body(Eng):
                    for needed, fn, inc, inck in sch.q[key]:
                        for sname, val in needed:
                            Eng.wait_ge(sems[sname], val)
                        ins = fn(Eng)
                        if inc is not None:
                            ins.then_inc(sems[inc], inck)
                return body
            blk.tensor(runner("t"))
            blk.vector(runner("v"))
            blk.scalar(runner("a"))
            blk.gpsimd(runner("g"))
            blk.sync(runner("s"))

    return nc


_NC_CACHE = None


def _get_nc():
    global _NC_CACHE
    if _NC_CACHE is None:
        _NC_CACHE = build_program()
    return _NC_CACHE


def _host_consts():
    f = np.float32
    idm = np.eye(128, dtype=f)
    sel = np.zeros((16, NCH, 128), f)
    ind = np.zeros((128, NCH, 16), f)
    for c in range(NCH):
        for dd in range(128):
            h = 2 * c + (1 if dd >= 64 else 0)
            sel[h, c, dd] = 1.0
            ind[dd, c, h] = 0.125
    ones1 = np.ones((1, 128), f)
    onesc = np.ones((128, 1), f)
    return idm, sel.reshape(16, NCH * 128), ind.reshape(128, NCH * 16), ones1, onesc


def _masks_for_core(t0):
    f = np.float32
    mpre = np.ones((16, NT, 6, 16), f)
    mpost = np.ones((16, NT, 6, 16), f)
    for t in range(NT):
        for col in range(16):
            if col < 8:
                g = t0 + 512 * t + col
            else:
                g = t0 + 512 * t + 504 + (col - 8)
            for j in range(3):
                if not (g >= max(1, j)):
                    mpre[:, t, j, col] = 0.0
                if not (g >= j):
                    mpost[:, t, j, col] = 0.0
                if not (g <= min(SEQ - 2, SEQ - 1 - j)):
                    mpre[:, t, 3 + j, col] = 0.0
                if not (g <= SEQ - 1 - j):
                    mpost[:, t, 3 + j, col] = 0.0
    return mpre.reshape(16, NT * 6 * 16), mpost.reshape(16, NT * 6 * 16)


def kernel(L0_attn_input_feature, L0_attn_Wq, L0_attn_bq, L0_attn_Wk, L0_attn_bk,
           L0_attn_Wv, L0_attn_bv, L0_attn_Wfb, L0_attn_bfb, L0_attn_Wo,
           L0_attn_bo, L0_attn_g1, L0_attn_b1, L0_attn_g2, L0_attn_b2,
           L0_ffn_W1, L0_ffn_b1, L0_ffn_W2, L0_ffn_b2):
    f = np.float32
    x = np.ascontiguousarray(np.asarray(L0_attn_input_feature, f))
    W = {k: np.ascontiguousarray(np.asarray(v, f)) for k, v in dict(
        Wq=L0_attn_Wq, Wk=L0_attn_Wk, Wv=L0_attn_Wv, Wfb=L0_attn_Wfb,
        Wo=L0_attn_Wo, W1=L0_ffn_W1, W2=L0_ffn_W2).items()}
    vl = [L0_attn_g1, L0_attn_b1, L0_attn_g2, L0_attn_b2, L0_attn_bq,
          L0_attn_bk, L0_attn_bv, L0_attn_bfb, L0_attn_bo, L0_ffn_b2]
    vecs = np.stack([np.asarray(v, f).reshape(NCH, 128).T for v in vl], axis=1)
    vecs = np.ascontiguousarray(vecs.reshape(128, 80))
    b1f = np.ascontiguousarray(np.asarray(L0_ffn_b1, f).reshape(32, 128).T)
    vb4 = np.stack([np.repeat(np.asarray(v, f).reshape(NCH, 128).T[:, :, None],
                              4, axis=2)
                    for v in (L0_attn_bq, L0_attn_bk, L0_attn_bv)], axis=1)
    vb4 = np.ascontiguousarray(vb4.reshape(128, 3, 32))
    idm, sel, ind, ones1, onesc = _host_consts()

    zeros2 = np.zeros((2, D), f)
    in_maps = []
    for core in range(8):
        b, half = divmod(core, 2)
        if half == 0:
            xs = np.concatenate([zeros2, x[b, 0:TC + 2]], axis=0)
        else:
            xs = np.concatenate([x[b, TC - 2:2 * TC], zeros2], axis=0)
        mpre, mpost = _masks_for_core(half * TC)
        in_maps.append(dict(
            xs=np.ascontiguousarray(xs), vecs=vecs, b1f=b1f, vb4=vb4, idm=idm,
            sel=sel, ind=ind, ones1=ones1, onesc=onesc, mpre=mpre, mpost=mpost,
            **W))

    nc = _get_nc()
    res = run_bass_kernel_spmd(nc, in_maps, core_ids=list(range(8)))
    if _DBG or _DBG3:
        global _DBG_RES
        _DBG_RES = res.results
    ys = [res.results[i]["y"] for i in range(8)]
    out = np.stack([np.concatenate([ys[2 * b], ys[2 * b + 1]], axis=0)
                    for b in range(4)])
    return np.ascontiguousarray(out.astype(np.float32))


# version 13
# speedup vs baseline: 1.0534x; 1.0092x over previous
import numpy as np
import ml_dtypes
from contextlib import ExitStack

import concourse.bass as bass
import concourse.mybir as mybir
from concourse.bass_utils import run_bass_kernel_spmd

F32 = mybir.dt.float32
BF16 = mybir.dt.bfloat16
AF = mybir.ActivationFunctionType
ALU = mybir.AluOpType
AX = mybir.AxisListType.X

D = 1024
NCH = 8
TC = 2048
NL = 2052
TT = 512
NT = TC // TT
E = TT + 4
NB = 17
SEQ = 4096

_DBG = False
_SIMGF = False
_DBG3 = False

_SEMS = (["dD", "dX0", "dX1", "dX2", "dHA", "dHB", "dXR", "dB0", "dB1", "dB2",
          "dC", "dC2", "dWo", "dW1a", "dW1b", "dW2a", "dW2b", "dH0", "dH1",
          "dH2", "dH3",
          "dO", "dO1", "sT", "sV", "sA", "sG", "sS", "bt", "bv", "ba", "bg",
          "bs"])


class Sch:
    ENG = "tvags"

    def __init__(self):
        self.q = {k: [] for k in self.ENG}
        self.cnt = {}
        self.lastw = {k: {} for k in self.ENG}

    def tok(self, sem):
        return (sem, self.cnt.get(sem, 0))

    def op(self, eng, fn, waits=(), inc=None, inck=1):
        lw = self.lastw[eng]
        needed = []
        for w in waits:
            if w is None:
                continue
            sname, val = w
            if val > 0 and lw.get(sname, 0) < val:
                lw[sname] = val
                needed.append((sname, val))
        out_tok = None
        if inc is not None:
            self.cnt[inc] = self.cnt.get(inc, 0) + inck
            out_tok = (inc, self.cnt[inc])
        self.q[eng].append((needed, fn, inc, inck))
        return out_tok

    def barrier(self):
        toks = {}
        for e in self.ENG:
            self.op(e, lambda E: E.drain())
            toks[e] = self.op(e, lambda E: E.nop(), inc="b" + e)
        for e in self.ENG:
            self.op(e, lambda E: E.nop(),
                    waits=[toks[o] for o in self.ENG if o != e])


def build_program():
    nc = bass.Bass("TRN2", target_bir_lowering=False, detect_race_conditions=False)

    d_xs = nc.dram_tensor("xs", [NL, D], F32, kind="ExternalInput").ap()
    d_wq = nc.dram_tensor("Wq", [D, D], F32, kind="ExternalInput").ap()
    d_wk = nc.dram_tensor("Wk", [D, D], F32, kind="ExternalInput").ap()
    d_wv = nc.dram_tensor("Wv", [D, D], F32, kind="ExternalInput").ap()
    d_wfb = nc.dram_tensor("Wfb", [2 * D, D], F32, kind="ExternalInput").ap()
    d_wo = nc.dram_tensor("Wo", [D, D], F32, kind="ExternalInput").ap()
    d_w1 = nc.dram_tensor("W1", [D, 4 * D], F32, kind="ExternalInput").ap()
    d_w2 = nc.dram_tensor("W2", [4 * D, D], F32, kind="ExternalInput").ap()
    d_vecs = nc.dram_tensor("vecs", [128, 80], F32, kind="ExternalInput").ap()
    d_b1f = nc.dram_tensor("b1f", [128, 32], F32, kind="ExternalInput").ap()
    d_vb4 = nc.dram_tensor("vb4", [128, 3, 32], F32, kind="ExternalInput").ap()
    d_id = nc.dram_tensor("idm", [128, 128], F32, kind="ExternalInput").ap()
    d_sel = nc.dram_tensor("sel", [16, NCH * 128], F32, kind="ExternalInput").ap()
    d_ind = nc.dram_tensor("ind", [128, NCH * 16], F32, kind="ExternalInput").ap()
    d_ones1 = nc.dram_tensor("ones1", [1, 128], F32, kind="ExternalInput").ap()
    d_onesc = nc.dram_tensor("onesc", [128, 1], F32, kind="ExternalInput").ap()
    d_mpre = nc.dram_tensor("mpre", [16, NT * 6 * 16], F32, kind="ExternalInput").ap()
    d_mpost = nc.dram_tensor("mpost", [16, NT * 6 * 16], F32, kind="ExternalInput").ap()
    d_y = nc.dram_tensor("y", [TC, D], F32, kind="ExternalOutput").ap()
    d_h = nc.dram_tensor("hscr", [32, 128, TC], BF16, kind="Internal").ap()
    if _DBG3:
        d_dbg_sg = nc.dram_tensor("dbg_sg", [128, NCH * TT], BF16, kind="ExternalOutput").ap()
        d_dbg_gsc = nc.dram_tensor("dbg_gsc", [128, 2 * TT], F32, kind="ExternalOutput").ap()
        d_dbg_s2 = nc.dram_tensor("dbg_s2", [128, NCH * TC], F32, kind="ExternalOutput").ap()
    if _DBG:
        d_dbg_st = nc.dram_tensor("dbg_st", [128, NB * 16], F32, kind="ExternalOutput").ap()
        d_dbg_xhat = nc.dram_tensor("dbg_xhat", [128, NCH * NL], BF16, kind="ExternalOutput").ap()
        d_dbg_crow = nc.dram_tensor("dbg_crow", [1, NL], F32, kind="ExternalOutput").ap()
        d_dbg_h2 = nc.dram_tensor("dbg_h2", [128, NCH * TC], BF16, kind="ExternalOutput").ap()
        d_dbg_s = nc.dram_tensor("dbg_s", [128, NCH * TC], F32, kind="ExternalOutput").ap()

    sch = Sch()

    with ExitStack() as ctx:
        en = ctx.enter_context

        regions = []

        BASE = 16512

        def at(name, shape, dtype, off, stages):
            t = nc.alloc_sbuf_tensor_at(name, shape, dtype, offset=BASE + off)
            nbytes = int(np.prod(shape[1:])) * mybir.dt.size(dtype)
            regions.append((name, off, off + nbytes, frozenset(stages)))
            return t

        ALL = "0AWF2"

        s_id = at("c_id", [128, 128], F32, 0, ALL)
        s_idb = at("c_idb", [128, 128], BF16, 512, ALL)
        s_sel = at("c_sel", [16, NCH, 128], BF16, 768, ALL)
        s_ind = at("c_ind", [128, NCH, 16], BF16, 2816, ALL)
        s_on1 = at("c_on1", [1, 128], BF16, 3072, ALL)
        s_onc = at("c_onc", [128, 1], BF16, 3328, ALL)
        s_vecs = at("c_vecs", [128, 10, 8], F32, 3360, ALL)
        s_b1f = at("c_b1f", [128, 32], F32, 3680, ALL)
        s_mpre = at("c_mpre", [16, NT, 6, 16], F32, 3808, ALL)
        s_mpost = at("c_mpost", [16, NT, 6, 16], F32, 5344, ALL)
        VI = {n: i for i, n in enumerate(
            ["g1", "b1", "g2", "b2", "bq", "bk", "bv", "bfb", "bo", "b2f"])}

        def vec(n):
            return s_vecs[:, VI[n], :]

        s_xhat = at("s_xhat", [128, NCH, NL], BF16, 6912, "0AWF")
        s_w2b = at("s_w2b", [128, 16, D], BF16, 6912, "2")
        s_crow = at("s_crow", [1, NL], F32, 39776, "0A")
        s_sg = at("s_sg", [128, NCH, TT], BF16, 39776, "2")

        s_xtok = [at(f"s_xtok{i}", [128, D], F32, 48000 + 4096 * i, "0")
                  for i in range(3)]
        s_xht = [at(f"s_xht{i}", [128, D], BF16, 60288 + 2048 * i, "0")
                 for i in range(2)]
        s_sq = at("s_sq", [128, D], F32, 64384, "0")
        s_scol = at("s_scol", [128, 16], F32, 68480, "0")
        s_ccols = at("s_ccols", [128, 20], F32, 68544, "0")

        s_SM = at("s_SM", [16, 6, TT], F32, 48000, "A")
        s_smxm = at("s_smxm", [16, TT], F32, 60288, "A")
        s_smxr = at("s_smxr", [16, TT], F32, 62336, "A")
        s_wo = at("s_wo", [128, NCH, D], BF16, 48000, "W")
        s_on1f = at("c_on1f", [1, 128], F32, 64384, "A")
        s_vb4 = at("c_vb4", [128, 3, 32], F32, 67968, "A")
        s_SMb = at("s_SMb", [16, 3, TT], BF16, 64896, "A")
        s_hinA = at("s_hinA", [128, 16, TT], BF16, 48000, "2")

        s_h2 = at("s_h2", [128, NCH, TC], BF16, 68672, "AW")
        s_w1b = at("s_w1b", [128, NCH, 2048], BF16, 68672, "F")
        s_hinB = at("s_hinB", [128, 16, TT], BF16, 68672, "2")
        s_xre = at("s_xre", [128, 4, D], F32, 85056, "2")

        A12 = 101504
        s_B = [at(f"s_B{i}", [128, NCH, D], BF16, A12 + 16384 * i, "A")
               for i in range(3)]
        s_dbgst = nc.alloc_sbuf_tensor_at("s_dbgst", [128, NB, 16], F32,
                                          offset=BASE + 68672)
        s_stat = at("s_stat", [128, 15, 20], F32, 150656, "0")
        s_xpb = at("s_xpb", [128, NCH, E], BF16, 150656, "A")
        s_rbf = at("s_rbf", [128, NCH, TT], BF16, 150656, "A")
        s_q = at("s_q", [128, NCH, E], BF16, 158912, "A")
        s_k = at("s_k", [128, NCH, E], BF16, 167168, "A")
        s_v = at("s_v", [128, NCH, E], BF16, 175424, "A")
        s_u = at("s_u", [128, NCH, E - 2], BF16, 183680, "A")
        s_u2 = nc.alloc_sbuf_tensor_at("s_u2", [128, NCH, E - 2], BF16,
                                       offset=BASE + 150656)
        s_SMb2 = nc.alloc_sbuf_tensor_at("s_SMb2", [16, 3, TT], BF16,
                                         offset=BASE + 202144)
        s_ao = at("s_ao", [128, 16, TT], BF16, 183680, "A")
        s_cb = at("s_cb", [128, E], F32, 200064, "A")
        s_t1 = at("s_t1", [128, E], F32, 202144, "A")
        s_x2 = at("s_x2", [128, TT], F32, 204224, "A")
        s_wb = [at(f"s_wb{i}", [128, TT], BF16, 206272 + 1024 * i, "A")
                for i in range(2)]
        s_wvp = [at(f"s_wvp{i}", [128, TT], BF16, 208320 + 1024 * i, "A")
                 for i in range(2)]
        s_rsq = [at(f"s_rsq{i}", [128, TT], BF16, 210368 + 1024 * i, "A")
                 for i in range(2)]

        s_s = at("s_sgate", [128, NCH, TC], F32, A12, "WF2")
        s_w1a = at("s_w1a", [128, NCH, 2048], BF16, 167040, "WF")
        s_w2a = at("s_w2a", [128, 16, D], BF16, 167040, "2")
        s_hst = [at(f"s_hst{i}", [128, TC], BF16, 199808 + 4096 * i, "F")
                 for i in range(2)]
        s_hst.append(at("s_hst2", [128, TC], BF16, 39776, "F"))
        s_gsc = [at(f"s_gsc{i}", [128, TT], F32, 199808 + 2048 * i, "2")
                 for i in range(2)]
        s_outst = at("s_outst", [128, 2, D], F32, 203904, "2")

        for i in range(len(regions)):
            for j in range(i + 1, len(regions)):
                n1, a1, b1, st1 = regions[i]
                n2, a2, b2, st2 = regions[j]
                if a1 < b2 and a2 < b1 and (st1 & st2):
                    allowed = {("s_xpb", "s_rbf"), ("s_u", "s_ao")}
                    if (n1, n2) not in allowed and (n2, n1) not in allowed:
                        raise AssertionError(f"overlap {n1} {n2}")
                if b1 > 212700 or b2 > 212700:
                    raise AssertionError("sbuf overflow")

        pf = [en(nc.psum_tensor(f"pf{i}", [128, 512], F32)) for i in range(6)]
        pb = [en(nc.psum_tensor(f"pbb{i}", [128, 1024], BF16)) for i in range(2)]

        sems = {n: en(nc.semaphore(n)) for n in _SEMS}

        def OP(eng, fn, waits=(), inc=None, inck=1):
            return sch.op(eng, fn, waits, inc, inck)

        def gfence(target=None):
            if _SIMGF:
                return OP("t", lambda En, t_=target:
                          En.matmul(t_, s_onc[0:1, 0:1], s_on1[0:1, :],
                                    start=True, stop=True), inc="sT")
            anchor = sch.tok("sT")
            return OP("t", lambda En: En.nop(cycle_cnt=700), waits=[anchor],
                      inc="sT")

        def DMA(eng, out, in_, sem, waits=(), **kw):
            return OP(eng,
                      (lambda En, o_=out, i_=in_, k_=dict(kw): En.dma_start(out=o_, in_=i_, **k_)),
                      waits=waits, inc=sem, inck=16)

        DMA("g", s_id[:], d_id, "dC")
        DMA("g", s_idb[:], d_id, "dC")
        DMA("g", s_sel[:], d_sel.rearrange("h (c d) -> h c d", c=NCH), "dC")
        DMA("g", s_ind[:], d_ind.rearrange("p (c h) -> p c h", c=NCH), "dC")
        DMA("g", s_on1[:], d_ones1, "dC")
        DMA("g", s_onc[:], d_onesc, "dC")
        DMA("g", s_vecs[:], d_vecs.rearrange("p (n c) -> p n c", n=10), "dC")
        DMA("g", s_b1f[:], d_b1f, "dC")
        DMA("g", s_mpre[:], d_mpre.rearrange("h (t j c) -> h t j c", t=NT, j=6), "dC")
        tok_consts = DMA("g", s_mpost[:], d_mpost.rearrange("h (t j c) -> h t j c", t=NT, j=6), "dC")

        def wslice(d_w, lo_chunk, n_chunk=8, cols=None):
            ap = d_w[128 * lo_chunk:128 * (lo_chunk + n_chunk), :]
            if cols is not None:
                ap = ap[:, cols[0]:cols[1]]
            return ap.rearrange("(i p) o -> p i o", p=128)

        tok_wq = [None] * (NT + 1)
        tok_wk = [None] * (NT + 1)
        tok_wv = [None] * (NT + 1)
        tok_wq[0] = DMA("g", s_B[0][:], wslice(d_wq, 0), "dB0")
        tok_wk[0] = DMA("g", s_B[1][:], wslice(d_wk, 0), "dB1")
        tok_wv[0] = DMA("g", s_B[2][:], wslice(d_wv, 0), "dB2")

        OP("v", lambda En: En.memset(s_ccols[:], 0.0))

        tok_xtok_free = [None] * NB
        tok_xht_free = [None] * NB
        tok_pb_evac = [None, None]
        tok_pc_evac = None
        tok_load = [None] * NB
        ST = s_stat

        for g0 in range(0, NB, 3):
            grp = list(range(g0, min(g0 + 3, NB)))
            g1 = grp[-1] + 1
            for b in grp:
                nb_ = 4 if b == NB - 1 else 128
                w = [tok_xtok_free[b - 3]] if b >= 3 else []
                tok_load[b] = DMA("s", s_xtok[b % 3][0:nb_, :],
                                  d_xs[128 * b:128 * b + nb_, :],
                                  f"dX{b % 3}", waits=w)
            rtoks = []
            for b in grp:
                xt = s_xtok[b % 3]
                rtoks.append(OP("v", lambda En, a=ST[:, 0, b:b + 1], i=xt[:]:
                                En.tensor_reduce(a, i, AX, ALU.add),
                                waits=[tok_load[b]], inc="sV"))
                rtoks.append(OP("v", lambda En, i=xt[:], a=ST[:, 1, b:b + 1]:
                                En.scalar_tensor_tensor(
                                    s_sq[:], i, 1.0, i, ALU.mult, ALU.mult,
                                    accum_out=a), inc="sV"))
            w1 = OP("v", lambda En, o_=ST[:, 2, g0:g1], a_=ST[:, 0, g0:g1]:
                    En.tensor_scalar_mul(o_, a_, 1.0 / D),
                    waits=rtoks, inc="sV")
            OP("v", lambda En, o_=ST[:, 3, g0:g1], a_=ST[:, 1, g0:g1]:
               En.tensor_scalar_mul(o_, a_, 1.0 / D))
            w2 = OP("v", lambda En, o_=ST[:, 4, g0:g1], a_=ST[:, 2, g0:g1]:
                    En.tensor_mul(o_, a_, a_), waits=[w1], inc="sV")
            w3 = OP("v", lambda En, o_=ST[:, 5, g0:g1], a_=ST[:, 3, g0:g1],
                    b_=ST[:, 4, g0:g1]:
                    En.tensor_sub(o_, a_, b_), waits=[w2], inc="sV")
            tA = OP("v", lambda En, o_=ST[:, 6, g0:g1], a_=ST[:, 5, g0:g1]:
                    En.tensor_scalar_add(o_, a_, 1e-5), waits=[w3], inc="sV")
            wa1 = OP("a", lambda En, o_=ST[:, 7, g0:g1], a_=ST[:, 6, g0:g1]:
                     En.activation(o_, a_, AF.Ln), waits=[tA], inc="sA")
            OP("a", lambda En, o_=ST[:, 8, g0:g1], a_=ST[:, 7, g0:g1]:
               En.activation(o_, a_, AF.Exp, scale=-0.5), waits=[wa1])
            tB = OP("a", lambda En, o_=ST[:, 9, g0:g1], a_=ST[:, 7, g0:g1]:
                    En.activation(o_, a_, AF.Exp, scale=-1.0), inc="sA")
            w4 = OP("v", lambda En, o_=ST[:, 10, g0:g1], a_=ST[:, 5, g0:g1],
                    b_=ST[:, 9, g0:g1]:
                    En.tensor_mul(o_, a_, b_), waits=[tB], inc="sV")
            tC = OP("v", lambda En, o_=ST[:, 11, g0:g1], a_=ST[:, 10, g0:g1]:
                    En.tensor_scalar_add(o_, a_, 1e-6), waits=[w4], inc="sV")
            wa2 = OP("a", lambda En, o_=ST[:, 12, g0:g1], a_=ST[:, 11, g0:g1]:
                     En.activation(o_, a_, AF.Ln), waits=[tC], inc="sA")
            tD = OP("a", lambda En, o_=ST[:, 13, g0:g1], a_=ST[:, 12, g0:g1]:
                    En.activation(o_, a_, AF.Exp, scale=-0.5),
                    waits=[wa2], inc="sA")
            w5b = OP("v", lambda En, o_=ST[:, 14, g0:g1], a_=ST[:, 2, g0:g1],
                     b_=ST[:, 8, g0:g1]:
                     En.scalar_tensor_tensor(o_, a_, -1.0, b_,
                                             ALU.mult, ALU.mult),
                     waits=[tD], inc="sV")

            for b in grp:
                nb_ = 4 if b == NB - 1 else 128
                xs_ = b % 2
                L0 = 128 * b
                xt = s_xtok[b % 3]
                wx = [w5b]
                if b >= 2:
                    wx.append(tok_xht_free[b - 2])
                tE = OP("v", lambda En, o_=s_xht[xs_][:], i=xt[:],
                        g_=ST[:, 8, b:b + 1], m_=ST[:, 14, b:b + 1]:
                        En.tensor_scalar(o_, i, g_, m_, ALU.mult, ALU.add),
                        waits=wx, inc="sV")
                tok_xtok_free[b] = tE

                wt = [tE, tok_consts]
                if tok_pb_evac[xs_] is not None:
                    wt.append(tok_pb_evac[xs_])
                for c in range(NCH):
                    OP("t", lambda En, o_=pb[xs_][:, 128 * c:128 * (c + 1)],
                       i=s_xht[xs_][:, 128 * c:128 * (c + 1)]:
                       En.transpose(o_, i, s_idb[:]), waits=wt)
                    wt = []
                wc = [tok_pc_evac] if tok_pc_evac else []
                OP("t", lambda En, i=ST[:, 13, b:b + 1]:
                   En.transpose(pf[0][0:1, 0:128], i, s_id[:]), waits=wc)
                tok_tr = OP("t", lambda En:
                            En.matmul(pf[1][0:1, 0:128], s_onc[0:1, 0:1],
                                      s_on1[0:1, :],
                                      start=True, stop=True), inc="sT")
                tok_xht_free[b] = tok_tr
                tok_pb_evac[xs_] = OP(
                    "a", lambda En, o_=s_xhat[:, :, L0:L0 + nb_],
                    i=pb[xs_][:].rearrange("p (c t) -> p c t", c=NCH)[:, :, 0:nb_]:
                    En.copy(o_, i), waits=[tok_tr], inc="sA")
                tok_pc_evac = OP(
                    "a", lambda En, o_=s_crow[0:1, L0:L0 + nb_],
                    i=pf[0][0:1, 0:nb_]:
                    En.copy(o_, i), waits=[tok_tr], inc="sA")

        sch.barrier()
        if _DBG:
            DMA("s", d_dbg_st, s_dbgst[:].rearrange("p b c -> p (b c)"), "dD")
            DMA("s", d_dbg_xhat, s_xhat[:].rearrange("p c t -> p (c t)"), "dD")
            DMA("s", d_dbg_crow, s_crow[:], "dD")

        tok_on1f = DMA("g", s_on1f[:], d_ones1, "dC2")
        tok_vb4 = DMA("g", s_vb4[:], d_vb4, "dC2")
        t_pf_free = [None] * 6
        tok_fa = [None] * NT
        tok_fb = [None] * NT
        tok_u_freed = None
        tok_xpb_free = None
        tok_ao_free = None

        for t in range(NT):
            e0 = TT * t
            qb = s_B[(0 - t) % 3]
            kb = s_B[(1 - t) % 3]
            vb = s_B[(2 - t) % 3]

            t1m = OP("t", lambda En, r_=s_crow[0:1, e0:e0 + 512]:
                     En.matmul(pf[4][:], s_on1f[:], r_, start=True, stop=True),
                     waits=[tok_consts, tok_on1f, tok_pc_evac, t_pf_free[4]], inc="sT")
            t2m = OP("t", lambda En, r_=s_crow[0:1, e0 + 512:e0 + E]:
                     En.matmul(pf[5][:, 0:4], s_on1f[:], r_, start=True, stop=True),
                     waits=[t_pf_free[5]], inc="sT")
            fcb = gfence(pf[5][32:33, 256:384])
            ce1 = OP("a", lambda En: En.copy(s_cb[:, 0:512], pf[4][:]),
                     waits=[fcb], inc="sA")
            ce2 = OP("a", lambda En: En.copy(s_cb[:, 512:E], pf[5][:, 0:4]),
                     inc="sA")
            t_pf_free[4] = ce1
            t_pf_free[5] = ce2

            tok_xpb = [None] * NCH
            for c in range(NCH):
                OP("v", lambda En, i=s_xhat[:, c, e0:e0 + E]:
                   En.tensor_tensor(s_t1[:], i, s_cb[:], ALU.mult),
                   waits=[ce1, ce2, tok_xpb_free])
                tok_xpb[c] = OP(
                    "v", lambda En, o_=s_xpb[:, c, :], g_=vec("g1")[:, c:c + 1],
                    b_=vec("b1")[:, c:c + 1]:
                    En.tensor_scalar(o_, s_t1[:], g_, b_, ALU.mult, ALU.add),
                    inc="sV")

            tok_qe = [None] * NCH
            tok_ke = [None] * NCH
            tok_ve = [None] * NCH
            evs = {"q": tok_qe, "k": tok_ke, "v": tok_ve}
            obuf = {"q": s_q, "k": s_k, "v": s_v}
            bias = {"q": "bq", "k": "bk", "v": "bv"}
            wtk = {"q": tok_wq[t], "k": tok_wk[t], "v": tok_wv[t]}
            wbf = {"q": qb, "k": kb, "v": vb}
            lastmm = {}
            qkv_pend = None

            def qkv_flush(pv, cover):
                wn_, oc_, p5_ = pv
                ev1 = OP("a", lambda En, o_=obuf[wn_][:, oc_, 0:512], p_=p5_,
                         b_=vec(bias[wn_])[:, oc_:oc_ + 1]:
                         En.activation(o_, p_[:], AF.Identity, bias=b_, scale=1.0),
                         waits=[cover], inc="sA")
                evs[wn_][oc_] = ev1
                t_pf_free[(0, 1, 4)[oc_ % 3]] = ev1
                if oc_ == NCH - 1:
                    wni = {"q": 0, "k": 1, "v": 2}[wn_]
                    ev2 = OP("a", lambda En, o_=s_t1[:, 32 * wni:32 * wni + 32],
                             p_=pf[2][:, 0:32]: En.copy(o_, p_), inc="sA")
                    t_pf_free[2] = ev2
                    OP("v", lambda En,
                       o_=obuf[wn_][:, :, 512:E],
                       a_=s_t1[:, 32 * wni:32 * wni + 32],
                       b_=s_vb4[:, wni:wni + 1, :]:
                       En.tensor_tensor(o_, a_, b_, ALU.add),
                       waits=[ev2, tok_vb4])

            for wn in ("q", "k", "v"):
                B_ = wbf[wn]
                for oc in range(NCH):
                    bmain = (0, 1, 4)[oc % 3]
                    p5 = pf[bmain]
                    w0 = [wtk[wn], t_pf_free[bmain]]
                    for ic in range(NCH):
                        mmm = OP("t", lambda En, l_=B_[:, ic, 128 * oc:128 * (oc + 1)],
                                 r_=s_xpb[:, ic, 0:512], st=(ic == 0), sp=(ic == NCH - 1),
                                 p_=p5:
                                 En.matmul(p_[:], l_, r_, start=st, stop=sp),
                                 waits=w0 + [tok_xpb[ic]],
                                 inc=("sT" if ic == 0 else None))
                        w0 = []
                        if ic == 0 and qkv_pend is not None:
                            qkv_flush(qkv_pend, mmm)
                            qkv_pend = None
                        mmt = OP("t", lambda En, l_=B_[:, ic, 128 * oc:128 * (oc + 1)],
                                 r_=s_xpb[:, ic, 512:E], st=(ic == 0), sp=(ic == NCH - 1),
                                 p_=pf[2], co=4 * oc:
                                 En.matmul(p_[:, co:co + 4], l_, r_, start=st, stop=sp),
                                 waits=([t_pf_free[2]] if ic == 0 and oc == 0 else []),
                                 inc=("sT" if ic == NCH - 1 else None))
                    qkv_pend = (wn, oc, p5)
                lastmm[wn] = mmt

            tok_fa[t] = DMA("g", qb[:], wslice(d_wfb, 0), f"dB{(0 - t) % 3}", waits=[lastmm["q"]])
            tok_fb[t] = DMA("g", kb[:], wslice(d_wfb, 8), f"dB{(1 - t) % 3}", waits=[lastmm["k"]])
            if t + 1 < NT:
                tok_wq[t + 1] = DMA("g", vb[:], wslice(d_wq, 0), f"dB{(2 - t) % 3}",
                                    waits=[lastmm["v"]])

            tok_sm = []
            prev_scores = [None, None]
            sc_pend = None
            for j in range(3):
                ub = (s_u, s_u2, s_u)[j]
                ut = [None] * NCH
                for c in range(NCH):
                    w = [tok_qe[c], tok_ke[c]]
                    if j == 0 and c == 0 and tok_u_freed is not None:
                        w.append(tok_u_freed)
                    if c == 0 and j >= 2 and prev_scores[j % 2] is not None:
                        w.append(prev_scores[j % 2])
                    ut[c] = OP("v", lambda En, o_=ub[:, c, :],
                               a_=s_q[:, c, 1:E - 1], b_=s_k[:, c, 2 - j:E - j]:
                               En.tensor_tensor(o_, a_, b_, ALU.mult),
                               waits=w, inc="sV")
                wF = [t_pf_free[4]]
                for c in range(NCH):
                    mmF = OP("t", lambda En, l_=s_ind[:, c, :], r_=ub[:, c, 0:512],
                             st=(c == 0), sp=(c == NCH - 1):
                             En.matmul(pf[4][0:16, :], l_, r_, start=st, stop=sp),
                             waits=wF + [ut[c]],
                             inc=("sT" if c in (0, NCH - 1) else None))
                    wF = []
                    if c == 0:
                        if qkv_pend is not None:
                            qkv_flush(qkv_pend, mmF)
                            qkv_pend = None
                        if sc_pend is not None:
                            e2 = OP("a", lambda En, jp=sc_pend,
                                    o_=s_SM[:, 3 + (2 - sc_pend), :]:
                                    En.copy(o_, pf[5][0:16, :]),
                                    waits=[mmF], inc="sA")
                            t_pf_free[5] = e2
                            tok_sm.append(e2)
                            sc_pend = None
                wR = [t_pf_free[5]]
                mmR0 = None
                for c in range(NCH):
                    mmR = OP("t", lambda En, l_=s_ind[:, c, :], r_=ub[:, c, 2:514],
                             st=(c == 0), sp=(c == NCH - 1):
                             En.matmul(pf[5][0:16, :], l_, r_, start=st, stop=sp),
                             waits=wR,
                             inc=("sT" if c in (0, NCH - 1) else None))
                    wR = []
                    if c == 0:
                        mmR0 = mmR
                e1 = OP("a", lambda En, o_=s_SM[:, j, :]:
                        En.copy(o_, pf[4][0:16, :]), waits=[mmR0], inc="sA")
                t_pf_free[4] = e1
                tok_sm.append(e1)
                sc_pend = j
                prev_scores[j % 2] = mmR
            fsc = gfence(pf[4][32:33, 256:384])
            e2 = OP("a", lambda En, o_=s_SM[:, 3 + (2 - 2), :]:
                    En.copy(o_, pf[5][0:16, :]), waits=[fsc], inc="sA")
            t_pf_free[5] = e2
            tok_sm.append(e2)
            sc_pend = None
            tok_u_read = sch.tok("sT")

            OP("v", lambda En, m_=s_mpre[:, t, :, 0:8]:
               En.tensor_tensor(s_SM[:, :, 0:8], s_SM[:, :, 0:8], m_, ALU.mult),
               waits=tok_sm)
            OP("v", lambda En, m_=s_mpre[:, t, :, 8:16]:
               En.tensor_tensor(s_SM[:, :, TT - 8:TT], s_SM[:, :, TT - 8:TT], m_, ALU.mult))

            tok_ao = [None] * 16
            par = 0
            wb_read = [None, None, None, None]
            tok_prev_wv = None
            wvbbuf = (s_wb[0], s_wb[1], s_rsq[0], s_rsq[1])

            def wv_flush(pv, cover_tok):
                i_, pw_, c_, j_, dd_ = pv
                wbb = wvbbuf[i_ % 4]
                ewb = OP("a", lambda En, o_=wbb[:], p_=pw_:
                         En.copy(o_, p_[:]),
                         waits=[cover_tok, wb_read[i_ % 4]], inc="sA")
                t_pf_free[2 + i_ % 4] = ewb
                vs = (2 - j_) if dd_ == 0 else (2 + j_)
                if j_ == 0:
                    tok_ao[8 * dd_ + c_] = OP(
                        "v", lambda En, o_=s_ao[:, 8 * dd_ + c_, :],
                        a_=wbb[:], b_=s_v[:, c_, vs:vs + 512]:
                        En.tensor_mul(o_, a_, b_),
                        waits=[ewb, tok_ve[c_]], inc="sV")
                    wb_read[i_ % 4] = tok_ao[8 * dd_ + c_]
                else:
                    wb_read[i_ % 4] = OP(
                        "v", lambda En, o_=s_wvp[i_ % 2][:],
                        a_=wbb[:], b_=s_v[:, c_, vs:vs + 512]:
                        En.tensor_mul(o_, a_, b_), waits=[ewb], inc="sV")
                    tok_ao[8 * dd_ + c_] = OP(
                        "v", lambda En, o_=s_ao[:, 8 * dd_ + c_, :],
                        a_=s_wvp[i_ % 2][:]:
                        En.tensor_add(o_, o_, a_), inc="sV")
            wv_pend = None
            tok_wf = [None, None]
            for dd in range(2):
                SMB = (s_SMb, s_SMb2)[dd]
                d0 = 3 * dd
                OP("v", lambda En, a_=s_SM[:, d0, :], b_=s_SM[:, d0 + 1, :]:
                   En.tensor_max(s_smxm[:], a_, b_))
                OP("v", lambda En, a_=s_SM[:, d0 + 2, :]:
                   En.tensor_max(s_smxm[:], s_smxm[:], a_))
                for j in range(3):
                    tsb = OP("v", lambda En, a_=s_SM[:, d0 + j, :]:
                             En.tensor_sub(a_, a_, s_smxm[:]),
                             inc=("sV" if j == 2 else None))
                tex = OP("a", lambda En, a_=s_SM[:, d0:d0 + 3, :]:
                         En.activation(a_, a_, AF.Exp),
                         waits=[tsb], inc="sA")
                OP("v", lambda En, a_=s_SM[:, d0, :], b_=s_SM[:, d0 + 1, :]:
                   En.tensor_add(s_smxr[:], a_, b_), waits=[tex])
                tsum = OP("v", lambda En, a_=s_SM[:, d0 + 2, :]:
                          En.tensor_add(s_smxr[:], s_smxr[:], a_), inc="sV")
                OP("a", lambda En: En.activation(s_smxm[:], s_smxr[:], AF.Ln),
                   waits=[tsum])
                trc = OP("a", lambda En: En.activation(s_smxr[:], s_smxm[:], AF.Exp, scale=-1.0),
                         inc="sA")
                for j in range(3):
                    OP("v", lambda En, a_=s_SM[:, d0 + j, :], o_=SMB[:, j, :]:
                       En.tensor_mul(o_, a_, s_smxr[:]),
                       waits=([trc, tok_prev_wv] if j == 0 else []))
                OP("v", lambda En, m_=s_mpre[0:16, t, d0:d0 + 3, 0:8],
                   o_=SMB[:, :, 0:8], mm_=s_mpost[:, t, d0:d0 + 3, 0:8]:
                   En.tensor_tensor(o_, o_, mm_, ALU.mult))
                tok_wf[dd] = OP("v", lambda En, o_=SMB[:, :, TT - 8:TT],
                                m_=s_mpost[:, t, d0:d0 + 3, 8:16]:
                                En.tensor_tensor(o_, o_, m_, ALU.mult),
                                inc="sV")

            for dd in range(2):
                SMB = (s_SMb, s_SMb2)[dd]
                wvw = [tok_wf[dd], tok_u_read]
                if dd == 0 and tok_ao_free is not None:
                    wvw.append(tok_ao_free)
                for c in range(NCH):
                    for j in range(3):
                        pw = pf[2 + par % 4]
                        mmW = OP("t", lambda En, l_=s_sel[:, c, :], r_=SMB[:, j, :],
                                 p_=pw:
                                 En.matmul(p_[:], l_, r_, start=True, stop=True),
                                 waits=wvw + [t_pf_free[2 + par % 4]], inc="sT")
                        wvw = []
                        if wv_pend is not None:
                            wv_flush(wv_pend, mmW)
                        wv_pend = (par, pw, c, j, dd)
                        par += 1
            tok_prev_wv = sch.tok("sT")

            tok_rbf = [None] * NCH
            tok_rsq = [None] * NCH
            stat_mm2 = [None] * NCH
            def emit_stats(so):
                wS = [t_pf_free[4], tok_rbf[so]] if so == 0 else [tok_rbf[so]]
                mS = OP("t", lambda En, r_=s_rbf[:, so, :], st=(so == 0),
                        sp=(so == NCH - 1):
                        En.matmul(pf[4][0:1, :], s_onc[:], r_, start=st, stop=sp),
                        waits=wS, inc=("sT" if so == NCH - 1 else None))
                wS2 = [t_pf_free[5], tok_rsq[so]] if so == 0 else [tok_rsq[so]]
                stat_mm2[so] = mS2 = OP(
                    "t", lambda En, r_=s_rsq[so % 2][:], st=(so == 0),
                    sp=(so == NCH - 1):
                    En.matmul(pf[5][0:1, :], s_onc[:], r_, start=st, stop=sp),
                    waits=wS2, inc="sT")
                return mS, mS2

            for oc in range(NCH):
                p5 = pf[oc % 2]
                w0 = [tok_fa[t], tok_fb[t], t_pf_free[oc % 2]]
                for ic in range(16):
                    B_ = qb if ic < 8 else kb
                    mmt = OP("t", lambda En, l_=B_[:, ic % 8, 128 * oc:128 * (oc + 1)],
                             r_=s_ao[:, ic, :], st=(ic == 0), sp=(ic == 15), p_=p5:
                             En.matmul(p_[:], l_, r_, start=st, stop=sp),
                             waits=w0 + [tok_ao[ic]],
                             inc=("sT" if ic == 15 else None))
                    w0 = []
                    if wv_pend is not None:
                        wv_flush(wv_pend, mmt)
                        wv_pend = None
                if oc >= 1:
                    emit_stats(oc - 1)
                    cov = stat_mm2[oc - 1]
                else:
                    cov = gfence(pf[4][32:33, 256:384])
                OP("v", lambda En, i=s_xhat[:, oc, e0 + 2:e0 + 2 + TT]:
                   En.tensor_tensor(s_x2[:], i, s_cb[:, 2:2 + TT], ALU.mult),
                   waits=[cov])
                OP("v", lambda En, g_=vec("g1")[:, oc:oc + 1], b_=vec("b1")[:, oc:oc + 1]:
                   En.tensor_scalar(s_x2[:], s_x2[:], g_, b_, ALU.mult, ALU.add))
                tok_rbf[oc] = OP(
                    "v", lambda En, o_=s_rbf[:, oc, :], p_=p5, b_=vec("bfb")[:, oc:oc + 1]:
                    En.scalar_tensor_tensor(o_, p_[:], b_, s_x2[:], ALU.add, ALU.add),
                    inc="sV")
                t_pf_free[oc % 2] = tok_rbf[oc]
                tok_rsq[oc] = OP(
                    "v", lambda En, o_=s_rsq[oc % 2][:], a_=s_rbf[:, oc, :]:
                    En.tensor_mul(o_, a_, a_),
                    waits=[stat_mm2[oc - 2]] if oc >= 2 else [], inc="sV")
                if oc == NCH - 1:
                    emit_stats(oc)
            rowm = s_smxm[0:1, :]
            rowe = s_smxr[0:1, :]
            rowt = s_SM[0:1, 0, :]
            fst = gfence(pf[5][32:33, 256:384])
            OP("v", lambda En: En.tensor_scalar_mul(rowm, pf[4][0:1, :], 1.0 / D),
               waits=[fst])
            OP("v", lambda En: En.tensor_scalar_mul(rowe, pf[5][0:1, :], 1.0 / D))
            OP("v", lambda En: En.tensor_mul(rowt, rowm, rowm))
            OP("v", lambda En: En.tensor_sub(rowe, rowe, rowt))
            tve = OP("v", lambda En: En.tensor_scalar_add(rowe, rowe, 1e-6), inc="sV")
            t_pf_free[4] = tve
            t_pf_free[5] = tve
            OP("a", lambda En: En.activation(rowt, rowe, AF.Ln), waits=[tve])
            trs = OP("a", lambda En: En.activation(rowe, rowt, AF.Exp, scale=-0.5),
                     inc="sA")
            tnm = OP("v", lambda En: En.tensor_scalar_mul(rowm, rowm, -1.0), inc="sV")
            bc1 = OP("t", lambda En: En.matmul(pf[2][:], s_on1f[:], rowm, start=True, stop=True),
                     waits=[tnm, t_pf_free[2]], inc="sT")
            bc2 = OP("t", lambda En: En.matmul(pf[3][:], s_on1f[:], rowe, start=True, stop=True),
                     waits=[trs, t_pf_free[3]], inc="sT")
            fbc = gfence(pf[5][32:33, 256:384])
            th2 = None
            for oc in range(NCH):
                OP("v", lambda En, a_=s_rbf[:, oc, :]:
                   En.tensor_tensor(s_x2[:], a_, pf[2][:], ALU.add),
                   waits=([fbc] if oc == 0 else []))
                OP("v", lambda En: En.tensor_tensor(s_x2[:], s_x2[:], pf[3][:], ALU.mult))
                th2 = OP("v", lambda En, o_=s_h2[:, oc, e0:e0 + TT],
                         g_=vec("g2")[:, oc:oc + 1], b_=vec("b2")[:, oc:oc + 1]:
                         En.tensor_scalar(o_, s_x2[:], g_, b_, ALU.mult, ALU.add),
                         inc="sV")
            t_pf_free[2] = th2
            t_pf_free[3] = th2

            tok_u_freed = tok_u_read
            tok_u_freed = sch.tok("sT")
            tok_xpb_free = sch.tok("sT")
            tok_ao_free = sch.tok("sT")
            if t + 1 < NT:
                tok_wk[t + 1] = DMA("g", qb[:], wslice(d_wk, 0), f"dB{(0 - t) % 3}",
                                    waits=[sch.tok("sT")])
                tok_wv[t + 1] = DMA("g", kb[:], wslice(d_wv, 0), f"dB{(1 - t) % 3}",
                                    waits=[sch.tok("sT")])

        if _DBG:
            tD = DMA("s", d_dbg_h2, s_h2[:].rearrange("p c t -> p (c t)"), "dD",
                     waits=[sch.tok("sV")])
            OP("s", lambda En: En.nop(), waits=[tD])
        sch.barrier()

        tok_wo = DMA("g", s_wo[:], wslice(d_wo, 0), "dWo")
        tok_w1a = DMA("g", s_w1a[:], d_w1[:, 0:2048].rearrange("(i p) o -> p i o", p=128),
                      "dW1a")
        pf_free = [None] * 6
        a3_pend = None

        def a3_flush(po, cover):
            pf_free[NT - 1] = OP(
                "a", lambda En, o_=s_s[:, po, 512 * (NT - 1):512 * NT],
                b_=vec("bo")[:, po:po + 1], p_=pf[NT - 1]:
                En.activation(o_, p_[:], AF.Gelu, bias=b_, scale=1.0),
                waits=[cover], inc="sA")

        for oc in range(NCH):
            lastm = [None] * NT
            w0 = [tok_wo]
            for ic in range(NCH):
                for t4 in range(NT):
                    lastm[t4] = OP(
                        "t", lambda En, l_=s_wo[:, ic, 128 * oc:128 * (oc + 1)],
                        r_=s_h2[:, ic, 512 * t4:512 * (t4 + 1)], st=(ic == 0),
                        sp=(ic == NCH - 1), p_=pf[t4]:
                        En.matmul(p_[:], l_, r_, start=st, stop=sp),
                        waits=w0 + ([pf_free[t4]] if ic == 0 else []),
                        inc=("sT" if ic == NCH - 1 or (ic == 0 and t4 == 0)
                             else None))
                    w0 = []
                    if a3_pend is not None:
                        a3_flush(a3_pend, lastm[0])
                        a3_pend = None
            for t4 in range(NT - 1):
                pf_free[t4] = OP(
                    "a", lambda En, o_=s_s[:, oc, 512 * t4:512 * (t4 + 1)],
                    b_=vec("bo")[:, oc:oc + 1], p_=pf[t4]:
                    En.activation(o_, p_[:], AF.Gelu, bias=b_, scale=1.0),
                    waits=[lastm[t4 + 1]], inc="sA")
            a3_pend = oc
        a3_flush(a3_pend, gfence(pf[4][0:1, 0:128]))
        a3_pend = None

        if _DBG:
            tD = DMA("s", d_dbg_s, s_s[:].rearrange("p c t -> p (c t)"), "dD",
                     waits=[sch.tok("sA")])
            OP("s", lambda En: En.nop(), waits=[tD])
        sch.barrier()

        tok_w1b = DMA("g", s_w1b[:], d_w1[:, 2048:4096].rearrange("(i p) o -> p i o", p=128),
                      "dW1b")
        pf_free = [None] * 6
        tok_hdma = [None] * 32
        tok_w2a = None
        f1_pend = None
        hA_toks = []

        def f1_flush(po, cover):
            pf_free[NT - 1] = OP(
                "a", lambda En, o_=s_hst[po % 3][:, 512 * (NT - 1):512 * NT],
                b_=s_b1f[:, po:po + 1], p_=pf[NT - 1]:
                En.activation(o_, p_[:], AF.Relu, bias=b_, scale=1.0),
                waits=[cover], inc="sA")
            tok_hdma[po] = DMA("a", d_h[po], s_hst[po % 3][:], f"dH{po % 4}",
                               waits=[pf_free[t4] for t4 in range(NT)])
            if po == 15:
                hA_toks.extend(sch.tok(f"dH{i}") for i in range(4))

        for oc in range(32):
            wb_ = s_w1a if oc < 16 else s_w1b
            wtok = tok_w1a if oc < 16 else tok_w1b
            col = 128 * (oc % 16)
            lastm = [None] * NT
            w0 = [wtok]
            if oc >= 3:
                w0.append(tok_hdma[oc - 3])
            for ic in range(NCH):
                for t4 in range(NT):
                    lastm[t4] = OP(
                        "t", lambda En, l_=wb_[:, ic, col:col + 128],
                        r_=s_xhat[:, ic, 2 + 512 * t4:2 + 512 * (t4 + 1)],
                        st=(ic == 0), sp=(ic == NCH - 1), p_=pf[t4]:
                        En.matmul(p_[:], l_, r_, start=st, stop=sp),
                        waits=w0 + ([pf_free[t4]] if ic == 0 else []),
                        inc=("sT" if ic == NCH - 1 or (ic == 0 and t4 == 0)
                             else None))
                    w0 = []
                    if f1_pend is not None:
                        f1_flush(f1_pend, lastm[0])
                        f1_pend = None
            for t4 in range(NT - 1):
                pf_free[t4] = OP(
                    "a", lambda En, o_=s_hst[oc % 3][:, 512 * t4:512 * (t4 + 1)],
                    b_=s_b1f[:, oc:oc + 1], p_=pf[t4]:
                    En.activation(o_, p_[:], AF.Relu, bias=b_, scale=1.0),
                    waits=[lastm[t4 + 1]], inc="sA")
            f1_pend = oc
            if oc == 15:
                tok_w2a = DMA("g", s_w2a[:],
                              d_w2[0:2048, :].rearrange("(i p) o -> p i o", p=128),
                              "dW2a", waits=[lastm[NT - 1]])
        f1_flush(f1_pend, gfence(pf[4][0:1, 0:128]))
        f1_pend = None

        sch.barrier()

        tok_w2b = DMA("g", s_w2b[:],
                      d_w2[2048:4096, :].rearrange("(i p) o -> p i o", p=128), "dW2b")
        all_h = [sch.tok(f"dH{i}") for i in range(4)]
        pf_free = [None] * 6
        pb_free = [None, None]
        tok_hinA_free = None
        tok_hinB_free = None
        tok_xre_free = None
        tok_out_dma = None
        out_dma_bi = [None, None]
        for t in range(NT):
            tsl = (512 * t, 512 * (t + 1))
            thA = DMA("s", s_hinA[:],
                      d_h[0:16, :, tsl[0]:tsl[1]].rearrange("i p t -> p i t"),
                      "dHA", waits=(hA_toks if t == 0 else []) + [tok_hinA_free])
            thB = DMA("s", s_hinB[:],
                      d_h[16:32, :, tsl[0]:tsl[1]].rearrange("i p t -> p i t"),
                      "dHB", waits=[tok_hinB_free])
            txr = DMA("s", s_xre[:],
                      d_xs[2 + tsl[0]:2 + tsl[1], :].rearrange("(b p) d -> p b d", p=128),
                      "dXR", waits=[tok_xre_free])
            tok_sg = [None] * NCH
            f2_pend = None

            def f2_flush(pv, cover):
                po, pp = pv
                tgs = OP("a", lambda En, o_=s_gsc[po % 2][:], p_=pp,
                         b_=vec("b2f")[:, po:po + 1]:
                         En.activation(o_, p_[:], AF.Silu, bias=b_, scale=1.0),
                         waits=[cover], inc="sA")
                tok_sg[po] = OP(
                    "v", lambda En, o_=s_sg[:, po, :], a_=s_gsc[po % 2][:],
                    b_=s_s[:, po, tsl[0]:tsl[1]]:
                    En.tensor_mul(o_, a_, b_), waits=[tgs], inc="sV")
                pf_free[po % 2] = tok_sg[po]

            for oc in range(NCH):
                p5 = pf[oc % 2]
                w0 = [thA, thB, tok_w2a, tok_w2b, pf_free[oc % 2]]
                for ic in range(32):
                    wb_ = s_w2a if ic < 16 else s_w2b
                    hb_ = s_hinA if ic < 16 else s_hinB
                    mmt = OP("t", lambda En, l_=wb_[:, ic % 16, 128 * oc:128 * (oc + 1)],
                             r_=hb_[:, ic % 16, :], st=(ic == 0), sp=(ic == 31), p_=p5:
                             En.matmul(p_[:], l_, r_, start=st, stop=sp),
                             waits=w0, inc=("sT" if ic in (0, 31) else None))
                    w0 = []
                    if f2_pend is not None:
                        f2_flush(f2_pend, mmt)
                        f2_pend = None
                f2_pend = (oc, p5)
            f2_flush(f2_pend, gfence(pf[3][0:1, 0:128]))
            f2_pend = None
            tok_hinA_free = sch.tok("sT")
            tok_hinB_free = sch.tok("sT")
            for half in range(2):
                touts = []
                for bi in range(2):
                    b4 = 2 * half + bi
                    wv0 = [tok_sg[NCH - 1], pb_free[bi]]
                    if half == 0 and out_dma_bi[bi] is not None:
                        wv0.append(out_dma_bi[bi])
                    for oc in range(NCH):
                        OP("t", lambda En, o_=pb[bi][:, 128 * oc:128 * (oc + 1)],
                           i=s_sg[:, oc, 128 * b4:128 * (b4 + 1)]:
                           En.transpose(o_, i, s_idb[:]), waits=wv0)
                        wv0 = []
                    ttr = OP("t", lambda En:
                             En.matmul(pf[2][0:1, 0:128], s_onc[0:1, 0:1], s_on1[0:1, :],
                                       start=True, stop=True), inc="sT")
                    wadd = [ttr, txr]
                    if out_dma_bi[bi] is not None:
                        wadd.append(out_dma_bi[bi])
                    if t == NT - 1 and half == 1:
                        OP("v", lambda En: En.nop() if _SIMGF else
                           En.nop(cycle_cnt=3000), waits=[ttr])
                    touts.append(OP(
                        "v", lambda En, o_=s_outst[:, bi, :], p_=pb[bi],
                        x_=s_xre[:, b4, :]:
                        En.tensor_tensor(o_, p_[:], x_, ALU.add),
                        waits=wadd, inc="sV"))
                    pb_free[bi] = touts[-1]
                    tcm = OP("v", lambda En: En.nop() if _SIMGF else
                             En.nop(cycle_cnt=700), waits=[touts[-1]],
                             inc="sV")
                    r0 = tsl[0] + 256 * half + 128 * bi
                    out_dma_bi[bi] = DMA(
                        "a", d_y[r0:r0 + 128, :],
                        s_outst[:, bi, :], ("dO", "dO1")[bi], waits=[tcm])
                tok_out_dma = out_dma_bi[1]
            tok_xre_free = sch.tok("sV")

        if _DBG3:
            DMA("s", d_dbg_sg, s_sg[:].rearrange("p c t -> p (c t)"), "dD",
                waits=[sch.tok("sV")])
            DMA("s", d_dbg_gsc[:, 0:TT], s_gsc[0][:], "dD")
            DMA("s", d_dbg_gsc[:, TT:2 * TT], s_gsc[1][:], "dD")
            tD3 = DMA("s", d_dbg_s2, s_s[:].rearrange("p c t -> p (c t)"), "dD")
            OP("s", lambda En: En.nop(), waits=[tD3])
        OP("a", lambda En: En.nop(), waits=[sch.tok("dO"), sch.tok("dO1")])
        OP("s", lambda En: En.nop(),
           waits=[sch.tok("dO"), sch.tok("dO1"), sch.tok("dD")])

        with nc.Block() as blk:
            def runner(key):
                def body(Eng):
                    for needed, fn, inc, inck in sch.q[key]:
                        for sname, val in needed:
                            Eng.wait_ge(sems[sname], val)
                        ins = fn(Eng)
                        if inc is not None:
                            ins.then_inc(sems[inc], inck)
                return body
            blk.tensor(runner("t"))
            blk.vector(runner("v"))
            blk.scalar(runner("a"))
            blk.gpsimd(runner("g"))
            blk.sync(runner("s"))

    return nc


_NC_CACHE = None


def _get_nc():
    global _NC_CACHE
    if _NC_CACHE is None:
        _NC_CACHE = build_program()
    return _NC_CACHE


def _host_consts():
    f = np.float32
    idm = np.eye(128, dtype=f)
    sel = np.zeros((16, NCH, 128), f)
    ind = np.zeros((128, NCH, 16), f)
    for c in range(NCH):
        for dd in range(128):
            h = 2 * c + (1 if dd >= 64 else 0)
            sel[h, c, dd] = 1.0
            ind[dd, c, h] = 0.125
    ones1 = np.ones((1, 128), f)
    onesc = np.ones((128, 1), f)
    return idm, sel.reshape(16, NCH * 128), ind.reshape(128, NCH * 16), ones1, onesc


def _masks_for_core(t0):
    f = np.float32
    mpre = np.ones((16, NT, 6, 16), f)
    mpost = np.ones((16, NT, 6, 16), f)
    for t in range(NT):
        for col in range(16):
            if col < 8:
                g = t0 + 512 * t + col
            else:
                g = t0 + 512 * t + 504 + (col - 8)
            for j in range(3):
                if not (g >= max(1, j)):
                    mpre[:, t, j, col] = 0.0
                if not (g >= j):
                    mpost[:, t, j, col] = 0.0
                if not (g <= min(SEQ - 2, SEQ - 1 - j)):
                    mpre[:, t, 3 + j, col] = 0.0
                if not (g <= SEQ - 1 - j):
                    mpost[:, t, 3 + j, col] = 0.0
    return mpre.reshape(16, NT * 6 * 16), mpost.reshape(16, NT * 6 * 16)


def kernel(L0_attn_input_feature, L0_attn_Wq, L0_attn_bq, L0_attn_Wk, L0_attn_bk,
           L0_attn_Wv, L0_attn_bv, L0_attn_Wfb, L0_attn_bfb, L0_attn_Wo,
           L0_attn_bo, L0_attn_g1, L0_attn_b1, L0_attn_g2, L0_attn_b2,
           L0_ffn_W1, L0_ffn_b1, L0_ffn_W2, L0_ffn_b2):
    f = np.float32
    x = np.ascontiguousarray(np.asarray(L0_attn_input_feature, f))
    W = {k: np.ascontiguousarray(np.asarray(v, f)) for k, v in dict(
        Wq=L0_attn_Wq, Wk=L0_attn_Wk, Wv=L0_attn_Wv, Wfb=L0_attn_Wfb,
        Wo=L0_attn_Wo, W1=L0_ffn_W1, W2=L0_ffn_W2).items()}
    vl = [L0_attn_g1, L0_attn_b1, L0_attn_g2, L0_attn_b2, L0_attn_bq,
          L0_attn_bk, L0_attn_bv, L0_attn_bfb, L0_attn_bo, L0_ffn_b2]
    vecs = np.stack([np.asarray(v, f).reshape(NCH, 128).T for v in vl], axis=1)
    vecs = np.ascontiguousarray(vecs.reshape(128, 80))
    b1f = np.ascontiguousarray(np.asarray(L0_ffn_b1, f).reshape(32, 128).T)
    vb4 = np.stack([np.repeat(np.asarray(v, f).reshape(NCH, 128).T[:, :, None],
                              4, axis=2)
                    for v in (L0_attn_bq, L0_attn_bk, L0_attn_bv)], axis=1)
    vb4 = np.ascontiguousarray(vb4.reshape(128, 3, 32))
    idm, sel, ind, ones1, onesc = _host_consts()

    zeros2 = np.zeros((2, D), f)
    in_maps = []
    for core in range(8):
        b, half = divmod(core, 2)
        if half == 0:
            xs = np.concatenate([zeros2, x[b, 0:TC + 2]], axis=0)
        else:
            xs = np.concatenate([x[b, TC - 2:2 * TC], zeros2], axis=0)
        mpre, mpost = _masks_for_core(half * TC)
        in_maps.append(dict(
            xs=np.ascontiguousarray(xs), vecs=vecs, b1f=b1f, vb4=vb4, idm=idm,
            sel=sel, ind=ind, ones1=ones1, onesc=onesc, mpre=mpre, mpost=mpost,
            **W))

    nc = _get_nc()
    res = run_bass_kernel_spmd(nc, in_maps, core_ids=list(range(8)))
    if _DBG or _DBG3:
        global _DBG_RES
        _DBG_RES = res.results
    ys = [res.results[i]["y"] for i in range(8)]
    out = np.stack([np.concatenate([ys[2 * b], ys[2 * b + 1]], axis=0)
                    for b in range(4)])
    return np.ascontiguousarray(out.astype(np.float32))


# version 15
# speedup vs baseline: 1.0572x; 1.0036x over previous
import numpy as np
import ml_dtypes
from contextlib import ExitStack

import concourse.bass as bass
import concourse.mybir as mybir
from concourse.bass_utils import run_bass_kernel_spmd

F32 = mybir.dt.float32
BF16 = mybir.dt.bfloat16
AF = mybir.ActivationFunctionType
ALU = mybir.AluOpType
AX = mybir.AxisListType.X

D = 1024
NCH = 8
TC = 2048
NL = 2052
TT = 512
NT = TC // TT
E = TT + 4
NB = 17
SEQ = 4096

_DBG = False
_SIMGF = False
_DBG3 = False

_SEMS = (["dD", "dX0", "dX1", "dX2", "dHA", "dHB", "dXR", "dB0", "dB1", "dB2",
          "dC", "dC2", "dWo", "dW1a", "dW1b", "dW2a", "dW2b", "dH0", "dH1",
          "dH2", "dH3", "dHA0", "dHA1", "dHA2", "dHA3", "dHB0", "dHB1",
          "dHB2", "dHB3",
          "dO", "dO1", "sT", "sV", "sA", "sG", "sS", "bt", "bv", "ba", "bg",
          "bs"])


class Sch:
    ENG = "tvags"

    def __init__(self):
        self.q = {k: [] for k in self.ENG}
        self.cnt = {}
        self.lastw = {k: {} for k in self.ENG}

    def tok(self, sem):
        return (sem, self.cnt.get(sem, 0))

    def op(self, eng, fn, waits=(), inc=None, inck=1):
        lw = self.lastw[eng]
        needed = []
        for w in waits:
            if w is None:
                continue
            sname, val = w
            if val > 0 and lw.get(sname, 0) < val:
                lw[sname] = val
                needed.append((sname, val))
        out_tok = None
        if inc is not None:
            self.cnt[inc] = self.cnt.get(inc, 0) + inck
            out_tok = (inc, self.cnt[inc])
        self.q[eng].append((needed, fn, inc, inck))
        return out_tok

    def barrier(self):
        toks = {}
        for e in self.ENG:
            self.op(e, lambda E: E.drain())
            toks[e] = self.op(e, lambda E: E.nop(), inc="b" + e)
        for e in self.ENG:
            self.op(e, lambda E: E.nop(),
                    waits=[toks[o] for o in self.ENG if o != e])


def build_program():
    nc = bass.Bass("TRN2", target_bir_lowering=False, detect_race_conditions=False)

    d_xs = nc.dram_tensor("xs", [NL, D], F32, kind="ExternalInput").ap()
    d_wq = nc.dram_tensor("Wq", [D, D], F32, kind="ExternalInput").ap()
    d_wk = nc.dram_tensor("Wk", [D, D], F32, kind="ExternalInput").ap()
    d_wv = nc.dram_tensor("Wv", [D, D], F32, kind="ExternalInput").ap()
    d_wfb = nc.dram_tensor("Wfb", [2 * D, D], F32, kind="ExternalInput").ap()
    d_wo = nc.dram_tensor("Wo", [D, D], F32, kind="ExternalInput").ap()
    d_w1 = nc.dram_tensor("W1", [D, 4 * D], F32, kind="ExternalInput").ap()
    d_w2 = nc.dram_tensor("W2", [4 * D, D], F32, kind="ExternalInput").ap()
    d_vecs = nc.dram_tensor("vecs", [128, 80], F32, kind="ExternalInput").ap()
    d_b1f = nc.dram_tensor("b1f", [128, 32], F32, kind="ExternalInput").ap()
    d_vb4 = nc.dram_tensor("vb4", [128, 3, 32], F32, kind="ExternalInput").ap()
    d_id = nc.dram_tensor("idm", [128, 128], F32, kind="ExternalInput").ap()
    d_sel = nc.dram_tensor("sel", [16, NCH * 128], F32, kind="ExternalInput").ap()
    d_ind = nc.dram_tensor("ind", [128, NCH * 16], F32, kind="ExternalInput").ap()
    d_ones1 = nc.dram_tensor("ones1", [1, 128], F32, kind="ExternalInput").ap()
    d_onesc = nc.dram_tensor("onesc", [128, 1], F32, kind="ExternalInput").ap()
    d_mpre = nc.dram_tensor("mpre", [16, NT * 6 * 16], F32, kind="ExternalInput").ap()
    d_mpost = nc.dram_tensor("mpost", [16, NT * 6 * 16], F32, kind="ExternalInput").ap()
    d_y = nc.dram_tensor("y", [TC, D], F32, kind="ExternalOutput").ap()
    d_h = nc.dram_tensor("hscr", [32, 128, TC], BF16, kind="Internal").ap()
    if _DBG3:
        d_dbg_sg = nc.dram_tensor("dbg_sg", [128, NCH * TT], BF16, kind="ExternalOutput").ap()
        d_dbg_gsc = nc.dram_tensor("dbg_gsc", [128, 2 * TT], F32, kind="ExternalOutput").ap()
        d_dbg_s2 = nc.dram_tensor("dbg_s2", [128, NCH * TC], F32, kind="ExternalOutput").ap()
    if _DBG:
        d_dbg_st = nc.dram_tensor("dbg_st", [128, NB * 16], F32, kind="ExternalOutput").ap()
        d_dbg_xhat = nc.dram_tensor("dbg_xhat", [128, NCH * NL], BF16, kind="ExternalOutput").ap()
        d_dbg_crow = nc.dram_tensor("dbg_crow", [1, NL], F32, kind="ExternalOutput").ap()
        d_dbg_h2 = nc.dram_tensor("dbg_h2", [128, NCH * TC], BF16, kind="ExternalOutput").ap()
        d_dbg_s = nc.dram_tensor("dbg_s", [128, NCH * TC], F32, kind="ExternalOutput").ap()

    sch = Sch()

    with ExitStack() as ctx:
        en = ctx.enter_context

        regions = []

        BASE = 16512

        def at(name, shape, dtype, off, stages):
            t = nc.alloc_sbuf_tensor_at(name, shape, dtype, offset=BASE + off)
            nbytes = int(np.prod(shape[1:])) * mybir.dt.size(dtype)
            regions.append((name, off, off + nbytes, frozenset(stages)))
            return t

        ALL = "0AWF2"

        s_id = at("c_id", [128, 128], F32, 0, ALL)
        s_idb = at("c_idb", [128, 128], BF16, 512, ALL)
        s_sel = at("c_sel", [16, NCH, 128], BF16, 768, ALL)
        s_ind = at("c_ind", [128, NCH, 16], BF16, 2816, ALL)
        s_on1 = at("c_on1", [1, 128], BF16, 3072, ALL)
        s_onc = at("c_onc", [128, 1], BF16, 3328, ALL)
        s_vecs = at("c_vecs", [128, 10, 8], F32, 3360, ALL)
        s_b1f = at("c_b1f", [128, 32], F32, 3680, ALL)
        s_mpre = at("c_mpre", [16, NT, 6, 16], F32, 3808, ALL)
        s_mpost = at("c_mpost", [16, NT, 6, 16], F32, 5344, ALL)
        VI = {n: i for i, n in enumerate(
            ["g1", "b1", "g2", "b2", "bq", "bk", "bv", "bfb", "bo", "b2f"])}

        def vec(n):
            return s_vecs[:, VI[n], :]

        s_xhat = at("s_xhat", [128, NCH, NL], BF16, 6912, "0AWF")
        s_w2b = at("s_w2b", [128, 16, D], BF16, 6912, "2")
        s_crow = at("s_crow", [1, NL], F32, 39776, "0A")
        s_sg = at("s_sg", [128, NCH, TT], BF16, 39776, "2")

        s_xtok = [at(f"s_xtok{i}", [128, D], F32, 48000 + 4096 * i, "0")
                  for i in range(3)]
        s_xht = [at(f"s_xht{i}", [128, D], BF16, 60288 + 2048 * i, "0")
                 for i in range(2)]
        s_sq = at("s_sq", [128, D], F32, 64384, "0")
        s_scol = at("s_scol", [128, 16], F32, 68480, "0")
        s_ccols = at("s_ccols", [128, 20], F32, 68544, "0")

        s_SM = at("s_SM", [16, 6, TT], F32, 48000, "A")
        s_smxm = at("s_smxm", [16, TT], F32, 60288, "A")
        s_smxr = at("s_smxr", [16, TT], F32, 62336, "A")
        s_wo = at("s_wo", [128, NCH, D], BF16, 48000, "W")
        s_on1f = at("c_on1f", [1, 128], F32, 64384, "A")
        s_vb4 = at("c_vb4", [128, 3, 32], F32, 67968, "A")
        s_SMb = at("s_SMb", [16, 3, TT], BF16, 64896, "A")
        s_hinA = at("s_hinA", [128, 16, TT], BF16, 48000, "2")

        s_h2 = at("s_h2", [128, NCH, TC], BF16, 68672, "AW")
        s_w1b = at("s_w1b", [128, NCH, 2048], BF16, 68672, "F")
        s_hinB = at("s_hinB", [128, 16, TT], BF16, 68672, "2")
        s_xre = at("s_xre", [128, 4, D], F32, 85056, "2")

        A12 = 101504
        s_B = [at(f"s_B{i}", [128, NCH, D], BF16, A12 + 16384 * i, "A")
               for i in range(3)]
        s_dbgst = nc.alloc_sbuf_tensor_at("s_dbgst", [128, NB, 16], F32,
                                          offset=BASE + 68672)
        s_stat = at("s_stat", [128, 15, 20], F32, 150656, "0")
        s_xpb = at("s_xpb", [128, NCH, E], BF16, 150656, "A")
        s_rbf = at("s_rbf", [128, NCH, TT], BF16, 150656, "A")
        s_q = at("s_q", [128, NCH, E], BF16, 158912, "A")
        s_k = at("s_k", [128, NCH, E], BF16, 167168, "A")
        s_v = at("s_v", [128, NCH, E], BF16, 175424, "A")
        s_u = at("s_u", [128, NCH, E - 2], BF16, 183680, "A")
        s_u2 = nc.alloc_sbuf_tensor_at("s_u2", [128, NCH, E - 2], BF16,
                                       offset=BASE + 150656)
        s_SMb2 = nc.alloc_sbuf_tensor_at("s_SMb2", [16, 3, TT], BF16,
                                         offset=BASE + 202144)
        s_ao = at("s_ao", [128, 16, TT], BF16, 183680, "A")
        s_cb = at("s_cb", [128, E], F32, 200064, "A")
        s_t1 = at("s_t1", [128, E], F32, 202144, "A")
        s_x2 = at("s_x2", [128, TT], F32, 204224, "A")
        s_wb = [at(f"s_wb{i}", [128, TT], BF16, 206272 + 1024 * i, "A")
                for i in range(2)]
        s_wvp = [at(f"s_wvp{i}", [128, TT], BF16, 208320 + 1024 * i, "A")
                 for i in range(2)]
        s_rsq = [at(f"s_rsq{i}", [128, TT], BF16, 210368 + 1024 * i, "A")
                 for i in range(2)]

        s_s = at("s_sgate", [128, NCH, TC], F32, A12, "WF2")
        s_w1a = at("s_w1a", [128, NCH, 2048], BF16, 167040, "WF")
        s_w2a = at("s_w2a", [128, 16, D], BF16, 167040, "2")
        s_hst = [at(f"s_hst{i}", [128, TC], BF16, 199808 + 4096 * i, "F")
                 for i in range(2)]
        s_hst.append(at("s_hst2", [128, TC], BF16, 39776, "F"))
        s_gsc = [at(f"s_gsc{i}", [128, TT], F32, 199808 + 2048 * i, "2")
                 for i in range(2)]
        s_outst = at("s_outst", [128, 2, D], F32, 203904, "2")

        for i in range(len(regions)):
            for j in range(i + 1, len(regions)):
                n1, a1, b1, st1 = regions[i]
                n2, a2, b2, st2 = regions[j]
                if a1 < b2 and a2 < b1 and (st1 & st2):
                    allowed = {("s_xpb", "s_rbf"), ("s_u", "s_ao")}
                    if (n1, n2) not in allowed and (n2, n1) not in allowed:
                        raise AssertionError(f"overlap {n1} {n2}")
                if b1 > 212700 or b2 > 212700:
                    raise AssertionError("sbuf overflow")

        pf = [en(nc.psum_tensor(f"pf{i}", [128, 512], F32)) for i in range(6)]
        pb = [en(nc.psum_tensor(f"pbb{i}", [128, 1024], BF16)) for i in range(2)]

        sems = {n: en(nc.semaphore(n)) for n in _SEMS}

        def OP(eng, fn, waits=(), inc=None, inck=1):
            return sch.op(eng, fn, waits, inc, inck)

        def gfence(target=None):
            if _SIMGF:
                return OP("t", lambda En, t_=target:
                          En.matmul(t_, s_onc[0:1, 0:1], s_on1[0:1, :],
                                    start=True, stop=True), inc="sT")
            anchor = sch.tok("sT")
            return OP("t", lambda En: En.nop(cycle_cnt=700), waits=[anchor],
                      inc="sT")

        def DMA(eng, out, in_, sem, waits=(), **kw):
            return OP(eng,
                      (lambda En, o_=out, i_=in_, k_=dict(kw): En.dma_start(out=o_, in_=i_, **k_)),
                      waits=waits, inc=sem, inck=16)

        DMA("g", s_id[:], d_id, "dC")
        DMA("g", s_idb[:], d_id, "dC")
        DMA("g", s_sel[:], d_sel.rearrange("h (c d) -> h c d", c=NCH), "dC")
        DMA("g", s_ind[:], d_ind.rearrange("p (c h) -> p c h", c=NCH), "dC")
        DMA("g", s_on1[:], d_ones1, "dC")
        DMA("g", s_onc[:], d_onesc, "dC")
        DMA("g", s_vecs[:], d_vecs.rearrange("p (n c) -> p n c", n=10), "dC")
        DMA("g", s_b1f[:], d_b1f, "dC")
        DMA("g", s_mpre[:], d_mpre.rearrange("h (t j c) -> h t j c", t=NT, j=6), "dC")
        tok_consts = DMA("g", s_mpost[:], d_mpost.rearrange("h (t j c) -> h t j c", t=NT, j=6), "dC")

        def wslice(d_w, lo_chunk, n_chunk=8, cols=None):
            ap = d_w[128 * lo_chunk:128 * (lo_chunk + n_chunk), :]
            if cols is not None:
                ap = ap[:, cols[0]:cols[1]]
            return ap.rearrange("(i p) o -> p i o", p=128)

        tok_wq = [None] * (NT + 1)
        tok_wk = [None] * (NT + 1)
        tok_wv = [None] * (NT + 1)
        tok_wq[0] = DMA("g", s_B[0][:], wslice(d_wq, 0), "dB0")
        tok_wk[0] = DMA("g", s_B[1][:], wslice(d_wk, 0), "dB1")
        tok_wv[0] = DMA("g", s_B[2][:], wslice(d_wv, 0), "dB2")

        OP("v", lambda En: En.memset(s_ccols[:], 0.0))

        tok_xtok_free = [None] * NB
        tok_xht_free = [None] * NB
        tok_pb_evac = [None, None]
        tok_pc_evac = None
        tok_load = [None] * NB
        ST = s_stat

        for g0 in range(0, NB, 3):
            grp = list(range(g0, min(g0 + 3, NB)))
            g1 = grp[-1] + 1
            for b in grp:
                nb_ = 4 if b == NB - 1 else 128
                w = [tok_xtok_free[b - 3]] if b >= 3 else []
                tok_load[b] = DMA("s", s_xtok[b % 3][0:nb_, :],
                                  d_xs[128 * b:128 * b + nb_, :],
                                  f"dX{b % 3}", waits=w)
            rtoks = []
            for b in grp:
                xt = s_xtok[b % 3]
                rtoks.append(OP("v", lambda En, a=ST[:, 0, b:b + 1], i=xt[:]:
                                En.tensor_reduce(a, i, AX, ALU.add),
                                waits=[tok_load[b]], inc="sV"))
                rtoks.append(OP("v", lambda En, i=xt[:], a=ST[:, 1, b:b + 1]:
                                En.scalar_tensor_tensor(
                                    s_sq[:], i, 1.0, i, ALU.mult, ALU.mult,
                                    accum_out=a), inc="sV"))
            w1 = OP("v", lambda En, o_=ST[:, 2, g0:g1], a_=ST[:, 0, g0:g1]:
                    En.tensor_scalar_mul(o_, a_, 1.0 / D),
                    waits=rtoks, inc="sV")
            OP("v", lambda En, o_=ST[:, 3, g0:g1], a_=ST[:, 1, g0:g1]:
               En.tensor_scalar_mul(o_, a_, 1.0 / D))
            w2 = OP("v", lambda En, o_=ST[:, 4, g0:g1], a_=ST[:, 2, g0:g1]:
                    En.tensor_mul(o_, a_, a_), waits=[w1], inc="sV")
            w3 = OP("v", lambda En, o_=ST[:, 5, g0:g1], a_=ST[:, 3, g0:g1],
                    b_=ST[:, 4, g0:g1]:
                    En.tensor_sub(o_, a_, b_), waits=[w2], inc="sV")
            tA = OP("v", lambda En, o_=ST[:, 6, g0:g1], a_=ST[:, 5, g0:g1]:
                    En.tensor_scalar_add(o_, a_, 1e-5), waits=[w3], inc="sV")
            wa1 = OP("a", lambda En, o_=ST[:, 7, g0:g1], a_=ST[:, 6, g0:g1]:
                     En.activation(o_, a_, AF.Ln), waits=[tA], inc="sA")
            OP("a", lambda En, o_=ST[:, 8, g0:g1], a_=ST[:, 7, g0:g1]:
               En.activation(o_, a_, AF.Exp, scale=-0.5), waits=[wa1])
            tB = OP("a", lambda En, o_=ST[:, 9, g0:g1], a_=ST[:, 7, g0:g1]:
                    En.activation(o_, a_, AF.Exp, scale=-1.0), inc="sA")
            w4 = OP("v", lambda En, o_=ST[:, 10, g0:g1], a_=ST[:, 5, g0:g1],
                    b_=ST[:, 9, g0:g1]:
                    En.tensor_mul(o_, a_, b_), waits=[tB], inc="sV")
            tC = OP("v", lambda En, o_=ST[:, 11, g0:g1], a_=ST[:, 10, g0:g1]:
                    En.tensor_scalar_add(o_, a_, 1e-6), waits=[w4], inc="sV")
            wa2 = OP("a", lambda En, o_=ST[:, 12, g0:g1], a_=ST[:, 11, g0:g1]:
                     En.activation(o_, a_, AF.Ln), waits=[tC], inc="sA")
            tD = OP("a", lambda En, o_=ST[:, 13, g0:g1], a_=ST[:, 12, g0:g1]:
                    En.activation(o_, a_, AF.Exp, scale=-0.5),
                    waits=[wa2], inc="sA")
            w5b = OP("v", lambda En, o_=ST[:, 14, g0:g1], a_=ST[:, 2, g0:g1],
                     b_=ST[:, 8, g0:g1]:
                     En.scalar_tensor_tensor(o_, a_, -1.0, b_,
                                             ALU.mult, ALU.mult),
                     waits=[tD], inc="sV")

            for b in grp:
                nb_ = 4 if b == NB - 1 else 128
                xs_ = b % 2
                L0 = 128 * b
                xt = s_xtok[b % 3]
                wx = [w5b]
                if b >= 2:
                    wx.append(tok_xht_free[b - 2])
                tE = OP("v", lambda En, o_=s_xht[xs_][:], i=xt[:],
                        g_=ST[:, 8, b:b + 1], m_=ST[:, 14, b:b + 1]:
                        En.tensor_scalar(o_, i, g_, m_, ALU.mult, ALU.add),
                        waits=wx, inc="sV")
                tok_xtok_free[b] = tE

                wt = [tE, tok_consts]
                if tok_pb_evac[xs_] is not None:
                    wt.append(tok_pb_evac[xs_])
                for c in range(NCH):
                    OP("t", lambda En, o_=pb[xs_][:, 128 * c:128 * (c + 1)],
                       i=s_xht[xs_][:, 128 * c:128 * (c + 1)]:
                       En.transpose(o_, i, s_idb[:]), waits=wt)
                    wt = []
                wc = [tok_pc_evac] if tok_pc_evac else []
                OP("t", lambda En, i=ST[:, 13, b:b + 1]:
                   En.transpose(pf[0][0:1, 0:128], i, s_id[:]), waits=wc)
                tok_tr = OP("t", lambda En:
                            En.matmul(pf[1][0:1, 0:128], s_onc[0:1, 0:1],
                                      s_on1[0:1, :],
                                      start=True, stop=True), inc="sT")
                tok_xht_free[b] = tok_tr
                tok_pb_evac[xs_] = OP(
                    "a", lambda En, o_=s_xhat[:, :, L0:L0 + nb_],
                    i=pb[xs_][:].rearrange("p (c t) -> p c t", c=NCH)[:, :, 0:nb_]:
                    En.copy(o_, i), waits=[tok_tr], inc="sA")
                tok_pc_evac = OP(
                    "a", lambda En, o_=s_crow[0:1, L0:L0 + nb_],
                    i=pf[0][0:1, 0:nb_]:
                    En.copy(o_, i), waits=[tok_tr], inc="sA")

        sch.barrier()
        if _DBG:
            DMA("s", d_dbg_st, s_dbgst[:].rearrange("p b c -> p (b c)"), "dD")
            DMA("s", d_dbg_xhat, s_xhat[:].rearrange("p c t -> p (c t)"), "dD")
            DMA("s", d_dbg_crow, s_crow[:], "dD")

        tok_on1f = DMA("g", s_on1f[:], d_ones1, "dC2")
        tok_vb4 = DMA("g", s_vb4[:], d_vb4, "dC2")
        t_pf_free = [None] * 6
        tok_fa = [None] * NT
        tok_fb = [None] * NT
        tok_u_freed = None
        tok_xpb_free = None
        tok_ao_free = None

        for t in range(NT):
            e0 = TT * t
            qb = s_B[(0 - t) % 3]
            kb = s_B[(1 - t) % 3]
            vb = s_B[(2 - t) % 3]

            t1m = OP("t", lambda En, r_=s_crow[0:1, e0:e0 + 512]:
                     En.matmul(pf[4][:], s_on1f[:], r_, start=True, stop=True),
                     waits=[tok_consts, tok_on1f, tok_pc_evac, t_pf_free[4]], inc="sT")
            t2m = OP("t", lambda En, r_=s_crow[0:1, e0 + 512:e0 + E]:
                     En.matmul(pf[5][:, 0:4], s_on1f[:], r_, start=True, stop=True),
                     waits=[t_pf_free[5]], inc="sT")
            fcb = gfence(pf[5][32:33, 256:384])
            ce1 = OP("a", lambda En: En.copy(s_cb[:, 0:512], pf[4][:]),
                     waits=[fcb], inc="sA")
            ce2 = OP("a", lambda En: En.copy(s_cb[:, 512:E], pf[5][:, 0:4]),
                     inc="sA")
            t_pf_free[4] = ce1
            t_pf_free[5] = ce2

            tok_xpb = [None] * NCH
            for c in range(NCH):
                OP("v", lambda En, i=s_xhat[:, c, e0:e0 + E]:
                   En.tensor_tensor(s_t1[:], i, s_cb[:], ALU.mult),
                   waits=[ce1, ce2, tok_xpb_free])
                tok_xpb[c] = OP(
                    "v", lambda En, o_=s_xpb[:, c, :], g_=vec("g1")[:, c:c + 1],
                    b_=vec("b1")[:, c:c + 1]:
                    En.tensor_scalar(o_, s_t1[:], g_, b_, ALU.mult, ALU.add),
                    inc="sV")

            tok_qe = [None] * NCH
            tok_ke = [None] * NCH
            tok_ve = [None] * NCH
            evs = {"q": tok_qe, "k": tok_ke, "v": tok_ve}
            obuf = {"q": s_q, "k": s_k, "v": s_v}
            bias = {"q": "bq", "k": "bk", "v": "bv"}
            wtk = {"q": tok_wq[t], "k": tok_wk[t], "v": tok_wv[t]}
            wbf = {"q": qb, "k": kb, "v": vb}
            lastmm = {}
            qkv_pend = None

            def qkv_flush(pv, cover):
                wn_, oc_, p5_ = pv
                ev1 = OP("a", lambda En, o_=obuf[wn_][:, oc_, 0:512], p_=p5_,
                         b_=vec(bias[wn_])[:, oc_:oc_ + 1]:
                         En.activation(o_, p_[:], AF.Identity, bias=b_, scale=1.0),
                         waits=[cover], inc="sA")
                evs[wn_][oc_] = ev1
                t_pf_free[(0, 1, 4)[oc_ % 3]] = ev1
                if oc_ == NCH - 1:
                    wni = {"q": 0, "k": 1, "v": 2}[wn_]
                    ev2 = OP("a", lambda En, o_=s_t1[:, 32 * wni:32 * wni + 32],
                             p_=pf[2][:, 0:32]: En.copy(o_, p_), inc="sA")
                    t_pf_free[2] = ev2
                    OP("v", lambda En,
                       o_=obuf[wn_][:, :, 512:E],
                       a_=s_t1[:, 32 * wni:32 * wni + 32],
                       b_=s_vb4[:, wni:wni + 1, :]:
                       En.tensor_tensor(o_, a_, b_, ALU.add),
                       waits=[ev2, tok_vb4])

            for wn in ("q", "k", "v"):
                B_ = wbf[wn]
                for oc in range(NCH):
                    bmain = (0, 1, 4)[oc % 3]
                    p5 = pf[bmain]
                    w0 = [wtk[wn], t_pf_free[bmain]]
                    for ic in range(NCH):
                        mmm = OP("t", lambda En, l_=B_[:, ic, 128 * oc:128 * (oc + 1)],
                                 r_=s_xpb[:, ic, 0:512], st=(ic == 0), sp=(ic == NCH - 1),
                                 p_=p5:
                                 En.matmul(p_[:], l_, r_, start=st, stop=sp),
                                 waits=w0 + [tok_xpb[ic]],
                                 inc=("sT" if ic == 0 else None))
                        w0 = []
                        if ic == 0 and qkv_pend is not None:
                            qkv_flush(qkv_pend, mmm)
                            qkv_pend = None
                        mmt = OP("t", lambda En, l_=B_[:, ic, 128 * oc:128 * (oc + 1)],
                                 r_=s_xpb[:, ic, 512:E], st=(ic == 0), sp=(ic == NCH - 1),
                                 p_=pf[2], co=4 * oc:
                                 En.matmul(p_[:, co:co + 4], l_, r_, start=st, stop=sp),
                                 waits=([t_pf_free[2]] if ic == 0 and oc == 0 else []),
                                 inc=("sT" if ic == NCH - 1 else None))
                    qkv_pend = (wn, oc, p5)
                lastmm[wn] = mmt

            tok_fa[t] = DMA("g", qb[:], wslice(d_wfb, 0), f"dB{(0 - t) % 3}", waits=[lastmm["q"]])
            tok_fb[t] = DMA("g", kb[:], wslice(d_wfb, 8), f"dB{(1 - t) % 3}", waits=[lastmm["k"]])
            if t + 1 < NT:
                tok_wq[t + 1] = DMA("g", vb[:], wslice(d_wq, 0), f"dB{(2 - t) % 3}",
                                    waits=[lastmm["v"]])

            tok_sm = []
            prev_scores = [None, None]
            sc_pend = None
            for j in range(3):
                ub = (s_u, s_u2, s_u)[j]
                ut = [None] * NCH
                for c in range(NCH):
                    w = [tok_qe[c], tok_ke[c]]
                    if j == 0 and c == 0 and tok_u_freed is not None:
                        w.append(tok_u_freed)
                    if c == 0 and j >= 2 and prev_scores[j % 2] is not None:
                        w.append(prev_scores[j % 2])
                    ut[c] = OP("v", lambda En, o_=ub[:, c, :],
                               a_=s_q[:, c, 1:E - 1], b_=s_k[:, c, 2 - j:E - j]:
                               En.tensor_tensor(o_, a_, b_, ALU.mult),
                               waits=w, inc="sV")
                wF = [t_pf_free[4]]
                for c in range(NCH):
                    mmF = OP("t", lambda En, l_=s_ind[:, c, :], r_=ub[:, c, 0:512],
                             st=(c == 0), sp=(c == NCH - 1):
                             En.matmul(pf[4][0:16, :], l_, r_, start=st, stop=sp),
                             waits=wF + [ut[c]],
                             inc=("sT" if c in (0, NCH - 1) else None))
                    wF = []
                    if c == 0:
                        if qkv_pend is not None:
                            qkv_flush(qkv_pend, mmF)
                            qkv_pend = None
                        if sc_pend is not None:
                            e2 = OP("a", lambda En, jp=sc_pend,
                                    o_=s_SM[:, 3 + (2 - sc_pend), :]:
                                    En.copy(o_, pf[5][0:16, :]),
                                    waits=[mmF], inc="sA")
                            t_pf_free[5] = e2
                            tok_sm.append(e2)
                            sc_pend = None
                wR = [t_pf_free[5]]
                mmR0 = None
                for c in range(NCH):
                    mmR = OP("t", lambda En, l_=s_ind[:, c, :], r_=ub[:, c, 2:514],
                             st=(c == 0), sp=(c == NCH - 1):
                             En.matmul(pf[5][0:16, :], l_, r_, start=st, stop=sp),
                             waits=wR,
                             inc=("sT" if c in (0, NCH - 1) else None))
                    wR = []
                    if c == 0:
                        mmR0 = mmR
                e1 = OP("a", lambda En, o_=s_SM[:, j, :]:
                        En.copy(o_, pf[4][0:16, :]), waits=[mmR0], inc="sA")
                t_pf_free[4] = e1
                tok_sm.append(e1)
                sc_pend = j
                prev_scores[j % 2] = mmR
            fsc = gfence(pf[4][32:33, 256:384])
            e2 = OP("a", lambda En, o_=s_SM[:, 3 + (2 - 2), :]:
                    En.copy(o_, pf[5][0:16, :]), waits=[fsc], inc="sA")
            t_pf_free[5] = e2
            tok_sm.append(e2)
            sc_pend = None
            tok_u_read = sch.tok("sT")

            OP("v", lambda En, m_=s_mpre[:, t, :, 0:8]:
               En.tensor_tensor(s_SM[:, :, 0:8], s_SM[:, :, 0:8], m_, ALU.mult),
               waits=tok_sm)
            OP("v", lambda En, m_=s_mpre[:, t, :, 8:16]:
               En.tensor_tensor(s_SM[:, :, TT - 8:TT], s_SM[:, :, TT - 8:TT], m_, ALU.mult))

            tok_ao = [None] * 16
            par = 0
            wb_read = [None, None, None, None]
            tok_prev_wv = None
            wvbbuf = (s_wb[0], s_wb[1], s_rsq[0], s_rsq[1])

            def wv_flush(pv, cover_tok):
                i_, pw_, c_, j_, dd_ = pv
                wbb = wvbbuf[i_ % 4]
                ewb = OP("a", lambda En, o_=wbb[:], p_=pw_:
                         En.copy(o_, p_[:]),
                         waits=[cover_tok, wb_read[i_ % 4]], inc="sA")
                t_pf_free[2 + i_ % 4] = ewb
                vs = (2 - j_) if dd_ == 0 else (2 + j_)
                if j_ == 0:
                    tok_ao[8 * dd_ + c_] = OP(
                        "v", lambda En, o_=s_ao[:, 8 * dd_ + c_, :],
                        a_=wbb[:], b_=s_v[:, c_, vs:vs + 512]:
                        En.tensor_mul(o_, a_, b_),
                        waits=[ewb, tok_ve[c_]], inc="sV")
                    wb_read[i_ % 4] = tok_ao[8 * dd_ + c_]
                else:
                    wb_read[i_ % 4] = OP(
                        "v", lambda En, o_=s_wvp[i_ % 2][:],
                        a_=wbb[:], b_=s_v[:, c_, vs:vs + 512]:
                        En.tensor_mul(o_, a_, b_), waits=[ewb], inc="sV")
                    tok_ao[8 * dd_ + c_] = OP(
                        "v", lambda En, o_=s_ao[:, 8 * dd_ + c_, :],
                        a_=s_wvp[i_ % 2][:]:
                        En.tensor_add(o_, o_, a_), inc="sV")
            wv_pend = None
            tok_wf = [None, None]
            for dd in range(2):
                SMB = (s_SMb, s_SMb2)[dd]
                d0 = 3 * dd
                OP("v", lambda En, a_=s_SM[:, d0, :], b_=s_SM[:, d0 + 1, :]:
                   En.tensor_max(s_smxm[:], a_, b_))
                OP("v", lambda En, a_=s_SM[:, d0 + 2, :]:
                   En.tensor_max(s_smxm[:], s_smxm[:], a_))
                for j in range(3):
                    tsb = OP("v", lambda En, a_=s_SM[:, d0 + j, :]:
                             En.tensor_sub(a_, a_, s_smxm[:]),
                             inc=("sV" if j == 2 else None))
                tex = OP("a", lambda En, a_=s_SM[:, d0:d0 + 3, :]:
                         En.activation(a_, a_, AF.Exp),
                         waits=[tsb], inc="sA")
                OP("v", lambda En, a_=s_SM[:, d0, :], b_=s_SM[:, d0 + 1, :]:
                   En.tensor_add(s_smxr[:], a_, b_), waits=[tex])
                tsum = OP("v", lambda En, a_=s_SM[:, d0 + 2, :]:
                          En.tensor_add(s_smxr[:], s_smxr[:], a_), inc="sV")
                OP("a", lambda En: En.activation(s_smxm[:], s_smxr[:], AF.Ln),
                   waits=[tsum])
                trc = OP("a", lambda En: En.activation(s_smxr[:], s_smxm[:], AF.Exp, scale=-1.0),
                         inc="sA")
                for j in range(3):
                    OP("v", lambda En, a_=s_SM[:, d0 + j, :], o_=SMB[:, j, :]:
                       En.tensor_mul(o_, a_, s_smxr[:]),
                       waits=([trc, tok_prev_wv] if j == 0 else []))
                OP("v", lambda En, m_=s_mpre[0:16, t, d0:d0 + 3, 0:8],
                   o_=SMB[:, :, 0:8], mm_=s_mpost[:, t, d0:d0 + 3, 0:8]:
                   En.tensor_tensor(o_, o_, mm_, ALU.mult))
                tok_wf[dd] = OP("v", lambda En, o_=SMB[:, :, TT - 8:TT],
                                m_=s_mpost[:, t, d0:d0 + 3, 8:16]:
                                En.tensor_tensor(o_, o_, m_, ALU.mult),
                                inc="sV")

            for dd in range(2):
                SMB = (s_SMb, s_SMb2)[dd]
                wvw = [tok_wf[dd], tok_u_read]
                if dd == 0 and tok_ao_free is not None:
                    wvw.append(tok_ao_free)
                for c in range(NCH):
                    for j in range(3):
                        pw = pf[2 + par % 4]
                        mmW = OP("t", lambda En, l_=s_sel[:, c, :], r_=SMB[:, j, :],
                                 p_=pw:
                                 En.matmul(p_[:], l_, r_, start=True, stop=True),
                                 waits=wvw + [t_pf_free[2 + par % 4]], inc="sT")
                        wvw = []
                        if wv_pend is not None:
                            wv_flush(wv_pend, mmW)
                        wv_pend = (par, pw, c, j, dd)
                        par += 1
            tok_prev_wv = sch.tok("sT")

            tok_rbf = [None] * NCH
            tok_rsq = [None] * NCH
            stat_mm2 = [None] * NCH
            def emit_stats(so):
                wS = [t_pf_free[4], tok_rbf[so]] if so == 0 else [tok_rbf[so]]
                mS = OP("t", lambda En, r_=s_rbf[:, so, :], st=(so == 0),
                        sp=(so == NCH - 1):
                        En.matmul(pf[4][0:1, :], s_onc[:], r_, start=st, stop=sp),
                        waits=wS, inc=("sT" if so == NCH - 1 else None))
                wS2 = [t_pf_free[5], tok_rsq[so]] if so == 0 else [tok_rsq[so]]
                stat_mm2[so] = mS2 = OP(
                    "t", lambda En, r_=s_rsq[so % 2][:], st=(so == 0),
                    sp=(so == NCH - 1):
                    En.matmul(pf[5][0:1, :], s_onc[:], r_, start=st, stop=sp),
                    waits=wS2, inc="sT")
                return mS, mS2

            for oc in range(NCH):
                p5 = pf[oc % 2]
                w0 = [tok_fa[t], tok_fb[t], t_pf_free[oc % 2]]
                for ic in range(16):
                    B_ = qb if ic < 8 else kb
                    mmt = OP("t", lambda En, l_=B_[:, ic % 8, 128 * oc:128 * (oc + 1)],
                             r_=s_ao[:, ic, :], st=(ic == 0), sp=(ic == 15), p_=p5:
                             En.matmul(p_[:], l_, r_, start=st, stop=sp),
                             waits=w0 + [tok_ao[ic]],
                             inc=("sT" if ic == 15 else None))
                    w0 = []
                    if wv_pend is not None:
                        wv_flush(wv_pend, mmt)
                        wv_pend = None
                if oc >= 1:
                    emit_stats(oc - 1)
                    cov = stat_mm2[oc - 1]
                else:
                    cov = gfence(pf[4][32:33, 256:384])
                OP("v", lambda En, i=s_xhat[:, oc, e0 + 2:e0 + 2 + TT]:
                   En.tensor_tensor(s_x2[:], i, s_cb[:, 2:2 + TT], ALU.mult),
                   waits=[cov])
                OP("v", lambda En, g_=vec("g1")[:, oc:oc + 1], b_=vec("b1")[:, oc:oc + 1]:
                   En.tensor_scalar(s_x2[:], s_x2[:], g_, b_, ALU.mult, ALU.add))
                tok_rbf[oc] = OP(
                    "v", lambda En, o_=s_rbf[:, oc, :], p_=p5, b_=vec("bfb")[:, oc:oc + 1]:
                    En.scalar_tensor_tensor(o_, p_[:], b_, s_x2[:], ALU.add, ALU.add),
                    inc="sV")
                t_pf_free[oc % 2] = tok_rbf[oc]
                tok_rsq[oc] = OP(
                    "v", lambda En, o_=s_rsq[oc % 2][:], a_=s_rbf[:, oc, :]:
                    En.tensor_mul(o_, a_, a_),
                    waits=[stat_mm2[oc - 2]] if oc >= 2 else [], inc="sV")
                if oc == NCH - 1:
                    emit_stats(oc)
            rowm = s_smxm[0:1, :]
            rowe = s_smxr[0:1, :]
            rowt = s_SM[0:1, 0, :]
            fst = gfence(pf[5][32:33, 256:384])
            OP("v", lambda En: En.tensor_scalar_mul(rowm, pf[4][0:1, :], 1.0 / D),
               waits=[fst])
            OP("v", lambda En: En.tensor_scalar_mul(rowe, pf[5][0:1, :], 1.0 / D))
            OP("v", lambda En: En.tensor_mul(rowt, rowm, rowm))
            OP("v", lambda En: En.tensor_sub(rowe, rowe, rowt))
            tve = OP("v", lambda En: En.tensor_scalar_add(rowe, rowe, 1e-6), inc="sV")
            t_pf_free[4] = tve
            t_pf_free[5] = tve
            OP("a", lambda En: En.activation(rowt, rowe, AF.Ln), waits=[tve])
            trs = OP("a", lambda En: En.activation(rowe, rowt, AF.Exp, scale=-0.5),
                     inc="sA")
            tnm = OP("v", lambda En: En.tensor_scalar_mul(rowm, rowm, -1.0), inc="sV")
            bc1 = OP("t", lambda En: En.matmul(pf[2][:], s_on1f[:], rowm, start=True, stop=True),
                     waits=[tnm, t_pf_free[2]], inc="sT")
            bc2 = OP("t", lambda En: En.matmul(pf[3][:], s_on1f[:], rowe, start=True, stop=True),
                     waits=[trs, t_pf_free[3]], inc="sT")
            fbc = gfence(pf[5][32:33, 256:384])
            th2 = None
            for oc in range(NCH):
                OP("v", lambda En, a_=s_rbf[:, oc, :]:
                   En.tensor_tensor(s_x2[:], a_, pf[2][:], ALU.add),
                   waits=([fbc] if oc == 0 else []))
                OP("v", lambda En: En.tensor_tensor(s_x2[:], s_x2[:], pf[3][:], ALU.mult))
                th2 = OP("v", lambda En, o_=s_h2[:, oc, e0:e0 + TT],
                         g_=vec("g2")[:, oc:oc + 1], b_=vec("b2")[:, oc:oc + 1]:
                         En.tensor_scalar(o_, s_x2[:], g_, b_, ALU.mult, ALU.add),
                         inc="sV")
            t_pf_free[2] = th2
            t_pf_free[3] = th2

            tok_u_freed = tok_u_read
            tok_u_freed = sch.tok("sT")
            tok_xpb_free = sch.tok("sT")
            tok_ao_free = sch.tok("sT")
            if t + 1 < NT:
                tok_wk[t + 1] = DMA("g", qb[:], wslice(d_wk, 0), f"dB{(0 - t) % 3}",
                                    waits=[sch.tok("sT")])
                tok_wv[t + 1] = DMA("g", kb[:], wslice(d_wv, 0), f"dB{(1 - t) % 3}",
                                    waits=[sch.tok("sT")])

        if _DBG:
            tD = DMA("s", d_dbg_h2, s_h2[:].rearrange("p c t -> p (c t)"), "dD",
                     waits=[sch.tok("sV")])
            OP("s", lambda En: En.nop(), waits=[tD])
        sch.barrier()

        tok_wo = DMA("g", s_wo[:], wslice(d_wo, 0), "dWo")
        tok_w1a = DMA("g", s_w1a[:], d_w1[:, 0:2048].rearrange("(i p) o -> p i o", p=128),
                      "dW1a")
        pf_free = [None] * 6
        a3_pend = None

        def a3_flush(po, cover):
            pf_free[NT - 1] = OP(
                "a", lambda En, o_=s_s[:, po, 512 * (NT - 1):512 * NT],
                b_=vec("bo")[:, po:po + 1], p_=pf[NT - 1]:
                En.activation(o_, p_[:], AF.Gelu, bias=b_, scale=1.0),
                waits=[cover], inc="sA")

        for oc in range(NCH):
            lastm = [None] * NT
            w0 = [tok_wo]
            for ic in range(NCH):
                for t4 in range(NT):
                    lastm[t4] = OP(
                        "t", lambda En, l_=s_wo[:, ic, 128 * oc:128 * (oc + 1)],
                        r_=s_h2[:, ic, 512 * t4:512 * (t4 + 1)], st=(ic == 0),
                        sp=(ic == NCH - 1), p_=pf[t4]:
                        En.matmul(p_[:], l_, r_, start=st, stop=sp),
                        waits=w0 + ([pf_free[t4]] if ic == 0 else []),
                        inc=("sT" if ic == NCH - 1 or (ic == 0 and t4 == 0)
                             else None))
                    w0 = []
                    if a3_pend is not None:
                        a3_flush(a3_pend, lastm[0])
                        a3_pend = None
            for t4 in range(NT - 1):
                pf_free[t4] = OP(
                    "a", lambda En, o_=s_s[:, oc, 512 * t4:512 * (t4 + 1)],
                    b_=vec("bo")[:, oc:oc + 1], p_=pf[t4]:
                    En.activation(o_, p_[:], AF.Gelu, bias=b_, scale=1.0),
                    waits=[lastm[t4 + 1]], inc="sA")
            a3_pend = oc
        a3_flush(a3_pend, gfence(pf[4][0:1, 0:128]))
        a3_pend = None

        if _DBG:
            tD = DMA("s", d_dbg_s, s_s[:].rearrange("p c t -> p (c t)"), "dD",
                     waits=[sch.tok("sA")])
            OP("s", lambda En: En.nop(), waits=[tD])
        sch.barrier()

        tok_w1b = DMA("g", s_w1b[:], d_w1[:, 2048:4096].rearrange("(i p) o -> p i o", p=128),
                      "dW1b")
        pf_free = [None] * 6
        tok_hdma = [None] * 32
        tok_w2a = None
        f1_pend = None
        hA_toks = []

        def f1_flush(po, cover):
            pf_free[NT - 1] = OP(
                "a", lambda En, o_=s_hst[po % 3][:, 512 * (NT - 1):512 * NT],
                b_=s_b1f[:, po:po + 1], p_=pf[NT - 1]:
                En.activation(o_, p_[:], AF.Relu, bias=b_, scale=1.0),
                waits=[cover], inc="sA")
            tok_hdma[po] = DMA("a", d_h[po], s_hst[po % 3][:], f"dH{po % 4}",
                               waits=[pf_free[t4] for t4 in range(NT)])
            if po == 15:
                hA_toks.extend(sch.tok(f"dH{i}") for i in range(4))

        for oc in range(32):
            wb_ = s_w1a if oc < 16 else s_w1b
            wtok = tok_w1a if oc < 16 else tok_w1b
            col = 128 * (oc % 16)
            lastm = [None] * NT
            w0 = [wtok]
            if oc >= 3:
                w0.append(tok_hdma[oc - 3])
            for ic in range(NCH):
                for t4 in range(NT):
                    lastm[t4] = OP(
                        "t", lambda En, l_=wb_[:, ic, col:col + 128],
                        r_=s_xhat[:, ic, 2 + 512 * t4:2 + 512 * (t4 + 1)],
                        st=(ic == 0), sp=(ic == NCH - 1), p_=pf[t4]:
                        En.matmul(p_[:], l_, r_, start=st, stop=sp),
                        waits=w0 + ([pf_free[t4]] if ic == 0 else []),
                        inc=("sT" if ic == NCH - 1 or (ic == 0 and t4 == 0)
                             else None))
                    w0 = []
                    if f1_pend is not None:
                        f1_flush(f1_pend, lastm[0])
                        f1_pend = None
            for t4 in range(NT - 1):
                pf_free[t4] = OP(
                    "a", lambda En, o_=s_hst[oc % 3][:, 512 * t4:512 * (t4 + 1)],
                    b_=s_b1f[:, oc:oc + 1], p_=pf[t4]:
                    En.activation(o_, p_[:], AF.Relu, bias=b_, scale=1.0),
                    waits=[lastm[t4 + 1]], inc="sA")
            f1_pend = oc
            if oc == 15:
                tok_w2a = DMA("g", s_w2a[:],
                              d_w2[0:2048, :].rearrange("(i p) o -> p i o", p=128),
                              "dW2a", waits=[lastm[NT - 1]])
        f1_flush(f1_pend, gfence(pf[4][0:1, 0:128]))
        f1_pend = None

        sch.barrier()

        tok_w2b = DMA("g", s_w2b[:],
                      d_w2[2048:4096, :].rearrange("(i p) o -> p i o", p=128), "dW2b")
        all_h = [sch.tok(f"dH{i}") for i in range(4)]
        pf_free = [None] * 6
        pb_free = [None, None]
        tok_hinA_free = None
        tok_hinB_free = None
        tok_xre_free = None
        tok_out_dma = None
        out_dma_bi = [None, None]
        for t in range(NT):
            tsl = (512 * t, 512 * (t + 1))
            thA_c = []
            thB_c = []
            for ci in range(4):
                thA_c.append(DMA(
                    "s", s_hinA[:, 4 * ci:4 * ci + 4, :],
                    d_h[4 * ci:4 * ci + 4, :, tsl[0]:tsl[1]].rearrange("i p t -> p i t"),
                    f"dHA{ci}",
                    waits=((hA_toks if t == 0 else []) + [tok_hinA_free])
                    if ci == 0 else []))
            for ci in range(4):
                thB_c.append(DMA(
                    "s", s_hinB[:, 4 * ci:4 * ci + 4, :],
                    d_h[16 + 4 * ci:20 + 4 * ci, :, tsl[0]:tsl[1]].rearrange("i p t -> p i t"),
                    f"dHB{ci}",
                    waits=[tok_hinB_free] if ci == 0 else []))
            txr = DMA("s", s_xre[:],
                      d_xs[2 + tsl[0]:2 + tsl[1], :].rearrange("(b p) d -> p b d", p=128),
                      "dXR", waits=[tok_xre_free])
            tok_sg = [None] * NCH
            f2_pend = None

            def f2_flush(pv, cover):
                po, pp = pv
                tgs = OP("a", lambda En, o_=s_gsc[po % 2][:], p_=pp,
                         b_=vec("b2f")[:, po:po + 1]:
                         En.activation(o_, p_[:], AF.Silu, bias=b_, scale=1.0),
                         waits=[cover], inc="sA")
                tok_sg[po] = OP(
                    "v", lambda En, o_=s_sg[:, po, :], a_=s_gsc[po % 2][:],
                    b_=s_s[:, po, tsl[0]:tsl[1]]:
                    En.tensor_mul(o_, a_, b_), waits=[tgs], inc="sV")
                pf_free[po % 2] = tok_sg[po]

            for oc in range(NCH):
                p5 = pf[oc % 2]
                w0 = [tok_w2a, tok_w2b, pf_free[oc % 2]]
                for ic in range(32):
                    wb_ = s_w2a if ic < 16 else s_w2b
                    hb_ = s_hinA if ic < 16 else s_hinB
                    if oc == 0:
                        w0.append((thA_c if ic < 16 else thB_c)[(ic % 16) // 4])
                    mmt = OP("t", lambda En, l_=wb_[:, ic % 16, 128 * oc:128 * (oc + 1)],
                             r_=hb_[:, ic % 16, :], st=(ic == 0), sp=(ic == 31), p_=p5:
                             En.matmul(p_[:], l_, r_, start=st, stop=sp),
                             waits=w0, inc=("sT" if ic in (0, 31) else None))
                    w0 = []
                    if f2_pend is not None:
                        f2_flush(f2_pend, mmt)
                        f2_pend = None
                f2_pend = (oc, p5)
            f2_flush(f2_pend, gfence(pf[3][0:1, 0:128]))
            f2_pend = None
            tok_hinA_free = sch.tok("sT")
            tok_hinB_free = sch.tok("sT")
            for half in range(2):
                touts = []
                for bi in range(2):
                    b4 = 2 * half + bi
                    wv0 = [tok_sg[NCH - 1], pb_free[bi]]
                    if half == 0 and out_dma_bi[bi] is not None:
                        wv0.append(out_dma_bi[bi])
                    for oc in range(NCH):
                        OP("t", lambda En, o_=pb[bi][:, 128 * oc:128 * (oc + 1)],
                           i=s_sg[:, oc, 128 * b4:128 * (b4 + 1)]:
                           En.transpose(o_, i, s_idb[:]), waits=wv0)
                        wv0 = []
                    ttr = OP("t", lambda En:
                             En.matmul(pf[2][0:1, 0:128], s_onc[0:1, 0:1], s_on1[0:1, :],
                                       start=True, stop=True), inc="sT")
                    wadd = [ttr, txr]
                    if out_dma_bi[bi] is not None:
                        wadd.append(out_dma_bi[bi])
                    if t == NT - 1 and half == 1:
                        OP("v", lambda En: En.nop() if _SIMGF else
                           En.nop(cycle_cnt=3000), waits=[ttr])
                    touts.append(OP(
                        "v", lambda En, o_=s_outst[:, bi, :], p_=pb[bi],
                        x_=s_xre[:, b4, :]:
                        En.tensor_tensor(o_, p_[:], x_, ALU.add),
                        waits=wadd, inc="sV"))
                    pb_free[bi] = touts[-1]
                    tcm = OP("v", lambda En: En.nop() if _SIMGF else
                             En.nop(cycle_cnt=700), waits=[touts[-1]],
                             inc="sV")
                    r0 = tsl[0] + 256 * half + 128 * bi
                    out_dma_bi[bi] = DMA(
                        "a", d_y[r0:r0 + 128, :],
                        s_outst[:, bi, :], ("dO", "dO1")[bi], waits=[tcm])
                tok_out_dma = out_dma_bi[1]
            tok_xre_free = sch.tok("sV")

        if _DBG3:
            DMA("s", d_dbg_sg, s_sg[:].rearrange("p c t -> p (c t)"), "dD",
                waits=[sch.tok("sV")])
            DMA("s", d_dbg_gsc[:, 0:TT], s_gsc[0][:], "dD")
            DMA("s", d_dbg_gsc[:, TT:2 * TT], s_gsc[1][:], "dD")
            tD3 = DMA("s", d_dbg_s2, s_s[:].rearrange("p c t -> p (c t)"), "dD")
            OP("s", lambda En: En.nop(), waits=[tD3])
        OP("a", lambda En: En.nop(), waits=[sch.tok("dO"), sch.tok("dO1")])
        OP("s", lambda En: En.nop(),
           waits=[sch.tok("dO"), sch.tok("dO1"), sch.tok("dD")])

        with nc.Block() as blk:
            def runner(key):
                def body(Eng):
                    for needed, fn, inc, inck in sch.q[key]:
                        for sname, val in needed:
                            Eng.wait_ge(sems[sname], val)
                        ins = fn(Eng)
                        if inc is not None:
                            ins.then_inc(sems[inc], inck)
                return body
            blk.tensor(runner("t"))
            blk.vector(runner("v"))
            blk.scalar(runner("a"))
            blk.gpsimd(runner("g"))
            blk.sync(runner("s"))

    return nc


_NC_CACHE = None


def _get_nc():
    global _NC_CACHE
    if _NC_CACHE is None:
        _NC_CACHE = build_program()
    return _NC_CACHE


def _host_consts():
    f = np.float32
    idm = np.eye(128, dtype=f)
    sel = np.zeros((16, NCH, 128), f)
    ind = np.zeros((128, NCH, 16), f)
    for c in range(NCH):
        for dd in range(128):
            h = 2 * c + (1 if dd >= 64 else 0)
            sel[h, c, dd] = 1.0
            ind[dd, c, h] = 0.125
    ones1 = np.ones((1, 128), f)
    onesc = np.ones((128, 1), f)
    return idm, sel.reshape(16, NCH * 128), ind.reshape(128, NCH * 16), ones1, onesc


def _masks_for_core(t0):
    f = np.float32
    mpre = np.ones((16, NT, 6, 16), f)
    mpost = np.ones((16, NT, 6, 16), f)
    for t in range(NT):
        for col in range(16):
            if col < 8:
                g = t0 + 512 * t + col
            else:
                g = t0 + 512 * t + 504 + (col - 8)
            for j in range(3):
                if not (g >= max(1, j)):
                    mpre[:, t, j, col] = 0.0
                if not (g >= j):
                    mpost[:, t, j, col] = 0.0
                if not (g <= min(SEQ - 2, SEQ - 1 - j)):
                    mpre[:, t, 3 + j, col] = 0.0
                if not (g <= SEQ - 1 - j):
                    mpost[:, t, 3 + j, col] = 0.0
    return mpre.reshape(16, NT * 6 * 16), mpost.reshape(16, NT * 6 * 16)


def kernel(L0_attn_input_feature, L0_attn_Wq, L0_attn_bq, L0_attn_Wk, L0_attn_bk,
           L0_attn_Wv, L0_attn_bv, L0_attn_Wfb, L0_attn_bfb, L0_attn_Wo,
           L0_attn_bo, L0_attn_g1, L0_attn_b1, L0_attn_g2, L0_attn_b2,
           L0_ffn_W1, L0_ffn_b1, L0_ffn_W2, L0_ffn_b2):
    f = np.float32
    x = np.ascontiguousarray(np.asarray(L0_attn_input_feature, f))
    W = {k: np.ascontiguousarray(np.asarray(v, f)) for k, v in dict(
        Wq=L0_attn_Wq, Wk=L0_attn_Wk, Wv=L0_attn_Wv, Wfb=L0_attn_Wfb,
        Wo=L0_attn_Wo, W1=L0_ffn_W1, W2=L0_ffn_W2).items()}
    vl = [L0_attn_g1, L0_attn_b1, L0_attn_g2, L0_attn_b2, L0_attn_bq,
          L0_attn_bk, L0_attn_bv, L0_attn_bfb, L0_attn_bo, L0_ffn_b2]
    vecs = np.stack([np.asarray(v, f).reshape(NCH, 128).T for v in vl], axis=1)
    vecs = np.ascontiguousarray(vecs.reshape(128, 80))
    b1f = np.ascontiguousarray(np.asarray(L0_ffn_b1, f).reshape(32, 128).T)
    vb4 = np.stack([np.repeat(np.asarray(v, f).reshape(NCH, 128).T[:, :, None],
                              4, axis=2)
                    for v in (L0_attn_bq, L0_attn_bk, L0_attn_bv)], axis=1)
    vb4 = np.ascontiguousarray(vb4.reshape(128, 3, 32))
    idm, sel, ind, ones1, onesc = _host_consts()

    zeros2 = np.zeros((2, D), f)
    in_maps = []
    for core in range(8):
        b, half = divmod(core, 2)
        if half == 0:
            xs = np.concatenate([zeros2, x[b, 0:TC + 2]], axis=0)
        else:
            xs = np.concatenate([x[b, TC - 2:2 * TC], zeros2], axis=0)
        mpre, mpost = _masks_for_core(half * TC)
        in_maps.append(dict(
            xs=np.ascontiguousarray(xs), vecs=vecs, b1f=b1f, vb4=vb4, idm=idm,
            sel=sel, ind=ind, ones1=ones1, onesc=onesc, mpre=mpre, mpost=mpost,
            **W))

    nc = _get_nc()
    res = run_bass_kernel_spmd(nc, in_maps, core_ids=list(range(8)))
    if _DBG or _DBG3:
        global _DBG_RES
        _DBG_RES = res.results
    ys = [res.results[i]["y"] for i in range(8)]
    out = np.stack([np.concatenate([ys[2 * b], ys[2 * b + 1]], axis=0)
                    for b in range(4)])
    return np.ascontiguousarray(out.astype(np.float32))


# version 16
# speedup vs baseline: 1.0625x; 1.0050x over previous
import numpy as np
import ml_dtypes
from contextlib import ExitStack

import concourse.bass as bass
import concourse.mybir as mybir
from concourse.bass_utils import run_bass_kernel_spmd

F32 = mybir.dt.float32
BF16 = mybir.dt.bfloat16
AF = mybir.ActivationFunctionType
ALU = mybir.AluOpType
AX = mybir.AxisListType.X

D = 1024
NCH = 8
TC = 2048
NL = 2052
TT = 512
NT = TC // TT
E = TT + 4
NB = 17
SEQ = 4096

_DBG = False
_SIMGF = False
_DBG3 = False

_SEMS = (["dD", "dX0", "dX1", "dX2", "dHA", "dHB", "dXR", "dB0", "dB1", "dB2",
          "dC", "dC2", "dWo", "dW1a", "dW1b", "dW2a", "dW2b", "dH0", "dH1",
          "dH2", "dH3", "dHA0", "dHA1", "dHA2", "dHA3", "dHB0", "dHB1",
          "dHB2", "dHB3", "dW2b0", "dW2b1", "dW2b2", "dW2b3",
          "dO", "dO1", "sT", "sV", "sA", "sG", "sS", "bt", "bv", "ba", "bg",
          "bs"])


class Sch:
    ENG = "tvags"

    def __init__(self):
        self.q = {k: [] for k in self.ENG}
        self.cnt = {}
        self.lastw = {k: {} for k in self.ENG}

    def tok(self, sem):
        return (sem, self.cnt.get(sem, 0))

    def op(self, eng, fn, waits=(), inc=None, inck=1):
        lw = self.lastw[eng]
        needed = []
        for w in waits:
            if w is None:
                continue
            sname, val = w
            if val > 0 and lw.get(sname, 0) < val:
                lw[sname] = val
                needed.append((sname, val))
        out_tok = None
        if inc is not None:
            self.cnt[inc] = self.cnt.get(inc, 0) + inck
            out_tok = (inc, self.cnt[inc])
        self.q[eng].append((needed, fn, inc, inck))
        return out_tok

    def barrier(self):
        toks = {}
        for e in self.ENG:
            self.op(e, lambda E: E.drain())
            toks[e] = self.op(e, lambda E: E.nop(), inc="b" + e)
        for e in self.ENG:
            self.op(e, lambda E: E.nop(),
                    waits=[toks[o] for o in self.ENG if o != e])


def build_program():
    nc = bass.Bass("TRN2", target_bir_lowering=False, detect_race_conditions=False)

    d_xs = nc.dram_tensor("xs", [NL, D], F32, kind="ExternalInput").ap()
    d_wq = nc.dram_tensor("Wq", [D, D], F32, kind="ExternalInput").ap()
    d_wk = nc.dram_tensor("Wk", [D, D], F32, kind="ExternalInput").ap()
    d_wv = nc.dram_tensor("Wv", [D, D], F32, kind="ExternalInput").ap()
    d_wfb = nc.dram_tensor("Wfb", [2 * D, D], F32, kind="ExternalInput").ap()
    d_wo = nc.dram_tensor("Wo", [D, D], F32, kind="ExternalInput").ap()
    d_w1 = nc.dram_tensor("W1", [D, 4 * D], F32, kind="ExternalInput").ap()
    d_w2 = nc.dram_tensor("W2", [4 * D, D], F32, kind="ExternalInput").ap()
    d_vecs = nc.dram_tensor("vecs", [128, 80], F32, kind="ExternalInput").ap()
    d_b1f = nc.dram_tensor("b1f", [128, 32], F32, kind="ExternalInput").ap()
    d_vb4 = nc.dram_tensor("vb4", [128, 3, 32], F32, kind="ExternalInput").ap()
    d_id = nc.dram_tensor("idm", [128, 128], F32, kind="ExternalInput").ap()
    d_sel = nc.dram_tensor("sel", [16, NCH * 128], F32, kind="ExternalInput").ap()
    d_ind = nc.dram_tensor("ind", [128, NCH * 16], F32, kind="ExternalInput").ap()
    d_ones1 = nc.dram_tensor("ones1", [1, 128], F32, kind="ExternalInput").ap()
    d_onesc = nc.dram_tensor("onesc", [128, 1], F32, kind="ExternalInput").ap()
    d_mpre = nc.dram_tensor("mpre", [16, NT * 6 * 16], F32, kind="ExternalInput").ap()
    d_mpost = nc.dram_tensor("mpost", [16, NT * 6 * 16], F32, kind="ExternalInput").ap()
    d_y = nc.dram_tensor("y", [TC, D], F32, kind="ExternalOutput").ap()
    d_h = nc.dram_tensor("hscr", [32, 128, TC], BF16, kind="Internal").ap()
    if _DBG3:
        d_dbg_sg = nc.dram_tensor("dbg_sg", [128, NCH * TT], BF16, kind="ExternalOutput").ap()
        d_dbg_gsc = nc.dram_tensor("dbg_gsc", [128, 2 * TT], F32, kind="ExternalOutput").ap()
        d_dbg_s2 = nc.dram_tensor("dbg_s2", [128, NCH * TC], F32, kind="ExternalOutput").ap()
    if _DBG:
        d_dbg_st = nc.dram_tensor("dbg_st", [128, NB * 16], F32, kind="ExternalOutput").ap()
        d_dbg_xhat = nc.dram_tensor("dbg_xhat", [128, NCH * NL], BF16, kind="ExternalOutput").ap()
        d_dbg_crow = nc.dram_tensor("dbg_crow", [1, NL], F32, kind="ExternalOutput").ap()
        d_dbg_h2 = nc.dram_tensor("dbg_h2", [128, NCH * TC], BF16, kind="ExternalOutput").ap()
        d_dbg_s = nc.dram_tensor("dbg_s", [128, NCH * TC], F32, kind="ExternalOutput").ap()

    sch = Sch()

    with ExitStack() as ctx:
        en = ctx.enter_context

        regions = []

        BASE = 16512

        def at(name, shape, dtype, off, stages):
            t = nc.alloc_sbuf_tensor_at(name, shape, dtype, offset=BASE + off)
            nbytes = int(np.prod(shape[1:])) * mybir.dt.size(dtype)
            regions.append((name, off, off + nbytes, frozenset(stages)))
            return t

        ALL = "0AWF2"

        s_id = at("c_id", [128, 128], F32, 0, ALL)
        s_idb = at("c_idb", [128, 128], BF16, 512, ALL)
        s_sel = at("c_sel", [16, NCH, 128], BF16, 768, ALL)
        s_ind = at("c_ind", [128, NCH, 16], BF16, 2816, ALL)
        s_on1 = at("c_on1", [1, 128], BF16, 3072, ALL)
        s_onc = at("c_onc", [128, 1], BF16, 3328, ALL)
        s_vecs = at("c_vecs", [128, 10, 8], F32, 3360, ALL)
        s_b1f = at("c_b1f", [128, 32], F32, 3680, ALL)
        s_mpre = at("c_mpre", [16, NT, 6, 16], F32, 3808, ALL)
        s_mpost = at("c_mpost", [16, NT, 6, 16], F32, 5344, ALL)
        VI = {n: i for i, n in enumerate(
            ["g1", "b1", "g2", "b2", "bq", "bk", "bv", "bfb", "bo", "b2f"])}

        def vec(n):
            return s_vecs[:, VI[n], :]

        s_xhat = at("s_xhat", [128, NCH, NL], BF16, 6912, "0AWF")
        s_w2b = at("s_w2b", [128, 16, D], BF16, 6912, "2")
        s_crow = at("s_crow", [1, NL], F32, 39776, "0A")
        s_sg = at("s_sg", [128, NCH, TT], BF16, 39776, "2")

        s_xtok = [at(f"s_xtok{i}", [128, D], F32, 48000 + 4096 * i, "0")
                  for i in range(3)]
        s_xht = [at(f"s_xht{i}", [128, D], BF16, 60288 + 2048 * i, "0")
                 for i in range(2)]
        s_sq = at("s_sq", [128, D], F32, 64384, "0")
        s_scol = at("s_scol", [128, 16], F32, 68480, "0")
        s_ccols = at("s_ccols", [128, 20], F32, 68544, "0")

        s_SM = at("s_SM", [16, 6, TT], F32, 48000, "A")
        s_smxm = at("s_smxm", [16, TT], F32, 60288, "A")
        s_smxr = at("s_smxr", [16, TT], F32, 62336, "A")
        s_wo = at("s_wo", [128, NCH, D], BF16, 48000, "W")
        s_on1f = at("c_on1f", [1, 128], F32, 64384, "A")
        s_vb4 = at("c_vb4", [128, 3, 32], F32, 67968, "A")
        s_SMb = at("s_SMb", [16, 3, TT], BF16, 64896, "A")
        s_hinA = at("s_hinA", [128, 16, TT], BF16, 48000, "2")

        s_h2 = at("s_h2", [128, NCH, TC], BF16, 68672, "AW")
        s_w1b = at("s_w1b", [128, NCH, 2048], BF16, 68672, "F")
        s_hinB = at("s_hinB", [128, 16, TT], BF16, 68672, "2")
        s_xre = at("s_xre", [128, 4, D], F32, 85056, "2")

        A12 = 101504
        s_B = [at(f"s_B{i}", [128, NCH, D], BF16, A12 + 16384 * i, "A")
               for i in range(3)]
        s_dbgst = nc.alloc_sbuf_tensor_at("s_dbgst", [128, NB, 16], F32,
                                          offset=BASE + 68672)
        s_stat = at("s_stat", [128, 15, 20], F32, 150656, "0")
        s_xpb = at("s_xpb", [128, NCH, E], BF16, 150656, "A")
        s_rbf = at("s_rbf", [128, NCH, TT], BF16, 150656, "A")
        s_q = at("s_q", [128, NCH, E], BF16, 158912, "A")
        s_k = at("s_k", [128, NCH, E], BF16, 167168, "A")
        s_v = at("s_v", [128, NCH, E], BF16, 175424, "A")
        s_u = at("s_u", [128, NCH, E - 2], BF16, 183680, "A")
        s_u2 = nc.alloc_sbuf_tensor_at("s_u2", [128, NCH, E - 2], BF16,
                                       offset=BASE + 150656)
        s_SMb2 = nc.alloc_sbuf_tensor_at("s_SMb2", [16, 3, TT], BF16,
                                         offset=BASE + 202144)
        s_ao = at("s_ao", [128, 16, TT], BF16, 183680, "A")
        s_cb = at("s_cb", [128, E], F32, 200064, "A")
        s_t1 = at("s_t1", [128, E], F32, 202144, "A")
        s_x2 = at("s_x2", [128, TT], F32, 204224, "A")
        s_wb = [at(f"s_wb{i}", [128, TT], BF16, 206272 + 1024 * i, "A")
                for i in range(2)]
        s_wvp = [at(f"s_wvp{i}", [128, TT], BF16, 208320 + 1024 * i, "A")
                 for i in range(2)]
        s_rsq = [at(f"s_rsq{i}", [128, TT], BF16, 210368 + 1024 * i, "A")
                 for i in range(2)]

        s_s = at("s_sgate", [128, NCH, TC], F32, A12, "WF2")
        s_w1a = at("s_w1a", [128, NCH, 2048], BF16, 167040, "WF")
        s_w2a = at("s_w2a", [128, 16, D], BF16, 167040, "2")
        s_hst = [at(f"s_hst{i}", [128, TC], BF16, 199808 + 4096 * i, "F")
                 for i in range(2)]
        s_hst.append(at("s_hst2", [128, TC], BF16, 39776, "F"))
        s_gsc = [at(f"s_gsc{i}", [128, TT], F32, 199808 + 2048 * i, "2")
                 for i in range(2)]
        s_outst = at("s_outst", [128, 2, D], F32, 203904, "2")

        for i in range(len(regions)):
            for j in range(i + 1, len(regions)):
                n1, a1, b1, st1 = regions[i]
                n2, a2, b2, st2 = regions[j]
                if a1 < b2 and a2 < b1 and (st1 & st2):
                    allowed = {("s_xpb", "s_rbf"), ("s_u", "s_ao")}
                    if (n1, n2) not in allowed and (n2, n1) not in allowed:
                        raise AssertionError(f"overlap {n1} {n2}")
                if b1 > 212700 or b2 > 212700:
                    raise AssertionError("sbuf overflow")

        pf = [en(nc.psum_tensor(f"pf{i}", [128, 512], F32)) for i in range(6)]
        pb = [en(nc.psum_tensor(f"pbb{i}", [128, 1024], BF16)) for i in range(2)]

        sems = {n: en(nc.semaphore(n)) for n in _SEMS}

        def OP(eng, fn, waits=(), inc=None, inck=1):
            return sch.op(eng, fn, waits, inc, inck)

        def gfence(target=None):
            if _SIMGF:
                return OP("t", lambda En, t_=target:
                          En.matmul(t_, s_onc[0:1, 0:1], s_on1[0:1, :],
                                    start=True, stop=True), inc="sT")
            anchor = sch.tok("sT")
            return OP("t", lambda En: En.nop(cycle_cnt=700), waits=[anchor],
                      inc="sT")

        def DMA(eng, out, in_, sem, waits=(), **kw):
            return OP(eng,
                      (lambda En, o_=out, i_=in_, k_=dict(kw): En.dma_start(out=o_, in_=i_, **k_)),
                      waits=waits, inc=sem, inck=16)

        DMA("g", s_id[:], d_id, "dC")
        DMA("g", s_idb[:], d_id, "dC")
        DMA("g", s_sel[:], d_sel.rearrange("h (c d) -> h c d", c=NCH), "dC")
        DMA("g", s_ind[:], d_ind.rearrange("p (c h) -> p c h", c=NCH), "dC")
        DMA("g", s_on1[:], d_ones1, "dC")
        DMA("g", s_onc[:], d_onesc, "dC")
        DMA("g", s_vecs[:], d_vecs.rearrange("p (n c) -> p n c", n=10), "dC")
        DMA("g", s_b1f[:], d_b1f, "dC")
        DMA("g", s_mpre[:], d_mpre.rearrange("h (t j c) -> h t j c", t=NT, j=6), "dC")
        tok_consts = DMA("g", s_mpost[:], d_mpost.rearrange("h (t j c) -> h t j c", t=NT, j=6), "dC")

        def wslice(d_w, lo_chunk, n_chunk=8, cols=None):
            ap = d_w[128 * lo_chunk:128 * (lo_chunk + n_chunk), :]
            if cols is not None:
                ap = ap[:, cols[0]:cols[1]]
            return ap.rearrange("(i p) o -> p i o", p=128)

        tok_wq = [None] * (NT + 1)
        tok_wk = [None] * (NT + 1)
        tok_wv = [None] * (NT + 1)
        tok_wq[0] = DMA("g", s_B[0][:], wslice(d_wq, 0), "dB0")
        tok_wk[0] = DMA("g", s_B[1][:], wslice(d_wk, 0), "dB1")
        tok_wv[0] = DMA("g", s_B[2][:], wslice(d_wv, 0), "dB2")

        OP("v", lambda En: En.memset(s_ccols[:], 0.0))

        tok_xtok_free = [None] * NB
        tok_xht_free = [None] * NB
        tok_pb_evac = [None, None]
        tok_pc_evac = None
        tok_load = [None] * NB
        ST = s_stat

        for g0 in range(0, NB, 3):
            grp = list(range(g0, min(g0 + 3, NB)))
            g1 = grp[-1] + 1
            for b in grp:
                nb_ = 4 if b == NB - 1 else 128
                w = [tok_xtok_free[b - 3]] if b >= 3 else []
                tok_load[b] = DMA("s", s_xtok[b % 3][0:nb_, :],
                                  d_xs[128 * b:128 * b + nb_, :],
                                  f"dX{b % 3}", waits=w)
            rtoks = []
            for b in grp:
                xt = s_xtok[b % 3]
                rtoks.append(OP("v", lambda En, a=ST[:, 0, b:b + 1], i=xt[:]:
                                En.tensor_reduce(a, i, AX, ALU.add),
                                waits=[tok_load[b]], inc="sV"))
                rtoks.append(OP("v", lambda En, i=xt[:], a=ST[:, 1, b:b + 1]:
                                En.scalar_tensor_tensor(
                                    s_sq[:], i, 1.0, i, ALU.mult, ALU.mult,
                                    accum_out=a), inc="sV"))
            w1 = OP("v", lambda En, o_=ST[:, 2, g0:g1], a_=ST[:, 0, g0:g1]:
                    En.tensor_scalar_mul(o_, a_, 1.0 / D),
                    waits=rtoks, inc="sV")
            OP("v", lambda En, o_=ST[:, 3, g0:g1], a_=ST[:, 1, g0:g1]:
               En.tensor_scalar_mul(o_, a_, 1.0 / D))
            w2 = OP("v", lambda En, o_=ST[:, 4, g0:g1], a_=ST[:, 2, g0:g1]:
                    En.tensor_mul(o_, a_, a_), waits=[w1], inc="sV")
            w3 = OP("v", lambda En, o_=ST[:, 5, g0:g1], a_=ST[:, 3, g0:g1],
                    b_=ST[:, 4, g0:g1]:
                    En.tensor_sub(o_, a_, b_), waits=[w2], inc="sV")
            tA = OP("v", lambda En, o_=ST[:, 6, g0:g1], a_=ST[:, 5, g0:g1]:
                    En.tensor_scalar_add(o_, a_, 1e-5), waits=[w3], inc="sV")
            wa1 = OP("a", lambda En, o_=ST[:, 7, g0:g1], a_=ST[:, 6, g0:g1]:
                     En.activation(o_, a_, AF.Ln), waits=[tA], inc="sA")
            OP("a", lambda En, o_=ST[:, 8, g0:g1], a_=ST[:, 7, g0:g1]:
               En.activation(o_, a_, AF.Exp, scale=-0.5), waits=[wa1])
            tB = OP("a", lambda En, o_=ST[:, 9, g0:g1], a_=ST[:, 7, g0:g1]:
                    En.activation(o_, a_, AF.Exp, scale=-1.0), inc="sA")
            w4 = OP("v", lambda En, o_=ST[:, 10, g0:g1], a_=ST[:, 5, g0:g1],
                    b_=ST[:, 9, g0:g1]:
                    En.tensor_mul(o_, a_, b_), waits=[tB], inc="sV")
            tC = OP("v", lambda En, o_=ST[:, 11, g0:g1], a_=ST[:, 10, g0:g1]:
                    En.tensor_scalar_add(o_, a_, 1e-6), waits=[w4], inc="sV")
            wa2 = OP("a", lambda En, o_=ST[:, 12, g0:g1], a_=ST[:, 11, g0:g1]:
                     En.activation(o_, a_, AF.Ln), waits=[tC], inc="sA")
            tD = OP("a", lambda En, o_=ST[:, 13, g0:g1], a_=ST[:, 12, g0:g1]:
                    En.activation(o_, a_, AF.Exp, scale=-0.5),
                    waits=[wa2], inc="sA")
            w5b = OP("v", lambda En, o_=ST[:, 14, g0:g1], a_=ST[:, 2, g0:g1],
                     b_=ST[:, 8, g0:g1]:
                     En.scalar_tensor_tensor(o_, a_, -1.0, b_,
                                             ALU.mult, ALU.mult),
                     waits=[tD], inc="sV")

            for b in grp:
                nb_ = 4 if b == NB - 1 else 128
                xs_ = b % 2
                L0 = 128 * b
                xt = s_xtok[b % 3]
                wx = [w5b]
                if b >= 2:
                    wx.append(tok_xht_free[b - 2])
                tE = OP("v", lambda En, o_=s_xht[xs_][:], i=xt[:],
                        g_=ST[:, 8, b:b + 1], m_=ST[:, 14, b:b + 1]:
                        En.tensor_scalar(o_, i, g_, m_, ALU.mult, ALU.add),
                        waits=wx, inc="sV")
                tok_xtok_free[b] = tE

                wt = [tE, tok_consts]
                if tok_pb_evac[xs_] is not None:
                    wt.append(tok_pb_evac[xs_])
                for c in range(NCH):
                    OP("t", lambda En, o_=pb[xs_][:, 128 * c:128 * (c + 1)],
                       i=s_xht[xs_][:, 128 * c:128 * (c + 1)]:
                       En.transpose(o_, i, s_idb[:]), waits=wt)
                    wt = []
                wc = [tok_pc_evac] if tok_pc_evac else []
                OP("t", lambda En, i=ST[:, 13, b:b + 1]:
                   En.transpose(pf[0][0:1, 0:128], i, s_id[:]), waits=wc)
                tok_tr = OP("t", lambda En:
                            En.matmul(pf[1][0:1, 0:128], s_onc[0:1, 0:1],
                                      s_on1[0:1, :],
                                      start=True, stop=True), inc="sT")
                tok_xht_free[b] = tok_tr
                tok_pb_evac[xs_] = OP(
                    "a", lambda En, o_=s_xhat[:, :, L0:L0 + nb_],
                    i=pb[xs_][:].rearrange("p (c t) -> p c t", c=NCH)[:, :, 0:nb_]:
                    En.copy(o_, i), waits=[tok_tr], inc="sA")
                tok_pc_evac = OP(
                    "a", lambda En, o_=s_crow[0:1, L0:L0 + nb_],
                    i=pf[0][0:1, 0:nb_]:
                    En.copy(o_, i), waits=[tok_tr], inc="sA")

        sch.barrier()
        if _DBG:
            DMA("s", d_dbg_st, s_dbgst[:].rearrange("p b c -> p (b c)"), "dD")
            DMA("s", d_dbg_xhat, s_xhat[:].rearrange("p c t -> p (c t)"), "dD")
            DMA("s", d_dbg_crow, s_crow[:], "dD")

        tok_on1f = DMA("g", s_on1f[:], d_ones1, "dC2")
        tok_vb4 = DMA("g", s_vb4[:], d_vb4, "dC2")
        t_pf_free = [None] * 6
        tok_fa = [None] * NT
        tok_fb = [None] * NT
        tok_u_freed = None
        tok_xpb_free = None
        tok_ao_free = None

        for t in range(NT):
            e0 = TT * t
            qb = s_B[(0 - t) % 3]
            kb = s_B[(1 - t) % 3]
            vb = s_B[(2 - t) % 3]

            t1m = OP("t", lambda En, r_=s_crow[0:1, e0:e0 + 512]:
                     En.matmul(pf[4][:], s_on1f[:], r_, start=True, stop=True),
                     waits=[tok_consts, tok_on1f, tok_pc_evac, t_pf_free[4]], inc="sT")
            t2m = OP("t", lambda En, r_=s_crow[0:1, e0 + 512:e0 + E]:
                     En.matmul(pf[5][:, 0:4], s_on1f[:], r_, start=True, stop=True),
                     waits=[t_pf_free[5]], inc="sT")
            fcb = gfence(pf[5][32:33, 256:384])
            ce1 = OP("a", lambda En: En.copy(s_cb[:, 0:512], pf[4][:]),
                     waits=[fcb], inc="sA")
            ce2 = OP("a", lambda En: En.copy(s_cb[:, 512:E], pf[5][:, 0:4]),
                     inc="sA")
            t_pf_free[4] = ce1
            t_pf_free[5] = ce2

            tok_xpb = [None] * NCH
            for c in range(NCH):
                OP("v", lambda En, i=s_xhat[:, c, e0:e0 + E]:
                   En.tensor_tensor(s_t1[:], i, s_cb[:], ALU.mult),
                   waits=[ce1, ce2, tok_xpb_free])
                tok_xpb[c] = OP(
                    "v", lambda En, o_=s_xpb[:, c, :], g_=vec("g1")[:, c:c + 1],
                    b_=vec("b1")[:, c:c + 1]:
                    En.tensor_scalar(o_, s_t1[:], g_, b_, ALU.mult, ALU.add),
                    inc="sV")

            tok_qe = [None] * NCH
            tok_ke = [None] * NCH
            tok_ve = [None] * NCH
            evs = {"q": tok_qe, "k": tok_ke, "v": tok_ve}
            obuf = {"q": s_q, "k": s_k, "v": s_v}
            bias = {"q": "bq", "k": "bk", "v": "bv"}
            wtk = {"q": tok_wq[t], "k": tok_wk[t], "v": tok_wv[t]}
            wbf = {"q": qb, "k": kb, "v": vb}
            lastmm = {}
            qkv_pend = None

            def qkv_flush(pv, cover):
                wn_, oc_, p5_ = pv
                ev1 = OP("a", lambda En, o_=obuf[wn_][:, oc_, 0:512], p_=p5_,
                         b_=vec(bias[wn_])[:, oc_:oc_ + 1]:
                         En.activation(o_, p_[:], AF.Identity, bias=b_, scale=1.0),
                         waits=[cover], inc="sA")
                evs[wn_][oc_] = ev1
                t_pf_free[(0, 1, 4)[oc_ % 3]] = ev1
                if oc_ == NCH - 1:
                    wni = {"q": 0, "k": 1, "v": 2}[wn_]
                    ev2 = OP("a", lambda En, o_=s_t1[:, 32 * wni:32 * wni + 32],
                             p_=pf[2][:, 0:32]: En.copy(o_, p_), inc="sA")
                    t_pf_free[2] = ev2
                    OP("v", lambda En,
                       o_=obuf[wn_][:, :, 512:E],
                       a_=s_t1[:, 32 * wni:32 * wni + 32],
                       b_=s_vb4[:, wni:wni + 1, :]:
                       En.tensor_tensor(o_, a_, b_, ALU.add),
                       waits=[ev2, tok_vb4])

            for wn in ("q", "k", "v"):
                B_ = wbf[wn]
                for oc in range(NCH):
                    bmain = (0, 1, 4)[oc % 3]
                    p5 = pf[bmain]
                    w0 = [wtk[wn], t_pf_free[bmain]]
                    for ic in range(NCH):
                        mmm = OP("t", lambda En, l_=B_[:, ic, 128 * oc:128 * (oc + 1)],
                                 r_=s_xpb[:, ic, 0:512], st=(ic == 0), sp=(ic == NCH - 1),
                                 p_=p5:
                                 En.matmul(p_[:], l_, r_, start=st, stop=sp),
                                 waits=w0 + [tok_xpb[ic]],
                                 inc=("sT" if ic == 0 else None))
                        w0 = []
                        if ic == 0 and qkv_pend is not None:
                            qkv_flush(qkv_pend, mmm)
                            qkv_pend = None
                        mmt = OP("t", lambda En, l_=B_[:, ic, 128 * oc:128 * (oc + 1)],
                                 r_=s_xpb[:, ic, 512:E], st=(ic == 0), sp=(ic == NCH - 1),
                                 p_=pf[2], co=4 * oc:
                                 En.matmul(p_[:, co:co + 4], l_, r_, start=st, stop=sp),
                                 waits=([t_pf_free[2]] if ic == 0 and oc == 0 else []),
                                 inc=("sT" if ic == NCH - 1 else None))
                    qkv_pend = (wn, oc, p5)
                lastmm[wn] = mmt

            tok_fa[t] = DMA("g", qb[:], wslice(d_wfb, 0), f"dB{(0 - t) % 3}", waits=[lastmm["q"]])
            tok_fb[t] = DMA("g", kb[:], wslice(d_wfb, 8), f"dB{(1 - t) % 3}", waits=[lastmm["k"]])
            if t + 1 < NT:
                tok_wq[t + 1] = DMA("g", vb[:], wslice(d_wq, 0), f"dB{(2 - t) % 3}",
                                    waits=[lastmm["v"]])

            tok_sm = []
            prev_scores = [None, None]
            sc_pend = None
            for j in range(3):
                ub = (s_u, s_u2, s_u)[j]
                ut = [None] * NCH
                for c in range(NCH):
                    w = [tok_qe[c], tok_ke[c]]
                    if j == 0 and c == 0 and tok_u_freed is not None:
                        w.append(tok_u_freed)
                    if c == 0 and j >= 2 and prev_scores[j % 2] is not None:
                        w.append(prev_scores[j % 2])
                    ut[c] = OP("v", lambda En, o_=ub[:, c, :],
                               a_=s_q[:, c, 1:E - 1], b_=s_k[:, c, 2 - j:E - j]:
                               En.tensor_tensor(o_, a_, b_, ALU.mult),
                               waits=w, inc="sV")
                wF = [t_pf_free[4]]
                for c in range(NCH):
                    mmF = OP("t", lambda En, l_=s_ind[:, c, :], r_=ub[:, c, 0:512],
                             st=(c == 0), sp=(c == NCH - 1):
                             En.matmul(pf[4][0:16, :], l_, r_, start=st, stop=sp),
                             waits=wF + [ut[c]],
                             inc=("sT" if c in (0, NCH - 1) else None))
                    wF = []
                    if c == 0:
                        if qkv_pend is not None:
                            qkv_flush(qkv_pend, mmF)
                            qkv_pend = None
                        if sc_pend is not None:
                            e2 = OP("a", lambda En, jp=sc_pend,
                                    o_=s_SM[:, 3 + (2 - sc_pend), :]:
                                    En.copy(o_, pf[5][0:16, :]),
                                    waits=[mmF], inc="sA")
                            t_pf_free[5] = e2
                            tok_sm.append(e2)
                            sc_pend = None
                wR = [t_pf_free[5]]
                mmR0 = None
                for c in range(NCH):
                    mmR = OP("t", lambda En, l_=s_ind[:, c, :], r_=ub[:, c, 2:514],
                             st=(c == 0), sp=(c == NCH - 1):
                             En.matmul(pf[5][0:16, :], l_, r_, start=st, stop=sp),
                             waits=wR,
                             inc=("sT" if c in (0, NCH - 1) else None))
                    wR = []
                    if c == 0:
                        mmR0 = mmR
                e1 = OP("a", lambda En, o_=s_SM[:, j, :]:
                        En.copy(o_, pf[4][0:16, :]), waits=[mmR0], inc="sA")
                t_pf_free[4] = e1
                tok_sm.append(e1)
                sc_pend = j
                prev_scores[j % 2] = mmR
            fsc = gfence(pf[4][32:33, 256:384])
            e2 = OP("a", lambda En, o_=s_SM[:, 3 + (2 - 2), :]:
                    En.copy(o_, pf[5][0:16, :]), waits=[fsc], inc="sA")
            t_pf_free[5] = e2
            tok_sm.append(e2)
            sc_pend = None
            tok_u_read = sch.tok("sT")

            OP("v", lambda En, m_=s_mpre[:, t, :, 0:8]:
               En.tensor_tensor(s_SM[:, :, 0:8], s_SM[:, :, 0:8], m_, ALU.mult),
               waits=tok_sm)
            OP("v", lambda En, m_=s_mpre[:, t, :, 8:16]:
               En.tensor_tensor(s_SM[:, :, TT - 8:TT], s_SM[:, :, TT - 8:TT], m_, ALU.mult))

            tok_ao = [None] * 16
            par = 0
            wb_read = [None, None, None, None]
            tok_prev_wv = None
            wvbbuf = (s_wb[0], s_wb[1], s_rsq[0], s_rsq[1])

            def wv_flush(pv, cover_tok):
                i_, pw_, c_, j_, dd_ = pv
                wbb = wvbbuf[i_ % 4]
                ewb = OP("a", lambda En, o_=wbb[:], p_=pw_:
                         En.copy(o_, p_[:]),
                         waits=[cover_tok, wb_read[i_ % 4]], inc="sA")
                t_pf_free[2 + i_ % 4] = ewb
                vs = (2 - j_) if dd_ == 0 else (2 + j_)
                if j_ == 0:
                    tok_ao[8 * dd_ + c_] = OP(
                        "v", lambda En, o_=s_ao[:, 8 * dd_ + c_, :],
                        a_=wbb[:], b_=s_v[:, c_, vs:vs + 512]:
                        En.tensor_mul(o_, a_, b_),
                        waits=[ewb, tok_ve[c_]], inc="sV")
                    wb_read[i_ % 4] = tok_ao[8 * dd_ + c_]
                else:
                    wb_read[i_ % 4] = OP(
                        "v", lambda En, o_=s_wvp[i_ % 2][:],
                        a_=wbb[:], b_=s_v[:, c_, vs:vs + 512]:
                        En.tensor_mul(o_, a_, b_), waits=[ewb], inc="sV")
                    tok_ao[8 * dd_ + c_] = OP(
                        "v", lambda En, o_=s_ao[:, 8 * dd_ + c_, :],
                        a_=s_wvp[i_ % 2][:]:
                        En.tensor_add(o_, o_, a_), inc="sV")
            wv_pend = None
            tok_wf = [None, None]
            for dd in range(2):
                SMB = (s_SMb, s_SMb2)[dd]
                d0 = 3 * dd
                OP("v", lambda En, a_=s_SM[:, d0, :], b_=s_SM[:, d0 + 1, :]:
                   En.tensor_max(s_smxm[:], a_, b_))
                OP("v", lambda En, a_=s_SM[:, d0 + 2, :]:
                   En.tensor_max(s_smxm[:], s_smxm[:], a_))
                for j in range(3):
                    tsb = OP("v", lambda En, a_=s_SM[:, d0 + j, :]:
                             En.tensor_sub(a_, a_, s_smxm[:]),
                             inc=("sV" if j == 2 else None))
                tex = OP("a", lambda En, a_=s_SM[:, d0:d0 + 3, :]:
                         En.activation(a_, a_, AF.Exp),
                         waits=[tsb], inc="sA")
                OP("v", lambda En, a_=s_SM[:, d0, :], b_=s_SM[:, d0 + 1, :]:
                   En.tensor_add(s_smxr[:], a_, b_), waits=[tex])
                tsum = OP("v", lambda En, a_=s_SM[:, d0 + 2, :]:
                          En.tensor_add(s_smxr[:], s_smxr[:], a_), inc="sV")
                OP("a", lambda En: En.activation(s_smxm[:], s_smxr[:], AF.Ln),
                   waits=[tsum])
                trc = OP("a", lambda En: En.activation(s_smxr[:], s_smxm[:], AF.Exp, scale=-1.0),
                         inc="sA")
                for j in range(3):
                    OP("v", lambda En, a_=s_SM[:, d0 + j, :], o_=SMB[:, j, :]:
                       En.tensor_mul(o_, a_, s_smxr[:]),
                       waits=([trc, tok_prev_wv] if j == 0 else []))
                OP("v", lambda En, m_=s_mpre[0:16, t, d0:d0 + 3, 0:8],
                   o_=SMB[:, :, 0:8], mm_=s_mpost[:, t, d0:d0 + 3, 0:8]:
                   En.tensor_tensor(o_, o_, mm_, ALU.mult))
                tok_wf[dd] = OP("v", lambda En, o_=SMB[:, :, TT - 8:TT],
                                m_=s_mpost[:, t, d0:d0 + 3, 8:16]:
                                En.tensor_tensor(o_, o_, m_, ALU.mult),
                                inc="sV")

            for dd in range(2):
                SMB = (s_SMb, s_SMb2)[dd]
                wvw = [tok_wf[dd], tok_u_read]
                if dd == 0 and tok_ao_free is not None:
                    wvw.append(tok_ao_free)
                for c in range(NCH):
                    for j in range(3):
                        pw = pf[2 + par % 4]
                        mmW = OP("t", lambda En, l_=s_sel[:, c, :], r_=SMB[:, j, :],
                                 p_=pw:
                                 En.matmul(p_[:], l_, r_, start=True, stop=True),
                                 waits=wvw + [t_pf_free[2 + par % 4]], inc="sT")
                        wvw = []
                        if wv_pend is not None:
                            wv_flush(wv_pend, mmW)
                        wv_pend = (par, pw, c, j, dd)
                        par += 1
            tok_prev_wv = sch.tok("sT")

            tok_rbf = [None] * NCH
            tok_rsq = [None] * NCH
            stat_mm2 = [None] * NCH
            def emit_stats(so):
                wS = [t_pf_free[4], tok_rbf[so]] if so == 0 else [tok_rbf[so]]
                mS = OP("t", lambda En, r_=s_rbf[:, so, :], st=(so == 0),
                        sp=(so == NCH - 1):
                        En.matmul(pf[4][0:1, :], s_onc[:], r_, start=st, stop=sp),
                        waits=wS, inc=("sT" if so == NCH - 1 else None))
                wS2 = [t_pf_free[5], tok_rsq[so]] if so == 0 else [tok_rsq[so]]
                stat_mm2[so] = mS2 = OP(
                    "t", lambda En, r_=s_rsq[so % 2][:], st=(so == 0),
                    sp=(so == NCH - 1):
                    En.matmul(pf[5][0:1, :], s_onc[:], r_, start=st, stop=sp),
                    waits=wS2, inc="sT")
                return mS, mS2

            for oc in range(NCH):
                p5 = pf[oc % 2]
                w0 = [tok_fa[t], tok_fb[t], t_pf_free[oc % 2]]
                for ic in range(16):
                    B_ = qb if ic < 8 else kb
                    mmt = OP("t", lambda En, l_=B_[:, ic % 8, 128 * oc:128 * (oc + 1)],
                             r_=s_ao[:, ic, :], st=(ic == 0), sp=(ic == 15), p_=p5:
                             En.matmul(p_[:], l_, r_, start=st, stop=sp),
                             waits=w0 + [tok_ao[ic]],
                             inc=("sT" if ic == 15 else None))
                    w0 = []
                    if wv_pend is not None:
                        wv_flush(wv_pend, mmt)
                        wv_pend = None
                if oc >= 1:
                    emit_stats(oc - 1)
                    cov = stat_mm2[oc - 1]
                else:
                    cov = gfence(pf[4][32:33, 256:384])
                OP("v", lambda En, i=s_xhat[:, oc, e0 + 2:e0 + 2 + TT]:
                   En.tensor_tensor(s_x2[:], i, s_cb[:, 2:2 + TT], ALU.mult),
                   waits=[cov])
                OP("v", lambda En, g_=vec("g1")[:, oc:oc + 1], b_=vec("b1")[:, oc:oc + 1]:
                   En.tensor_scalar(s_x2[:], s_x2[:], g_, b_, ALU.mult, ALU.add))
                tok_rbf[oc] = OP(
                    "v", lambda En, o_=s_rbf[:, oc, :], p_=p5, b_=vec("bfb")[:, oc:oc + 1]:
                    En.scalar_tensor_tensor(o_, p_[:], b_, s_x2[:], ALU.add, ALU.add),
                    inc="sV")
                t_pf_free[oc % 2] = tok_rbf[oc]
                tok_rsq[oc] = OP(
                    "v", lambda En, o_=s_rsq[oc % 2][:], a_=s_rbf[:, oc, :]:
                    En.tensor_mul(o_, a_, a_),
                    waits=[stat_mm2[oc - 2]] if oc >= 2 else [], inc="sV")
                if oc == NCH - 1:
                    emit_stats(oc)
            rowm = s_smxm[0:1, :]
            rowe = s_smxr[0:1, :]
            rowt = s_SM[0:1, 0, :]
            fst = gfence(pf[5][32:33, 256:384])
            OP("v", lambda En: En.tensor_scalar_mul(rowm, pf[4][0:1, :], 1.0 / D),
               waits=[fst])
            OP("v", lambda En: En.tensor_scalar_mul(rowe, pf[5][0:1, :], 1.0 / D))
            OP("v", lambda En: En.tensor_mul(rowt, rowm, rowm))
            OP("v", lambda En: En.tensor_sub(rowe, rowe, rowt))
            tve = OP("v", lambda En: En.tensor_scalar_add(rowe, rowe, 1e-6), inc="sV")
            t_pf_free[4] = tve
            t_pf_free[5] = tve
            OP("a", lambda En: En.activation(rowt, rowe, AF.Ln), waits=[tve])
            trs = OP("a", lambda En: En.activation(rowe, rowt, AF.Exp, scale=-0.5),
                     inc="sA")
            tnm = OP("v", lambda En: En.tensor_scalar_mul(rowm, rowm, -1.0), inc="sV")
            bc1 = OP("t", lambda En: En.matmul(pf[2][:], s_on1f[:], rowm, start=True, stop=True),
                     waits=[tnm, t_pf_free[2]], inc="sT")
            bc2 = OP("t", lambda En: En.matmul(pf[3][:], s_on1f[:], rowe, start=True, stop=True),
                     waits=[trs, t_pf_free[3]], inc="sT")
            fbc = gfence(pf[5][32:33, 256:384])
            th2 = None
            for oc in range(NCH):
                OP("v", lambda En, a_=s_rbf[:, oc, :]:
                   En.tensor_tensor(s_x2[:], a_, pf[2][:], ALU.add),
                   waits=([fbc] if oc == 0 else []))
                OP("v", lambda En: En.tensor_tensor(s_x2[:], s_x2[:], pf[3][:], ALU.mult))
                th2 = OP("v", lambda En, o_=s_h2[:, oc, e0:e0 + TT],
                         g_=vec("g2")[:, oc:oc + 1], b_=vec("b2")[:, oc:oc + 1]:
                         En.tensor_scalar(o_, s_x2[:], g_, b_, ALU.mult, ALU.add),
                         inc="sV")
            t_pf_free[2] = th2
            t_pf_free[3] = th2

            tok_u_freed = tok_u_read
            tok_u_freed = sch.tok("sT")
            tok_xpb_free = sch.tok("sT")
            tok_ao_free = sch.tok("sT")
            if t + 1 < NT:
                tok_wk[t + 1] = DMA("g", qb[:], wslice(d_wk, 0), f"dB{(0 - t) % 3}",
                                    waits=[sch.tok("sT")])
                tok_wv[t + 1] = DMA("g", kb[:], wslice(d_wv, 0), f"dB{(1 - t) % 3}",
                                    waits=[sch.tok("sT")])

        if _DBG:
            tD = DMA("s", d_dbg_h2, s_h2[:].rearrange("p c t -> p (c t)"), "dD",
                     waits=[sch.tok("sV")])
            OP("s", lambda En: En.nop(), waits=[tD])
        sch.barrier()

        tok_wo = DMA("g", s_wo[:], wslice(d_wo, 0), "dWo")
        tok_w1a = DMA("g", s_w1a[:], d_w1[:, 0:2048].rearrange("(i p) o -> p i o", p=128),
                      "dW1a")
        pf_free = [None] * 6
        a3_pend = None

        def a3_flush(po, cover):
            pf_free[NT - 1] = OP(
                "a", lambda En, o_=s_s[:, po, 512 * (NT - 1):512 * NT],
                b_=vec("bo")[:, po:po + 1], p_=pf[NT - 1]:
                En.activation(o_, p_[:], AF.Gelu, bias=b_, scale=1.0),
                waits=[cover], inc="sA")

        for oc in range(NCH):
            lastm = [None] * NT
            w0 = [tok_wo]
            for ic in range(NCH):
                for t4 in range(NT):
                    lastm[t4] = OP(
                        "t", lambda En, l_=s_wo[:, ic, 128 * oc:128 * (oc + 1)],
                        r_=s_h2[:, ic, 512 * t4:512 * (t4 + 1)], st=(ic == 0),
                        sp=(ic == NCH - 1), p_=pf[t4]:
                        En.matmul(p_[:], l_, r_, start=st, stop=sp),
                        waits=w0 + ([pf_free[t4]] if ic == 0 else []),
                        inc=("sT" if ic == NCH - 1 or (ic == 0 and t4 == 0)
                             else None))
                    w0 = []
                    if a3_pend is not None:
                        a3_flush(a3_pend, lastm[0])
                        a3_pend = None
            for t4 in range(NT - 1):
                pf_free[t4] = OP(
                    "a", lambda En, o_=s_s[:, oc, 512 * t4:512 * (t4 + 1)],
                    b_=vec("bo")[:, oc:oc + 1], p_=pf[t4]:
                    En.activation(o_, p_[:], AF.Gelu, bias=b_, scale=1.0),
                    waits=[lastm[t4 + 1]], inc="sA")
            a3_pend = oc
        a3_flush(a3_pend, gfence(pf[4][0:1, 0:128]))
        a3_pend = None

        if _DBG:
            tD = DMA("s", d_dbg_s, s_s[:].rearrange("p c t -> p (c t)"), "dD",
                     waits=[sch.tok("sA")])
            OP("s", lambda En: En.nop(), waits=[tD])
        sch.barrier()

        tok_w1b = DMA("g", s_w1b[:], d_w1[:, 2048:4096].rearrange("(i p) o -> p i o", p=128),
                      "dW1b")
        pf_free = [None] * 6
        tok_hdma = [None] * 32
        tok_w2a = None
        f1_pend = None
        hA_toks = []

        def f1_flush(po, cover):
            pf_free[NT - 1] = OP(
                "a", lambda En, o_=s_hst[po % 3][:, 512 * (NT - 1):512 * NT],
                b_=s_b1f[:, po:po + 1], p_=pf[NT - 1]:
                En.activation(o_, p_[:], AF.Relu, bias=b_, scale=1.0),
                waits=[cover], inc="sA")
            tok_hdma[po] = DMA("a", d_h[po], s_hst[po % 3][:], f"dH{po % 4}",
                               waits=[pf_free[t4] for t4 in range(NT)])
            if po == 15:
                hA_toks.extend(sch.tok(f"dH{i}") for i in range(4))

        for oc in range(32):
            wb_ = s_w1a if oc < 16 else s_w1b
            wtok = tok_w1a if oc < 16 else tok_w1b
            col = 128 * (oc % 16)
            lastm = [None] * NT
            w0 = [wtok]
            if oc >= 3:
                w0.append(tok_hdma[oc - 3])
            for ic in range(NCH):
                for t4 in range(NT):
                    lastm[t4] = OP(
                        "t", lambda En, l_=wb_[:, ic, col:col + 128],
                        r_=s_xhat[:, ic, 2 + 512 * t4:2 + 512 * (t4 + 1)],
                        st=(ic == 0), sp=(ic == NCH - 1), p_=pf[t4]:
                        En.matmul(p_[:], l_, r_, start=st, stop=sp),
                        waits=w0 + ([pf_free[t4]] if ic == 0 else []),
                        inc=("sT" if ic == NCH - 1 or (ic == 0 and t4 == 0)
                             else None))
                    w0 = []
                    if f1_pend is not None:
                        f1_flush(f1_pend, lastm[0])
                        f1_pend = None
            for t4 in range(NT - 1):
                pf_free[t4] = OP(
                    "a", lambda En, o_=s_hst[oc % 3][:, 512 * t4:512 * (t4 + 1)],
                    b_=s_b1f[:, oc:oc + 1], p_=pf[t4]:
                    En.activation(o_, p_[:], AF.Relu, bias=b_, scale=1.0),
                    waits=[lastm[t4 + 1]], inc="sA")
            f1_pend = oc
            if oc == 15:
                tok_w2a = DMA("g", s_w2a[:],
                              d_w2[0:2048, :].rearrange("(i p) o -> p i o", p=128),
                              "dW2a", waits=[lastm[NT - 1]])
        f1_flush(f1_pend, gfence(pf[4][0:1, 0:128]))
        f1_pend = None

        sch.barrier()

        tw2b_c = [DMA("g", s_w2b[:, 4 * ci:4 * ci + 4, :],
                      d_w2[2048 + 512 * ci:2048 + 512 * (ci + 1), :]
                      .rearrange("(i p) o -> p i o", p=128), f"dW2b{ci}")
                  for ci in range(4)]
        all_h = [sch.tok(f"dH{i}") for i in range(4)]
        pf_free = [None] * 6
        pb_free = [None, None]
        tok_hinA_free = None
        tok_hinB_free = None
        tok_xre_free = None
        tok_out_dma = None
        out_dma_bi = [None, None]
        for t in range(NT):
            tsl = (512 * t, 512 * (t + 1))
            thA_c = []
            thB_c = []
            for ci in range(4):
                thA_c.append(DMA(
                    "s", s_hinA[:, 4 * ci:4 * ci + 4, :],
                    d_h[4 * ci:4 * ci + 4, :, tsl[0]:tsl[1]].rearrange("i p t -> p i t"),
                    f"dHA{ci}",
                    waits=((hA_toks if t == 0 else []) + [tok_hinA_free])
                    if ci == 0 else []))
            for ci in range(4):
                thB_c.append(DMA(
                    "s", s_hinB[:, 4 * ci:4 * ci + 4, :],
                    d_h[16 + 4 * ci:20 + 4 * ci, :, tsl[0]:tsl[1]].rearrange("i p t -> p i t"),
                    f"dHB{ci}",
                    waits=[tok_hinB_free] if ci == 0 else []))
            txr = DMA("s", s_xre[:],
                      d_xs[2 + tsl[0]:2 + tsl[1], :].rearrange("(b p) d -> p b d", p=128),
                      "dXR", waits=[tok_xre_free])
            tok_sg = [None] * NCH
            f2_pend = None

            def f2_flush(pv, cover):
                po, pp = pv
                tgs = OP("a", lambda En, o_=s_gsc[po % 2][:], p_=pp,
                         b_=vec("b2f")[:, po:po + 1]:
                         En.activation(o_, p_[:], AF.Silu, bias=b_, scale=1.0),
                         waits=[cover], inc="sA")
                tok_sg[po] = OP(
                    "v", lambda En, o_=s_sg[:, po, :], a_=s_gsc[po % 2][:],
                    b_=s_s[:, po, tsl[0]:tsl[1]]:
                    En.tensor_mul(o_, a_, b_), waits=[tgs], inc="sV")
                pf_free[po % 2] = tok_sg[po]

            for oc in range(NCH):
                p5 = pf[oc % 2]
                w0 = [tok_w2a, pf_free[oc % 2]]
                for ic in range(32):
                    wb_ = s_w2a if ic < 16 else s_w2b
                    hb_ = s_hinA if ic < 16 else s_hinB
                    if oc == 0:
                        w0.append((thA_c if ic < 16 else thB_c)[(ic % 16) // 4])
                    if oc == 0 and t == 0 and ic >= 16:
                        w0.append(tw2b_c[(ic - 16) // 4])
                    mmt = OP("t", lambda En, l_=wb_[:, ic % 16, 128 * oc:128 * (oc + 1)],
                             r_=hb_[:, ic % 16, :], st=(ic == 0), sp=(ic == 31), p_=p5:
                             En.matmul(p_[:], l_, r_, start=st, stop=sp),
                             waits=w0, inc=("sT" if ic in (0, 31) else None))
                    w0 = []
                    if f2_pend is not None:
                        f2_flush(f2_pend, mmt)
                        f2_pend = None
                f2_pend = (oc, p5)
            f2_flush(f2_pend, gfence(pf[3][0:1, 0:128]))
            f2_pend = None
            tok_hinA_free = sch.tok("sT")
            tok_hinB_free = sch.tok("sT")
            for half in range(2):
                touts = []
                for bi in range(2):
                    b4 = 2 * half + bi
                    wv0 = [tok_sg[NCH - 1], pb_free[bi]]
                    if half == 0 and out_dma_bi[bi] is not None:
                        wv0.append(out_dma_bi[bi])
                    for oc in range(NCH):
                        OP("t", lambda En, o_=pb[bi][:, 128 * oc:128 * (oc + 1)],
                           i=s_sg[:, oc, 128 * b4:128 * (b4 + 1)]:
                           En.transpose(o_, i, s_idb[:]), waits=wv0)
                        wv0 = []
                    ttr = OP("t", lambda En:
                             En.matmul(pf[2][0:1, 0:128], s_onc[0:1, 0:1], s_on1[0:1, :],
                                       start=True, stop=True), inc="sT")
                    wadd = [ttr, txr]
                    if out_dma_bi[bi] is not None:
                        wadd.append(out_dma_bi[bi])
                    if t == NT - 1 and half == 1:
                        OP("v", lambda En: En.nop() if _SIMGF else
                           En.nop(cycle_cnt=3000), waits=[ttr])
                    touts.append(OP(
                        "v", lambda En, o_=s_outst[:, bi, :], p_=pb[bi],
                        x_=s_xre[:, b4, :]:
                        En.tensor_tensor(o_, p_[:], x_, ALU.add),
                        waits=wadd, inc="sV"))
                    pb_free[bi] = touts[-1]
                    tcm = OP("v", lambda En: En.nop() if _SIMGF else
                             En.nop(cycle_cnt=700), waits=[touts[-1]],
                             inc="sV")
                    r0 = tsl[0] + 256 * half + 128 * bi
                    out_dma_bi[bi] = DMA(
                        "a", d_y[r0:r0 + 128, :],
                        s_outst[:, bi, :], ("dO", "dO1")[bi], waits=[tcm])
                tok_out_dma = out_dma_bi[1]
            tok_xre_free = sch.tok("sV")

        if _DBG3:
            DMA("s", d_dbg_sg, s_sg[:].rearrange("p c t -> p (c t)"), "dD",
                waits=[sch.tok("sV")])
            DMA("s", d_dbg_gsc[:, 0:TT], s_gsc[0][:], "dD")
            DMA("s", d_dbg_gsc[:, TT:2 * TT], s_gsc[1][:], "dD")
            tD3 = DMA("s", d_dbg_s2, s_s[:].rearrange("p c t -> p (c t)"), "dD")
            OP("s", lambda En: En.nop(), waits=[tD3])
        OP("a", lambda En: En.nop(), waits=[sch.tok("dO"), sch.tok("dO1")])
        OP("s", lambda En: En.nop(),
           waits=[sch.tok("dO"), sch.tok("dO1"), sch.tok("dD")])

        with nc.Block() as blk:
            def runner(key):
                def body(Eng):
                    for needed, fn, inc, inck in sch.q[key]:
                        for sname, val in needed:
                            Eng.wait_ge(sems[sname], val)
                        ins = fn(Eng)
                        if inc is not None:
                            ins.then_inc(sems[inc], inck)
                return body
            blk.tensor(runner("t"))
            blk.vector(runner("v"))
            blk.scalar(runner("a"))
            blk.gpsimd(runner("g"))
            blk.sync(runner("s"))

    return nc


_NC_CACHE = None


def _get_nc():
    global _NC_CACHE
    if _NC_CACHE is None:
        _NC_CACHE = build_program()
    return _NC_CACHE


def _host_consts():
    f = np.float32
    idm = np.eye(128, dtype=f)
    sel = np.zeros((16, NCH, 128), f)
    ind = np.zeros((128, NCH, 16), f)
    for c in range(NCH):
        for dd in range(128):
            h = 2 * c + (1 if dd >= 64 else 0)
            sel[h, c, dd] = 1.0
            ind[dd, c, h] = 0.125
    ones1 = np.ones((1, 128), f)
    onesc = np.ones((128, 1), f)
    return idm, sel.reshape(16, NCH * 128), ind.reshape(128, NCH * 16), ones1, onesc


def _masks_for_core(t0):
    f = np.float32
    mpre = np.ones((16, NT, 6, 16), f)
    mpost = np.ones((16, NT, 6, 16), f)
    for t in range(NT):
        for col in range(16):
            if col < 8:
                g = t0 + 512 * t + col
            else:
                g = t0 + 512 * t + 504 + (col - 8)
            for j in range(3):
                if not (g >= max(1, j)):
                    mpre[:, t, j, col] = 0.0
                if not (g >= j):
                    mpost[:, t, j, col] = 0.0
                if not (g <= min(SEQ - 2, SEQ - 1 - j)):
                    mpre[:, t, 3 + j, col] = 0.0
                if not (g <= SEQ - 1 - j):
                    mpost[:, t, 3 + j, col] = 0.0
    return mpre.reshape(16, NT * 6 * 16), mpost.reshape(16, NT * 6 * 16)


def kernel(L0_attn_input_feature, L0_attn_Wq, L0_attn_bq, L0_attn_Wk, L0_attn_bk,
           L0_attn_Wv, L0_attn_bv, L0_attn_Wfb, L0_attn_bfb, L0_attn_Wo,
           L0_attn_bo, L0_attn_g1, L0_attn_b1, L0_attn_g2, L0_attn_b2,
           L0_ffn_W1, L0_ffn_b1, L0_ffn_W2, L0_ffn_b2):
    f = np.float32
    x = np.ascontiguousarray(np.asarray(L0_attn_input_feature, f))
    W = {k: np.ascontiguousarray(np.asarray(v, f)) for k, v in dict(
        Wq=L0_attn_Wq, Wk=L0_attn_Wk, Wv=L0_attn_Wv, Wfb=L0_attn_Wfb,
        Wo=L0_attn_Wo, W1=L0_ffn_W1, W2=L0_ffn_W2).items()}
    vl = [L0_attn_g1, L0_attn_b1, L0_attn_g2, L0_attn_b2, L0_attn_bq,
          L0_attn_bk, L0_attn_bv, L0_attn_bfb, L0_attn_bo, L0_ffn_b2]
    vecs = np.stack([np.asarray(v, f).reshape(NCH, 128).T for v in vl], axis=1)
    vecs = np.ascontiguousarray(vecs.reshape(128, 80))
    b1f = np.ascontiguousarray(np.asarray(L0_ffn_b1, f).reshape(32, 128).T)
    vb4 = np.stack([np.repeat(np.asarray(v, f).reshape(NCH, 128).T[:, :, None],
                              4, axis=2)
                    for v in (L0_attn_bq, L0_attn_bk, L0_attn_bv)], axis=1)
    vb4 = np.ascontiguousarray(vb4.reshape(128, 3, 32))
    idm, sel, ind, ones1, onesc = _host_consts()

    zeros2 = np.zeros((2, D), f)
    in_maps = []
    for core in range(8):
        b, half = divmod(core, 2)
        if half == 0:
            xs = np.concatenate([zeros2, x[b, 0:TC + 2]], axis=0)
        else:
            xs = np.concatenate([x[b, TC - 2:2 * TC], zeros2], axis=0)
        mpre, mpost = _masks_for_core(half * TC)
        in_maps.append(dict(
            xs=np.ascontiguousarray(xs), vecs=vecs, b1f=b1f, vb4=vb4, idm=idm,
            sel=sel, ind=ind, ones1=ones1, onesc=onesc, mpre=mpre, mpost=mpost,
            **W))

    nc = _get_nc()
    res = run_bass_kernel_spmd(nc, in_maps, core_ids=list(range(8)))
    if _DBG or _DBG3:
        global _DBG_RES
        _DBG_RES = res.results
    ys = [res.results[i]["y"] for i in range(8)]
    out = np.stack([np.concatenate([ys[2 * b], ys[2 * b + 1]], axis=0)
                    for b in range(4)])
    return np.ascontiguousarray(out.astype(np.float32))


# version 17 (final)
# speedup vs baseline: 1.0749x; 1.0117x over previous
import numpy as np
import ml_dtypes
from contextlib import ExitStack

import concourse.bass as bass
import concourse.mybir as mybir
from concourse.bass_utils import run_bass_kernel_spmd

F32 = mybir.dt.float32
BF16 = mybir.dt.bfloat16
AF = mybir.ActivationFunctionType
ALU = mybir.AluOpType
AX = mybir.AxisListType.X

D = 1024
NCH = 8
TC = 2048
NL = 2052
TT = 512
NT = TC // TT
E = TT + 4
NB = 17
SEQ = 4096

_DBG = False
_SIMGF = False
_DBG3 = False

_SEMS = (["dD", "dX0", "dX1", "dX2", "dHA", "dHB", "dXR", "dB0", "dB1", "dB2",
          "dC", "dC2", "dWo", "dW1a", "dW1b", "dW2a", "dW2b", "dH0", "dH1",
          "dH2", "dH3", "dHA0", "dHA1", "dHA2", "dHA3", "dHB0", "dHB1",
          "dHB2", "dHB3", "dWo0", "dWo1", "dWo2", "dWo3",
          "dO", "dO1", "sT", "sV", "sA", "sG", "sS", "bt", "bv", "ba", "bg",
          "bs"])


class Sch:
    ENG = "tvags"

    def __init__(self):
        self.q = {k: [] for k in self.ENG}
        self.cnt = {}
        self.lastw = {k: {} for k in self.ENG}

    def tok(self, sem):
        return (sem, self.cnt.get(sem, 0))

    def op(self, eng, fn, waits=(), inc=None, inck=1):
        lw = self.lastw[eng]
        needed = []
        for w in waits:
            if w is None:
                continue
            sname, val = w
            if val > 0 and lw.get(sname, 0) < val:
                lw[sname] = val
                needed.append((sname, val))
        out_tok = None
        if inc is not None:
            self.cnt[inc] = self.cnt.get(inc, 0) + inck
            out_tok = (inc, self.cnt[inc])
        self.q[eng].append((needed, fn, inc, inck))
        return out_tok

    def barrier(self):
        toks = {}
        for e in self.ENG:
            self.op(e, lambda E: E.drain())
            toks[e] = self.op(e, lambda E: E.nop(), inc="b" + e)
        for e in self.ENG:
            self.op(e, lambda E: E.nop(),
                    waits=[toks[o] for o in self.ENG if o != e])


def build_program():
    nc = bass.Bass("TRN2", target_bir_lowering=False, detect_race_conditions=False)

    d_xs = nc.dram_tensor("xs", [NL, D], F32, kind="ExternalInput").ap()
    d_wq = nc.dram_tensor("Wq", [D, D], F32, kind="ExternalInput").ap()
    d_wk = nc.dram_tensor("Wk", [D, D], F32, kind="ExternalInput").ap()
    d_wv = nc.dram_tensor("Wv", [D, D], F32, kind="ExternalInput").ap()
    d_wfb = nc.dram_tensor("Wfb", [2 * D, D], F32, kind="ExternalInput").ap()
    d_wo = nc.dram_tensor("Wo", [D, D], F32, kind="ExternalInput").ap()
    d_w1 = nc.dram_tensor("W1", [D, 4 * D], F32, kind="ExternalInput").ap()
    d_w2 = nc.dram_tensor("W2", [4 * D, D], F32, kind="ExternalInput").ap()
    d_vecs = nc.dram_tensor("vecs", [128, 80], F32, kind="ExternalInput").ap()
    d_b1f = nc.dram_tensor("b1f", [128, 32], F32, kind="ExternalInput").ap()
    d_vb4 = nc.dram_tensor("vb4", [128, 3, 32], F32, kind="ExternalInput").ap()
    d_id = nc.dram_tensor("idm", [128, 128], F32, kind="ExternalInput").ap()
    d_sel = nc.dram_tensor("sel", [16, NCH * 128], F32, kind="ExternalInput").ap()
    d_ind = nc.dram_tensor("ind", [128, NCH * 16], F32, kind="ExternalInput").ap()
    d_ones1 = nc.dram_tensor("ones1", [1, 128], F32, kind="ExternalInput").ap()
    d_onesc = nc.dram_tensor("onesc", [128, 1], F32, kind="ExternalInput").ap()
    d_mpre = nc.dram_tensor("mpre", [16, NT * 6 * 16], F32, kind="ExternalInput").ap()
    d_mpost = nc.dram_tensor("mpost", [16, NT * 6 * 16], F32, kind="ExternalInput").ap()
    d_y = nc.dram_tensor("y", [TC, D], F32, kind="ExternalOutput").ap()
    d_h = nc.dram_tensor("hscr", [32, 128, TC], BF16, kind="Internal").ap()
    if _DBG3:
        d_dbg_sg = nc.dram_tensor("dbg_sg", [128, NCH * TT], BF16, kind="ExternalOutput").ap()
        d_dbg_gsc = nc.dram_tensor("dbg_gsc", [128, 2 * TT], F32, kind="ExternalOutput").ap()
        d_dbg_s2 = nc.dram_tensor("dbg_s2", [128, NCH * TC], F32, kind="ExternalOutput").ap()
    if _DBG:
        d_dbg_st = nc.dram_tensor("dbg_st", [128, NB * 16], F32, kind="ExternalOutput").ap()
        d_dbg_xhat = nc.dram_tensor("dbg_xhat", [128, NCH * NL], BF16, kind="ExternalOutput").ap()
        d_dbg_crow = nc.dram_tensor("dbg_crow", [1, NL], F32, kind="ExternalOutput").ap()
        d_dbg_h2 = nc.dram_tensor("dbg_h2", [128, NCH * TC], BF16, kind="ExternalOutput").ap()
        d_dbg_s = nc.dram_tensor("dbg_s", [128, NCH * TC], F32, kind="ExternalOutput").ap()

    sch = Sch()

    with ExitStack() as ctx:
        en = ctx.enter_context

        regions = []

        BASE = 16512

        def at(name, shape, dtype, off, stages):
            t = nc.alloc_sbuf_tensor_at(name, shape, dtype, offset=BASE + off)
            nbytes = int(np.prod(shape[1:])) * mybir.dt.size(dtype)
            regions.append((name, off, off + nbytes, frozenset(stages)))
            return t

        ALL = "0AWF2"

        s_id = at("c_id", [128, 128], F32, 0, ALL)
        s_idb = at("c_idb", [128, 128], BF16, 512, ALL)
        s_sel = at("c_sel", [16, NCH, 128], BF16, 768, ALL)
        s_ind = at("c_ind", [128, NCH, 16], BF16, 2816, ALL)
        s_on1 = at("c_on1", [1, 128], BF16, 3072, ALL)
        s_onc = at("c_onc", [128, 1], BF16, 3328, ALL)
        s_vecs = at("c_vecs", [128, 10, 8], F32, 3360, ALL)
        s_b1f = at("c_b1f", [128, 32], F32, 3680, ALL)
        s_mpre = at("c_mpre", [16, NT, 6, 16], F32, 3808, ALL)
        s_mpost = at("c_mpost", [16, NT, 6, 16], F32, 5344, ALL)
        VI = {n: i for i, n in enumerate(
            ["g1", "b1", "g2", "b2", "bq", "bk", "bv", "bfb", "bo", "b2f"])}

        def vec(n):
            return s_vecs[:, VI[n], :]

        s_xhat = at("s_xhat", [128, NCH, NL], BF16, 6912, "0AWF")
        s_w2b = at("s_w2b", [128, 16, D], BF16, 6912, "2")
        s_crow = at("s_crow", [1, NL], F32, 39776, "0A")
        s_sg = at("s_sg", [128, NCH, TT], BF16, 39776, "2")

        s_xtok = [at(f"s_xtok{i}", [128, D], F32, 48000 + 4096 * i, "0")
                  for i in range(3)]
        s_xht = [at(f"s_xht{i}", [128, D], BF16, 60288 + 2048 * i, "0")
                 for i in range(2)]
        s_sq = at("s_sq", [128, D], F32, 64384, "0")
        s_scol = at("s_scol", [128, 16], F32, 68480, "0")
        s_ccols = at("s_ccols", [128, 20], F32, 68544, "0")

        s_SM = at("s_SM", [16, 6, TT], F32, 48000, "A")
        s_smxm = at("s_smxm", [16, TT], F32, 60288, "A")
        s_smxr = at("s_smxr", [16, TT], F32, 62336, "A")
        s_wo = at("s_wo", [128, NCH, D], BF16, 48000, "W")
        s_on1f = at("c_on1f", [1, 128], F32, 64384, "A")
        s_vb4 = at("c_vb4", [128, 3, 32], F32, 67968, "A")
        s_SMb = at("s_SMb", [16, 3, TT], BF16, 64896, "A")
        s_hinA = at("s_hinA", [128, 16, TT], BF16, 48000, "2")

        s_h2 = at("s_h2", [128, NCH, TC], BF16, 68672, "AW")
        s_w1b = at("s_w1b", [128, NCH, 2048], BF16, 68672, "F")
        s_hinB = at("s_hinB", [128, 16, TT], BF16, 68672, "2")
        s_xre = at("s_xre", [128, 4, D], F32, 85056, "2")

        A12 = 101504
        s_B = [at(f"s_B{i}", [128, NCH, D], BF16, A12 + 16384 * i, "A")
               for i in range(3)]
        s_dbgst = nc.alloc_sbuf_tensor_at("s_dbgst", [128, NB, 16], F32,
                                          offset=BASE + 68672)
        s_stat = at("s_stat", [128, 15, 20], F32, 150656, "0")
        s_xpb = at("s_xpb", [128, NCH, E], BF16, 150656, "A")
        s_rbf = at("s_rbf", [128, NCH, TT], BF16, 150656, "A")
        s_q = at("s_q", [128, NCH, E], BF16, 158912, "A")
        s_k = at("s_k", [128, NCH, E], BF16, 167168, "A")
        s_v = at("s_v", [128, NCH, E], BF16, 175424, "A")
        s_u = at("s_u", [128, NCH, E - 2], BF16, 183680, "A")
        s_u2 = nc.alloc_sbuf_tensor_at("s_u2", [128, NCH, E - 2], BF16,
                                       offset=BASE + 150656)
        s_SMb2 = nc.alloc_sbuf_tensor_at("s_SMb2", [16, 3, TT], BF16,
                                         offset=BASE + 202144)
        s_ao = at("s_ao", [128, 16, TT], BF16, 183680, "A")
        s_cb = at("s_cb", [128, E], F32, 200064, "A")
        s_t1 = at("s_t1", [128, E], F32, 202144, "A")
        s_x2 = at("s_x2", [128, TT], F32, 204224, "A")
        s_wb = [at(f"s_wb{i}", [128, TT], BF16, 206272 + 1024 * i, "A")
                for i in range(2)]
        s_wvp = [at(f"s_wvp{i}", [128, TT], BF16, 208320 + 1024 * i, "A")
                 for i in range(2)]
        s_rsq = [at(f"s_rsq{i}", [128, TT], BF16, 210368 + 1024 * i, "A")
                 for i in range(2)]

        s_s = at("s_sgate", [128, NCH, TC], F32, A12, "WF2")
        s_w1a = at("s_w1a", [128, NCH, 2048], BF16, 167040, "WF")
        s_w2a = at("s_w2a", [128, 16, D], BF16, 167040, "2")
        s_hst = [at(f"s_hst{i}", [128, TC], BF16, 199808 + 4096 * i, "F")
                 for i in range(2)]
        s_hst.append(at("s_hst2", [128, TC], BF16, 39776, "F"))
        s_gsc = [at(f"s_gsc{i}", [128, TT], F32, 199808 + 2048 * i, "2")
                 for i in range(2)]
        s_outst = at("s_outst", [128, 2, D], F32, 203904, "2")

        for i in range(len(regions)):
            for j in range(i + 1, len(regions)):
                n1, a1, b1, st1 = regions[i]
                n2, a2, b2, st2 = regions[j]
                if a1 < b2 and a2 < b1 and (st1 & st2):
                    allowed = {("s_xpb", "s_rbf"), ("s_u", "s_ao")}
                    if (n1, n2) not in allowed and (n2, n1) not in allowed:
                        raise AssertionError(f"overlap {n1} {n2}")
                if b1 > 212700 or b2 > 212700:
                    raise AssertionError("sbuf overflow")

        pf = [en(nc.psum_tensor(f"pf{i}", [128, 512], F32)) for i in range(6)]
        pb = [en(nc.psum_tensor(f"pbb{i}", [128, 1024], BF16)) for i in range(2)]

        sems = {n: en(nc.semaphore(n)) for n in _SEMS}

        def OP(eng, fn, waits=(), inc=None, inck=1):
            return sch.op(eng, fn, waits, inc, inck)

        def gfence(target=None):
            if _SIMGF:
                return OP("t", lambda En, t_=target:
                          En.matmul(t_, s_onc[0:1, 0:1], s_on1[0:1, :],
                                    start=True, stop=True), inc="sT")
            anchor = sch.tok("sT")
            return OP("t", lambda En: En.nop(cycle_cnt=700), waits=[anchor],
                      inc="sT")

        def DMA(eng, out, in_, sem, waits=(), **kw):
            return OP(eng,
                      (lambda En, o_=out, i_=in_, k_=dict(kw): En.dma_start(out=o_, in_=i_, **k_)),
                      waits=waits, inc=sem, inck=16)

        DMA("g", s_id[:], d_id, "dC")
        DMA("g", s_idb[:], d_id, "dC")
        DMA("g", s_sel[:], d_sel.rearrange("h (c d) -> h c d", c=NCH), "dC")
        DMA("g", s_ind[:], d_ind.rearrange("p (c h) -> p c h", c=NCH), "dC")
        DMA("g", s_on1[:], d_ones1, "dC")
        DMA("g", s_onc[:], d_onesc, "dC")
        DMA("g", s_vecs[:], d_vecs.rearrange("p (n c) -> p n c", n=10), "dC")
        DMA("g", s_b1f[:], d_b1f, "dC")
        DMA("g", s_mpre[:], d_mpre.rearrange("h (t j c) -> h t j c", t=NT, j=6), "dC")
        tok_consts = DMA("g", s_mpost[:], d_mpost.rearrange("h (t j c) -> h t j c", t=NT, j=6), "dC")

        def wslice(d_w, lo_chunk, n_chunk=8, cols=None):
            ap = d_w[128 * lo_chunk:128 * (lo_chunk + n_chunk), :]
            if cols is not None:
                ap = ap[:, cols[0]:cols[1]]
            return ap.rearrange("(i p) o -> p i o", p=128)

        tok_wq = [None] * (NT + 1)
        tok_wk = [None] * (NT + 1)
        tok_wv = [None] * (NT + 1)
        tok_wq[0] = DMA("g", s_B[0][:], wslice(d_wq, 0), "dB0")
        tok_wk[0] = DMA("g", s_B[1][:], wslice(d_wk, 0), "dB1")
        tok_wv[0] = DMA("g", s_B[2][:], wslice(d_wv, 0), "dB2")

        OP("v", lambda En: En.memset(s_ccols[:], 0.0))

        tok_xtok_free = [None] * NB
        tok_xht_free = [None] * NB
        tok_pb_evac = [None, None]
        tok_pc_evac = None
        tok_load = [None] * NB
        ST = s_stat

        for g0 in range(0, NB, 3):
            grp = list(range(g0, min(g0 + 3, NB)))
            g1 = grp[-1] + 1
            for b in grp:
                nb_ = 4 if b == NB - 1 else 128
                w = [tok_xtok_free[b - 3]] if b >= 3 else []
                tok_load[b] = DMA("s", s_xtok[b % 3][0:nb_, :],
                                  d_xs[128 * b:128 * b + nb_, :],
                                  f"dX{b % 3}", waits=w)
            rtoks = []
            for b in grp:
                xt = s_xtok[b % 3]
                rtoks.append(OP("v", lambda En, a=ST[:, 0, b:b + 1], i=xt[:]:
                                En.tensor_reduce(a, i, AX, ALU.add),
                                waits=[tok_load[b]], inc="sV"))
                rtoks.append(OP("v", lambda En, i=xt[:], a=ST[:, 1, b:b + 1]:
                                En.scalar_tensor_tensor(
                                    s_sq[:], i, 1.0, i, ALU.mult, ALU.mult,
                                    accum_out=a), inc="sV"))
            w1 = OP("v", lambda En, o_=ST[:, 2, g0:g1], a_=ST[:, 0, g0:g1]:
                    En.tensor_scalar_mul(o_, a_, 1.0 / D),
                    waits=rtoks, inc="sV")
            OP("v", lambda En, o_=ST[:, 3, g0:g1], a_=ST[:, 1, g0:g1]:
               En.tensor_scalar_mul(o_, a_, 1.0 / D))
            w2 = OP("v", lambda En, o_=ST[:, 4, g0:g1], a_=ST[:, 2, g0:g1]:
                    En.tensor_mul(o_, a_, a_), waits=[w1], inc="sV")
            w3 = OP("v", lambda En, o_=ST[:, 5, g0:g1], a_=ST[:, 3, g0:g1],
                    b_=ST[:, 4, g0:g1]:
                    En.tensor_sub(o_, a_, b_), waits=[w2], inc="sV")
            tA = OP("v", lambda En, o_=ST[:, 6, g0:g1], a_=ST[:, 5, g0:g1]:
                    En.tensor_scalar_add(o_, a_, 1e-5), waits=[w3], inc="sV")
            wa1 = OP("a", lambda En, o_=ST[:, 7, g0:g1], a_=ST[:, 6, g0:g1]:
                     En.activation(o_, a_, AF.Ln), waits=[tA], inc="sA")
            OP("a", lambda En, o_=ST[:, 8, g0:g1], a_=ST[:, 7, g0:g1]:
               En.activation(o_, a_, AF.Exp, scale=-0.5), waits=[wa1])
            tB = OP("a", lambda En, o_=ST[:, 9, g0:g1], a_=ST[:, 7, g0:g1]:
                    En.activation(o_, a_, AF.Exp, scale=-1.0), inc="sA")
            w4 = OP("v", lambda En, o_=ST[:, 10, g0:g1], a_=ST[:, 5, g0:g1],
                    b_=ST[:, 9, g0:g1]:
                    En.tensor_mul(o_, a_, b_), waits=[tB], inc="sV")
            tC = OP("v", lambda En, o_=ST[:, 11, g0:g1], a_=ST[:, 10, g0:g1]:
                    En.tensor_scalar_add(o_, a_, 1e-6), waits=[w4], inc="sV")
            wa2 = OP("a", lambda En, o_=ST[:, 12, g0:g1], a_=ST[:, 11, g0:g1]:
                     En.activation(o_, a_, AF.Ln), waits=[tC], inc="sA")
            tD = OP("a", lambda En, o_=ST[:, 13, g0:g1], a_=ST[:, 12, g0:g1]:
                    En.activation(o_, a_, AF.Exp, scale=-0.5),
                    waits=[wa2], inc="sA")
            w5b = OP("v", lambda En, o_=ST[:, 14, g0:g1], a_=ST[:, 2, g0:g1],
                     b_=ST[:, 8, g0:g1]:
                     En.scalar_tensor_tensor(o_, a_, -1.0, b_,
                                             ALU.mult, ALU.mult),
                     waits=[tD], inc="sV")

            for b in grp:
                nb_ = 4 if b == NB - 1 else 128
                xs_ = b % 2
                L0 = 128 * b
                xt = s_xtok[b % 3]
                wx = [w5b]
                if b >= 2:
                    wx.append(tok_xht_free[b - 2])
                tE = OP("v", lambda En, o_=s_xht[xs_][:], i=xt[:],
                        g_=ST[:, 8, b:b + 1], m_=ST[:, 14, b:b + 1]:
                        En.tensor_scalar(o_, i, g_, m_, ALU.mult, ALU.add),
                        waits=wx, inc="sV")
                tok_xtok_free[b] = tE

                wt = [tE, tok_consts]
                if tok_pb_evac[xs_] is not None:
                    wt.append(tok_pb_evac[xs_])
                for c in range(NCH):
                    OP("t", lambda En, o_=pb[xs_][:, 128 * c:128 * (c + 1)],
                       i=s_xht[xs_][:, 128 * c:128 * (c + 1)]:
                       En.transpose(o_, i, s_idb[:]), waits=wt)
                    wt = []
                wc = [tok_pc_evac] if tok_pc_evac else []
                OP("t", lambda En, i=ST[:, 13, b:b + 1]:
                   En.transpose(pf[0][0:1, 0:128], i, s_id[:]), waits=wc)
                tok_tr = OP("t", lambda En:
                            En.matmul(pf[1][0:1, 0:128], s_onc[0:1, 0:1],
                                      s_on1[0:1, :],
                                      start=True, stop=True), inc="sT")
                tok_xht_free[b] = tok_tr
                tok_pb_evac[xs_] = OP(
                    "a", lambda En, o_=s_xhat[:, :, L0:L0 + nb_],
                    i=pb[xs_][:].rearrange("p (c t) -> p c t", c=NCH)[:, :, 0:nb_]:
                    En.copy(o_, i), waits=[tok_tr], inc="sA")
                tok_pc_evac = OP(
                    "a", lambda En, o_=s_crow[0:1, L0:L0 + nb_],
                    i=pf[0][0:1, 0:nb_]:
                    En.copy(o_, i), waits=[tok_tr], inc="sA")

        sch.barrier()
        if _DBG:
            DMA("s", d_dbg_st, s_dbgst[:].rearrange("p b c -> p (b c)"), "dD")
            DMA("s", d_dbg_xhat, s_xhat[:].rearrange("p c t -> p (c t)"), "dD")
            DMA("s", d_dbg_crow, s_crow[:], "dD")

        tok_on1f = DMA("g", s_on1f[:], d_ones1, "dC2")
        tok_vb4 = DMA("g", s_vb4[:], d_vb4, "dC2")
        t_pf_free = [None] * 6
        tok_fa = [None] * NT
        tok_fb = [None] * NT
        tok_u_freed = None
        tok_xpb_free = None
        tok_ao_free = None

        for t in range(NT):
            e0 = TT * t
            qb = s_B[(0 - t) % 3]
            kb = s_B[(1 - t) % 3]
            vb = s_B[(2 - t) % 3]

            t1m = OP("t", lambda En, r_=s_crow[0:1, e0:e0 + 512]:
                     En.matmul(pf[4][:], s_on1f[:], r_, start=True, stop=True),
                     waits=[tok_consts, tok_on1f, tok_pc_evac, t_pf_free[4]], inc="sT")
            t2m = OP("t", lambda En, r_=s_crow[0:1, e0 + 512:e0 + E]:
                     En.matmul(pf[5][:, 0:4], s_on1f[:], r_, start=True, stop=True),
                     waits=[t_pf_free[5]], inc="sT")
            fcb = gfence(pf[5][32:33, 256:384])
            ce1 = OP("a", lambda En: En.copy(s_cb[:, 0:512], pf[4][:]),
                     waits=[fcb], inc="sA")
            ce2 = OP("a", lambda En: En.copy(s_cb[:, 512:E], pf[5][:, 0:4]),
                     inc="sA")
            t_pf_free[4] = ce1
            t_pf_free[5] = ce2

            tok_xpb = [None] * NCH
            for c in range(NCH):
                OP("v", lambda En, i=s_xhat[:, c, e0:e0 + E]:
                   En.tensor_tensor(s_t1[:], i, s_cb[:], ALU.mult),
                   waits=[ce1, ce2, tok_xpb_free])
                tok_xpb[c] = OP(
                    "v", lambda En, o_=s_xpb[:, c, :], g_=vec("g1")[:, c:c + 1],
                    b_=vec("b1")[:, c:c + 1]:
                    En.tensor_scalar(o_, s_t1[:], g_, b_, ALU.mult, ALU.add),
                    inc="sV")

            tok_qe = [None] * NCH
            tok_ke = [None] * NCH
            tok_ve = [None] * NCH
            evs = {"q": tok_qe, "k": tok_ke, "v": tok_ve}
            obuf = {"q": s_q, "k": s_k, "v": s_v}
            bias = {"q": "bq", "k": "bk", "v": "bv"}
            wtk = {"q": tok_wq[t], "k": tok_wk[t], "v": tok_wv[t]}
            wbf = {"q": qb, "k": kb, "v": vb}
            lastmm = {}
            qkv_pend = None

            def qkv_flush(pv, cover):
                wn_, oc_, p5_ = pv
                ev1 = OP("a", lambda En, o_=obuf[wn_][:, oc_, 0:512], p_=p5_,
                         b_=vec(bias[wn_])[:, oc_:oc_ + 1]:
                         En.activation(o_, p_[:], AF.Identity, bias=b_, scale=1.0),
                         waits=[cover], inc="sA")
                evs[wn_][oc_] = ev1
                t_pf_free[(0, 1, 4)[oc_ % 3]] = ev1
                if oc_ == NCH - 1:
                    wni = {"q": 0, "k": 1, "v": 2}[wn_]
                    ev2 = OP("a", lambda En, o_=s_t1[:, 32 * wni:32 * wni + 32],
                             p_=pf[2][:, 0:32]: En.copy(o_, p_), inc="sA")
                    t_pf_free[2] = ev2
                    OP("v", lambda En,
                       o_=obuf[wn_][:, :, 512:E],
                       a_=s_t1[:, 32 * wni:32 * wni + 32],
                       b_=s_vb4[:, wni:wni + 1, :]:
                       En.tensor_tensor(o_, a_, b_, ALU.add),
                       waits=[ev2, tok_vb4])

            for wn in ("q", "k", "v"):
                B_ = wbf[wn]
                for oc in range(NCH):
                    bmain = (0, 1, 4)[oc % 3]
                    p5 = pf[bmain]
                    w0 = [wtk[wn], t_pf_free[bmain]]
                    for ic in range(NCH):
                        mmm = OP("t", lambda En, l_=B_[:, ic, 128 * oc:128 * (oc + 1)],
                                 r_=s_xpb[:, ic, 0:512], st=(ic == 0), sp=(ic == NCH - 1),
                                 p_=p5:
                                 En.matmul(p_[:], l_, r_, start=st, stop=sp),
                                 waits=w0 + [tok_xpb[ic]],
                                 inc=("sT" if ic == 0 else None))
                        w0 = []
                        if ic == 0 and qkv_pend is not None:
                            qkv_flush(qkv_pend, mmm)
                            qkv_pend = None
                        mmt = OP("t", lambda En, l_=B_[:, ic, 128 * oc:128 * (oc + 1)],
                                 r_=s_xpb[:, ic, 512:E], st=(ic == 0), sp=(ic == NCH - 1),
                                 p_=pf[2], co=4 * oc:
                                 En.matmul(p_[:, co:co + 4], l_, r_, start=st, stop=sp),
                                 waits=([t_pf_free[2]] if ic == 0 and oc == 0 else []),
                                 inc=("sT" if ic == NCH - 1 else None))
                    qkv_pend = (wn, oc, p5)
                lastmm[wn] = mmt

            tok_fa[t] = DMA("g", qb[:], wslice(d_wfb, 0), f"dB{(0 - t) % 3}", waits=[lastmm["q"]])
            tok_fb[t] = DMA("g", kb[:], wslice(d_wfb, 8), f"dB{(1 - t) % 3}", waits=[lastmm["k"]])
            if t + 1 < NT:
                tok_wq[t + 1] = DMA("g", vb[:], wslice(d_wq, 0), f"dB{(2 - t) % 3}",
                                    waits=[lastmm["v"]])

            tok_sm = []
            prev_scores = [None, None]
            sc_pend = None
            for j in range(3):
                ub = (s_u, s_u2, s_u)[j]
                ut = [None] * NCH
                for c in range(NCH):
                    w = [tok_qe[c], tok_ke[c]]
                    if j == 0 and c == 0 and tok_u_freed is not None:
                        w.append(tok_u_freed)
                    if c == 0 and j >= 2 and prev_scores[j % 2] is not None:
                        w.append(prev_scores[j % 2])
                    ut[c] = OP("v", lambda En, o_=ub[:, c, :],
                               a_=s_q[:, c, 1:E - 1], b_=s_k[:, c, 2 - j:E - j]:
                               En.tensor_tensor(o_, a_, b_, ALU.mult),
                               waits=w, inc="sV")
                wF = [t_pf_free[4]]
                for c in range(NCH):
                    mmF = OP("t", lambda En, l_=s_ind[:, c, :], r_=ub[:, c, 0:512],
                             st=(c == 0), sp=(c == NCH - 1):
                             En.matmul(pf[4][0:16, :], l_, r_, start=st, stop=sp),
                             waits=wF + [ut[c]],
                             inc=("sT" if c in (0, NCH - 1) else None))
                    wF = []
                    if c == 0:
                        if qkv_pend is not None:
                            qkv_flush(qkv_pend, mmF)
                            qkv_pend = None
                        if sc_pend is not None:
                            e2 = OP("a", lambda En, jp=sc_pend,
                                    o_=s_SM[:, 3 + (2 - sc_pend), :]:
                                    En.copy(o_, pf[5][0:16, :]),
                                    waits=[mmF], inc="sA")
                            t_pf_free[5] = e2
                            tok_sm.append(e2)
                            sc_pend = None
                wR = [t_pf_free[5]]
                mmR0 = None
                for c in range(NCH):
                    mmR = OP("t", lambda En, l_=s_ind[:, c, :], r_=ub[:, c, 2:514],
                             st=(c == 0), sp=(c == NCH - 1):
                             En.matmul(pf[5][0:16, :], l_, r_, start=st, stop=sp),
                             waits=wR,
                             inc=("sT" if c in (0, NCH - 1) else None))
                    wR = []
                    if c == 0:
                        mmR0 = mmR
                e1 = OP("a", lambda En, o_=s_SM[:, j, :]:
                        En.copy(o_, pf[4][0:16, :]), waits=[mmR0], inc="sA")
                t_pf_free[4] = e1
                tok_sm.append(e1)
                sc_pend = j
                prev_scores[j % 2] = mmR
            fsc = gfence(pf[4][32:33, 256:384])
            e2 = OP("a", lambda En, o_=s_SM[:, 3 + (2 - 2), :]:
                    En.copy(o_, pf[5][0:16, :]), waits=[fsc], inc="sA")
            t_pf_free[5] = e2
            tok_sm.append(e2)
            sc_pend = None
            tok_u_read = sch.tok("sT")

            OP("v", lambda En, m_=s_mpre[:, t, :, 0:8]:
               En.tensor_tensor(s_SM[:, :, 0:8], s_SM[:, :, 0:8], m_, ALU.mult),
               waits=tok_sm)
            OP("v", lambda En, m_=s_mpre[:, t, :, 8:16]:
               En.tensor_tensor(s_SM[:, :, TT - 8:TT], s_SM[:, :, TT - 8:TT], m_, ALU.mult))

            tok_ao = [None] * 16
            par = 0
            wb_read = [None, None, None, None]
            tok_prev_wv = None
            wvbbuf = (s_wb[0], s_wb[1], s_rsq[0], s_rsq[1])

            def wv_flush(pv, cover_tok):
                i_, pw_, c_, j_, dd_ = pv
                wbb = wvbbuf[i_ % 4]
                ewb = OP("a", lambda En, o_=wbb[:], p_=pw_:
                         En.copy(o_, p_[:]),
                         waits=[cover_tok, wb_read[i_ % 4]], inc="sA")
                t_pf_free[2 + i_ % 4] = ewb
                vs = (2 - j_) if dd_ == 0 else (2 + j_)
                if j_ == 0:
                    tok_ao[8 * dd_ + c_] = OP(
                        "v", lambda En, o_=s_ao[:, 8 * dd_ + c_, :],
                        a_=wbb[:], b_=s_v[:, c_, vs:vs + 512]:
                        En.tensor_mul(o_, a_, b_),
                        waits=[ewb, tok_ve[c_]], inc="sV")
                    wb_read[i_ % 4] = tok_ao[8 * dd_ + c_]
                else:
                    wb_read[i_ % 4] = OP(
                        "v", lambda En, o_=s_wvp[i_ % 2][:],
                        a_=wbb[:], b_=s_v[:, c_, vs:vs + 512]:
                        En.tensor_mul(o_, a_, b_), waits=[ewb], inc="sV")
                    tok_ao[8 * dd_ + c_] = OP(
                        "v", lambda En, o_=s_ao[:, 8 * dd_ + c_, :],
                        a_=s_wvp[i_ % 2][:]:
                        En.tensor_add(o_, o_, a_), inc="sV")
            wv_pend = None
            tok_wf = [None, None]
            for dd in range(2):
                SMB = (s_SMb, s_SMb2)[dd]
                d0 = 3 * dd
                OP("v", lambda En, a_=s_SM[:, d0, :], b_=s_SM[:, d0 + 1, :]:
                   En.tensor_max(s_smxm[:], a_, b_))
                OP("v", lambda En, a_=s_SM[:, d0 + 2, :]:
                   En.tensor_max(s_smxm[:], s_smxm[:], a_))
                for j in range(3):
                    tsb = OP("v", lambda En, a_=s_SM[:, d0 + j, :]:
                             En.tensor_sub(a_, a_, s_smxm[:]),
                             inc=("sV" if j == 2 else None))
                tex = OP("a", lambda En, a_=s_SM[:, d0:d0 + 3, :]:
                         En.activation(a_, a_, AF.Exp),
                         waits=[tsb], inc="sA")
                OP("v", lambda En, a_=s_SM[:, d0, :], b_=s_SM[:, d0 + 1, :]:
                   En.tensor_add(s_smxr[:], a_, b_), waits=[tex])
                tsum = OP("v", lambda En, a_=s_SM[:, d0 + 2, :]:
                          En.tensor_add(s_smxr[:], s_smxr[:], a_), inc="sV")
                OP("a", lambda En: En.activation(s_smxm[:], s_smxr[:], AF.Ln),
                   waits=[tsum])
                trc = OP("a", lambda En: En.activation(s_smxr[:], s_smxm[:], AF.Exp, scale=-1.0),
                         inc="sA")
                for j in range(3):
                    OP("v", lambda En, a_=s_SM[:, d0 + j, :], o_=SMB[:, j, :]:
                       En.tensor_mul(o_, a_, s_smxr[:]),
                       waits=([trc, tok_prev_wv] if j == 0 else []))
                OP("v", lambda En, m_=s_mpre[0:16, t, d0:d0 + 3, 0:8],
                   o_=SMB[:, :, 0:8], mm_=s_mpost[:, t, d0:d0 + 3, 0:8]:
                   En.tensor_tensor(o_, o_, mm_, ALU.mult))
                tok_wf[dd] = OP("v", lambda En, o_=SMB[:, :, TT - 8:TT],
                                m_=s_mpost[:, t, d0:d0 + 3, 8:16]:
                                En.tensor_tensor(o_, o_, m_, ALU.mult),
                                inc="sV")

            for dd in range(2):
                SMB = (s_SMb, s_SMb2)[dd]
                wvw = [tok_wf[dd], tok_u_read]
                if dd == 0 and tok_ao_free is not None:
                    wvw.append(tok_ao_free)
                for c in range(NCH):
                    for j in range(3):
                        pw = pf[2 + par % 4]
                        mmW = OP("t", lambda En, l_=s_sel[:, c, :], r_=SMB[:, j, :],
                                 p_=pw:
                                 En.matmul(p_[:], l_, r_, start=True, stop=True),
                                 waits=wvw + [t_pf_free[2 + par % 4]], inc="sT")
                        wvw = []
                        if wv_pend is not None:
                            wv_flush(wv_pend, mmW)
                        wv_pend = (par, pw, c, j, dd)
                        par += 1
            tok_prev_wv = sch.tok("sT")

            tok_rbf = [None] * NCH
            tok_rsq = [None] * NCH
            stat_mm2 = [None] * NCH
            def emit_stats(so):
                wS = [t_pf_free[4], tok_rbf[so]] if so == 0 else [tok_rbf[so]]
                mS = OP("t", lambda En, r_=s_rbf[:, so, :], st=(so == 0),
                        sp=(so == NCH - 1):
                        En.matmul(pf[4][0:1, :], s_onc[:], r_, start=st, stop=sp),
                        waits=wS, inc=("sT" if so == NCH - 1 else None))
                wS2 = [t_pf_free[5], tok_rsq[so]] if so == 0 else [tok_rsq[so]]
                stat_mm2[so] = mS2 = OP(
                    "t", lambda En, r_=s_rsq[so % 2][:], st=(so == 0),
                    sp=(so == NCH - 1):
                    En.matmul(pf[5][0:1, :], s_onc[:], r_, start=st, stop=sp),
                    waits=wS2, inc="sT")
                return mS, mS2

            for oc in range(NCH):
                p5 = pf[oc % 2]
                w0 = [tok_fa[t], tok_fb[t], t_pf_free[oc % 2]]
                for ic in range(16):
                    B_ = qb if ic < 8 else kb
                    mmt = OP("t", lambda En, l_=B_[:, ic % 8, 128 * oc:128 * (oc + 1)],
                             r_=s_ao[:, ic, :], st=(ic == 0), sp=(ic == 15), p_=p5:
                             En.matmul(p_[:], l_, r_, start=st, stop=sp),
                             waits=w0 + [tok_ao[ic]],
                             inc=("sT" if ic == 15 else None))
                    w0 = []
                    if wv_pend is not None:
                        wv_flush(wv_pend, mmt)
                        wv_pend = None
                if oc >= 1:
                    emit_stats(oc - 1)
                    cov = stat_mm2[oc - 1]
                else:
                    cov = gfence(pf[4][32:33, 256:384])
                OP("v", lambda En, i=s_xhat[:, oc, e0 + 2:e0 + 2 + TT]:
                   En.tensor_tensor(s_x2[:], i, s_cb[:, 2:2 + TT], ALU.mult),
                   waits=[cov])
                OP("v", lambda En, g_=vec("g1")[:, oc:oc + 1], b_=vec("b1")[:, oc:oc + 1]:
                   En.tensor_scalar(s_x2[:], s_x2[:], g_, b_, ALU.mult, ALU.add))
                tok_rbf[oc] = OP(
                    "v", lambda En, o_=s_rbf[:, oc, :], p_=p5, b_=vec("bfb")[:, oc:oc + 1]:
                    En.scalar_tensor_tensor(o_, p_[:], b_, s_x2[:], ALU.add, ALU.add),
                    inc="sV")
                t_pf_free[oc % 2] = tok_rbf[oc]
                tok_rsq[oc] = OP(
                    "v", lambda En, o_=s_rsq[oc % 2][:], a_=s_rbf[:, oc, :]:
                    En.tensor_mul(o_, a_, a_),
                    waits=[stat_mm2[oc - 2]] if oc >= 2 else [], inc="sV")
                if oc == NCH - 1:
                    emit_stats(oc)
            rowm = s_smxm[0:1, :]
            rowe = s_smxr[0:1, :]
            rowt = s_SM[0:1, 0, :]
            fst = gfence(pf[5][32:33, 256:384])
            OP("v", lambda En: En.tensor_scalar_mul(rowm, pf[4][0:1, :], 1.0 / D),
               waits=[fst])
            OP("v", lambda En: En.tensor_scalar_mul(rowe, pf[5][0:1, :], 1.0 / D))
            OP("v", lambda En: En.tensor_mul(rowt, rowm, rowm))
            OP("v", lambda En: En.tensor_sub(rowe, rowe, rowt))
            tve = OP("v", lambda En: En.tensor_scalar_add(rowe, rowe, 1e-6), inc="sV")
            t_pf_free[4] = tve
            t_pf_free[5] = tve
            OP("a", lambda En: En.activation(rowt, rowe, AF.Ln), waits=[tve])
            trs = OP("a", lambda En: En.activation(rowe, rowt, AF.Exp, scale=-0.5),
                     inc="sA")
            tnm = OP("v", lambda En: En.tensor_scalar_mul(rowm, rowm, -1.0), inc="sV")
            bc1 = OP("t", lambda En: En.matmul(pf[2][:], s_on1f[:], rowm, start=True, stop=True),
                     waits=[tnm, t_pf_free[2]], inc="sT")
            bc2 = OP("t", lambda En: En.matmul(pf[3][:], s_on1f[:], rowe, start=True, stop=True),
                     waits=[trs, t_pf_free[3]], inc="sT")
            fbc = gfence(pf[5][32:33, 256:384])
            th2 = None
            for oc in range(NCH):
                OP("v", lambda En, a_=s_rbf[:, oc, :]:
                   En.tensor_tensor(s_x2[:], a_, pf[2][:], ALU.add),
                   waits=([fbc] if oc == 0 else []))
                OP("v", lambda En: En.tensor_tensor(s_x2[:], s_x2[:], pf[3][:], ALU.mult))
                th2 = OP("v", lambda En, o_=s_h2[:, oc, e0:e0 + TT],
                         g_=vec("g2")[:, oc:oc + 1], b_=vec("b2")[:, oc:oc + 1]:
                         En.tensor_scalar(o_, s_x2[:], g_, b_, ALU.mult, ALU.add),
                         inc="sV")
            t_pf_free[2] = th2
            t_pf_free[3] = th2

            tok_u_freed = tok_u_read
            tok_u_freed = sch.tok("sT")
            tok_xpb_free = sch.tok("sT")
            tok_ao_free = sch.tok("sT")
            if t + 1 < NT:
                tok_wk[t + 1] = DMA("g", qb[:], wslice(d_wk, 0), f"dB{(0 - t) % 3}",
                                    waits=[sch.tok("sT")])
                tok_wv[t + 1] = DMA("g", kb[:], wslice(d_wv, 0), f"dB{(1 - t) % 3}",
                                    waits=[sch.tok("sT")])

        if _DBG:
            tD = DMA("s", d_dbg_h2, s_h2[:].rearrange("p c t -> p (c t)"), "dD",
                     waits=[sch.tok("sV")])
            OP("s", lambda En: En.nop(), waits=[tD])
        sch.barrier()

        two_c = [DMA("g", s_wo[:, 2 * ci:2 * ci + 2, :],
                     wslice(d_wo, 2 * ci, n_chunk=2), f"dWo{ci}")
                 for ci in range(4)]
        tok_w1a = DMA("g", s_w1a[:], d_w1[:, 0:2048].rearrange("(i p) o -> p i o", p=128),
                      "dW1a")
        pf_free = [None] * 6
        a3_pend = None

        def a3_flush(po, cover):
            pf_free[NT - 1] = OP(
                "a", lambda En, o_=s_s[:, po, 512 * (NT - 1):512 * NT],
                b_=vec("bo")[:, po:po + 1], p_=pf[NT - 1]:
                En.activation(o_, p_[:], AF.Gelu, bias=b_, scale=1.0),
                waits=[cover], inc="sA")

        for oc in range(NCH):
            lastm = [None] * NT
            w0 = []
            for ic in range(NCH):
                for t4 in range(NT):
                    if oc == 0 and t4 == 0:
                        w0.append(two_c[ic // 2])
                    lastm[t4] = OP(
                        "t", lambda En, l_=s_wo[:, ic, 128 * oc:128 * (oc + 1)],
                        r_=s_h2[:, ic, 512 * t4:512 * (t4 + 1)], st=(ic == 0),
                        sp=(ic == NCH - 1), p_=pf[t4]:
                        En.matmul(p_[:], l_, r_, start=st, stop=sp),
                        waits=w0 + ([pf_free[t4]] if ic == 0 else []),
                        inc=("sT" if ic == NCH - 1 or (ic == 0 and t4 == 0)
                             else None))
                    w0 = []
                    if a3_pend is not None:
                        a3_flush(a3_pend, lastm[0])
                        a3_pend = None
            for t4 in range(NT - 1):
                pf_free[t4] = OP(
                    "a", lambda En, o_=s_s[:, oc, 512 * t4:512 * (t4 + 1)],
                    b_=vec("bo")[:, oc:oc + 1], p_=pf[t4]:
                    En.activation(o_, p_[:], AF.Gelu, bias=b_, scale=1.0),
                    waits=[lastm[t4 + 1]], inc="sA")
            a3_pend = oc
        a3_flush(a3_pend, gfence(pf[4][0:1, 0:128]))
        a3_pend = None

        if _DBG:
            tD = DMA("s", d_dbg_s, s_s[:].rearrange("p c t -> p (c t)"), "dD",
                     waits=[sch.tok("sA")])
            OP("s", lambda En: En.nop(), waits=[tD])
        sch.barrier()

        tok_w1b = DMA("g", s_w1b[:], d_w1[:, 2048:4096].rearrange("(i p) o -> p i o", p=128),
                      "dW1b")
        pf_free = [None] * 6
        tok_hdma = [None] * 32
        tok_w2a = None
        f1_pend = None
        hA_toks = []

        def f1_flush(po, cover):
            pf_free[NT - 1] = OP(
                "a", lambda En, o_=s_hst[po % 3][:, 512 * (NT - 1):512 * NT],
                b_=s_b1f[:, po:po + 1], p_=pf[NT - 1]:
                En.activation(o_, p_[:], AF.Relu, bias=b_, scale=1.0),
                waits=[cover], inc="sA")
            tok_hdma[po] = DMA("a", d_h[po], s_hst[po % 3][:], f"dH{po % 4}",
                               waits=[pf_free[t4] for t4 in range(NT)])
            if po == 15:
                hA_toks.extend(sch.tok(f"dH{i}") for i in range(4))

        for oc in range(32):
            wb_ = s_w1a if oc < 16 else s_w1b
            wtok = tok_w1a if oc < 16 else tok_w1b
            col = 128 * (oc % 16)
            lastm = [None] * NT
            w0 = [wtok]
            if oc >= 3:
                w0.append(tok_hdma[oc - 3])
            for ic in range(NCH):
                for t4 in range(NT):
                    lastm[t4] = OP(
                        "t", lambda En, l_=wb_[:, ic, col:col + 128],
                        r_=s_xhat[:, ic, 2 + 512 * t4:2 + 512 * (t4 + 1)],
                        st=(ic == 0), sp=(ic == NCH - 1), p_=pf[t4]:
                        En.matmul(p_[:], l_, r_, start=st, stop=sp),
                        waits=w0 + ([pf_free[t4]] if ic == 0 else []),
                        inc=("sT" if ic == NCH - 1 or (ic == 0 and t4 == 0)
                             else None))
                    w0 = []
                    if f1_pend is not None:
                        f1_flush(f1_pend, lastm[0])
                        f1_pend = None
            for t4 in range(NT - 1):
                pf_free[t4] = OP(
                    "a", lambda En, o_=s_hst[oc % 3][:, 512 * t4:512 * (t4 + 1)],
                    b_=s_b1f[:, oc:oc + 1], p_=pf[t4]:
                    En.activation(o_, p_[:], AF.Relu, bias=b_, scale=1.0),
                    waits=[lastm[t4 + 1]], inc="sA")
            f1_pend = oc
            if oc == 15:
                tok_w2a = DMA("g", s_w2a[:],
                              d_w2[0:2048, :].rearrange("(i p) o -> p i o", p=128),
                              "dW2a", waits=[lastm[NT - 1]])
        f1_flush(f1_pend, gfence(pf[4][0:1, 0:128]))
        f1_pend = None

        sch.barrier()

        tok_w2b = DMA("g", s_w2b[:],
                      d_w2[2048:4096, :].rearrange("(i p) o -> p i o", p=128), "dW2b")
        all_h = [sch.tok(f"dH{i}") for i in range(4)]
        pf_free = [None] * 6
        pb_free = [None, None]
        tok_hinA_free = None
        tok_hinB_free = None
        tok_xre_free = None
        tok_out_dma = None
        out_dma_bi = [None, None]
        for t in range(NT):
            tsl = (512 * t, 512 * (t + 1))
            thA_c = []
            thB_c = []
            for ci in range(4):
                thA_c.append(DMA(
                    "s", s_hinA[:, 4 * ci:4 * ci + 4, :],
                    d_h[4 * ci:4 * ci + 4, :, tsl[0]:tsl[1]].rearrange("i p t -> p i t"),
                    f"dHA{ci}",
                    waits=((hA_toks if t == 0 else []) + [tok_hinA_free])
                    if ci == 0 else []))
            for ci in range(4):
                thB_c.append(DMA(
                    "s", s_hinB[:, 4 * ci:4 * ci + 4, :],
                    d_h[16 + 4 * ci:20 + 4 * ci, :, tsl[0]:tsl[1]].rearrange("i p t -> p i t"),
                    f"dHB{ci}",
                    waits=[tok_hinB_free] if ci == 0 else []))
            txr = DMA("s", s_xre[:],
                      d_xs[2 + tsl[0]:2 + tsl[1], :].rearrange("(b p) d -> p b d", p=128),
                      "dXR", waits=[tok_xre_free])
            tok_sg = [None] * NCH
            f2_pend = None

            def f2_flush(pv, cover):
                po, pp = pv
                tgs = OP("a", lambda En, o_=s_gsc[po % 2][:], p_=pp,
                         b_=vec("b2f")[:, po:po + 1]:
                         En.activation(o_, p_[:], AF.Silu, bias=b_, scale=1.0),
                         waits=[cover], inc="sA")
                tok_sg[po] = OP(
                    "v", lambda En, o_=s_sg[:, po, :], a_=s_gsc[po % 2][:],
                    b_=s_s[:, po, tsl[0]:tsl[1]]:
                    En.tensor_mul(o_, a_, b_), waits=[tgs], inc="sV")
                pf_free[po % 2] = tok_sg[po]

            for oc in range(NCH):
                p5 = pf[oc % 2]
                w0 = [tok_w2a, tok_w2b, pf_free[oc % 2]]
                for ic in range(32):
                    wb_ = s_w2a if ic < 16 else s_w2b
                    hb_ = s_hinA if ic < 16 else s_hinB
                    if oc == 0:
                        w0.append((thA_c if ic < 16 else thB_c)[(ic % 16) // 4])
                    mmt = OP("t", lambda En, l_=wb_[:, ic % 16, 128 * oc:128 * (oc + 1)],
                             r_=hb_[:, ic % 16, :], st=(ic == 0), sp=(ic == 31), p_=p5:
                             En.matmul(p_[:], l_, r_, start=st, stop=sp),
                             waits=w0, inc=("sT" if ic in (0, 31) else None))
                    w0 = []
                    if f2_pend is not None:
                        f2_flush(f2_pend, mmt)
                        f2_pend = None
                f2_pend = (oc, p5)
            f2_flush(f2_pend, gfence(pf[3][0:1, 0:128]))
            f2_pend = None
            tok_hinA_free = sch.tok("sT")
            tok_hinB_free = sch.tok("sT")
            for half in range(2):
                touts = []
                for bi in range(2):
                    b4 = 2 * half + bi
                    wv0 = [tok_sg[NCH - 1], pb_free[bi]]
                    if half == 0 and out_dma_bi[bi] is not None:
                        wv0.append(out_dma_bi[bi])
                    for oc in range(NCH):
                        OP("t", lambda En, o_=pb[bi][:, 128 * oc:128 * (oc + 1)],
                           i=s_sg[:, oc, 128 * b4:128 * (b4 + 1)]:
                           En.transpose(o_, i, s_idb[:]), waits=wv0)
                        wv0 = []
                    ttr = OP("t", lambda En:
                             En.matmul(pf[2][0:1, 0:128], s_onc[0:1, 0:1], s_on1[0:1, :],
                                       start=True, stop=True), inc="sT")
                    wadd = [ttr, txr]
                    if out_dma_bi[bi] is not None:
                        wadd.append(out_dma_bi[bi])
                    if t == NT - 1 and half == 1:
                        OP("v", lambda En: En.nop() if _SIMGF else
                           En.nop(cycle_cnt=3000), waits=[ttr])
                    touts.append(OP(
                        "v", lambda En, o_=s_outst[:, bi, :], p_=pb[bi],
                        x_=s_xre[:, b4, :]:
                        En.tensor_tensor(o_, p_[:], x_, ALU.add),
                        waits=wadd, inc="sV"))
                    pb_free[bi] = touts[-1]
                    tcm = OP("v", lambda En: En.nop() if _SIMGF else
                             En.nop(cycle_cnt=700), waits=[touts[-1]],
                             inc="sV")
                    r0 = tsl[0] + 256 * half + 128 * bi
                    out_dma_bi[bi] = DMA(
                        "a", d_y[r0:r0 + 128, :],
                        s_outst[:, bi, :], ("dO", "dO1")[bi], waits=[tcm])
                tok_out_dma = out_dma_bi[1]
            tok_xre_free = sch.tok("sV")

        if _DBG3:
            DMA("s", d_dbg_sg, s_sg[:].rearrange("p c t -> p (c t)"), "dD",
                waits=[sch.tok("sV")])
            DMA("s", d_dbg_gsc[:, 0:TT], s_gsc[0][:], "dD")
            DMA("s", d_dbg_gsc[:, TT:2 * TT], s_gsc[1][:], "dD")
            tD3 = DMA("s", d_dbg_s2, s_s[:].rearrange("p c t -> p (c t)"), "dD")
            OP("s", lambda En: En.nop(), waits=[tD3])
        OP("a", lambda En: En.nop(), waits=[sch.tok("dO"), sch.tok("dO1")])
        OP("s", lambda En: En.nop(),
           waits=[sch.tok("dO"), sch.tok("dO1"), sch.tok("dD")])

        with nc.Block() as blk:
            def runner(key):
                def body(Eng):
                    for needed, fn, inc, inck in sch.q[key]:
                        for sname, val in needed:
                            Eng.wait_ge(sems[sname], val)
                        ins = fn(Eng)
                        if inc is not None:
                            ins.then_inc(sems[inc], inck)
                return body
            blk.tensor(runner("t"))
            blk.vector(runner("v"))
            blk.scalar(runner("a"))
            blk.gpsimd(runner("g"))
            blk.sync(runner("s"))

    return nc


_NC_CACHE = None


def _get_nc():
    global _NC_CACHE
    if _NC_CACHE is None:
        _NC_CACHE = build_program()
    return _NC_CACHE


def _host_consts():
    f = np.float32
    idm = np.eye(128, dtype=f)
    sel = np.zeros((16, NCH, 128), f)
    ind = np.zeros((128, NCH, 16), f)
    for c in range(NCH):
        for dd in range(128):
            h = 2 * c + (1 if dd >= 64 else 0)
            sel[h, c, dd] = 1.0
            ind[dd, c, h] = 0.125
    ones1 = np.ones((1, 128), f)
    onesc = np.ones((128, 1), f)
    return idm, sel.reshape(16, NCH * 128), ind.reshape(128, NCH * 16), ones1, onesc


def _masks_for_core(t0):
    f = np.float32
    mpre = np.ones((16, NT, 6, 16), f)
    mpost = np.ones((16, NT, 6, 16), f)
    for t in range(NT):
        for col in range(16):
            if col < 8:
                g = t0 + 512 * t + col
            else:
                g = t0 + 512 * t + 504 + (col - 8)
            for j in range(3):
                if not (g >= max(1, j)):
                    mpre[:, t, j, col] = 0.0
                if not (g >= j):
                    mpost[:, t, j, col] = 0.0
                if not (g <= min(SEQ - 2, SEQ - 1 - j)):
                    mpre[:, t, 3 + j, col] = 0.0
                if not (g <= SEQ - 1 - j):
                    mpost[:, t, 3 + j, col] = 0.0
    return mpre.reshape(16, NT * 6 * 16), mpost.reshape(16, NT * 6 * 16)


def kernel(L0_attn_input_feature, L0_attn_Wq, L0_attn_bq, L0_attn_Wk, L0_attn_bk,
           L0_attn_Wv, L0_attn_bv, L0_attn_Wfb, L0_attn_bfb, L0_attn_Wo,
           L0_attn_bo, L0_attn_g1, L0_attn_b1, L0_attn_g2, L0_attn_b2,
           L0_ffn_W1, L0_ffn_b1, L0_ffn_W2, L0_ffn_b2):
    f = np.float32
    x = np.ascontiguousarray(np.asarray(L0_attn_input_feature, f))
    W = {k: np.ascontiguousarray(np.asarray(v, f)) for k, v in dict(
        Wq=L0_attn_Wq, Wk=L0_attn_Wk, Wv=L0_attn_Wv, Wfb=L0_attn_Wfb,
        Wo=L0_attn_Wo, W1=L0_ffn_W1, W2=L0_ffn_W2).items()}
    vl = [L0_attn_g1, L0_attn_b1, L0_attn_g2, L0_attn_b2, L0_attn_bq,
          L0_attn_bk, L0_attn_bv, L0_attn_bfb, L0_attn_bo, L0_ffn_b2]
    vecs = np.stack([np.asarray(v, f).reshape(NCH, 128).T for v in vl], axis=1)
    vecs = np.ascontiguousarray(vecs.reshape(128, 80))
    b1f = np.ascontiguousarray(np.asarray(L0_ffn_b1, f).reshape(32, 128).T)
    vb4 = np.stack([np.repeat(np.asarray(v, f).reshape(NCH, 128).T[:, :, None],
                              4, axis=2)
                    for v in (L0_attn_bq, L0_attn_bk, L0_attn_bv)], axis=1)
    vb4 = np.ascontiguousarray(vb4.reshape(128, 3, 32))
    idm, sel, ind, ones1, onesc = _host_consts()

    zeros2 = np.zeros((2, D), f)
    in_maps = []
    for core in range(8):
        b, half = divmod(core, 2)
        if half == 0:
            xs = np.concatenate([zeros2, x[b, 0:TC + 2]], axis=0)
        else:
            xs = np.concatenate([x[b, TC - 2:2 * TC], zeros2], axis=0)
        mpre, mpost = _masks_for_core(half * TC)
        in_maps.append(dict(
            xs=np.ascontiguousarray(xs), vecs=vecs, b1f=b1f, vb4=vb4, idm=idm,
            sel=sel, ind=ind, ones1=ones1, onesc=onesc, mpre=mpre, mpost=mpost,
            **W))

    nc = _get_nc()
    res = run_bass_kernel_spmd(nc, in_maps, core_ids=list(range(8)))
    if _DBG or _DBG3:
        global _DBG_RES
        _DBG_RES = res.results
    ys = [res.results[i]["y"] for i in range(8)]
    out = np.stack([np.concatenate([ys[2 * b], ys[2 * b + 1]], axis=0)
                    for b in range(4)])
    return np.ascontiguousarray(out.astype(np.float32))
